# version 09
# baseline (speedup 1.0000x reference)
import math
import numpy as np
import ml_dtypes
import concourse.bass as bass
import concourse.mybir as mybir
from concourse.bass_utils import run_bass_kernel_spmd

F32 = mybir.dt.float32
BF16 = mybir.dt.bfloat16
I32 = mybir.dt.int32
U32 = mybir.dt.uint32
U8 = mybir.dt.uint8
AF = mybir.ActivationFunctionType
ALU = mybir.AluOpType
AX = mybir.AxisListType

BF = np.dtype(ml_dtypes.bfloat16)


COMPUTE_STREAMS = ("tensor", "vector", "scalar", "gpsimd", "sync")


class Tok:
    __slots__ = ("stream", "seq", "is_dma", "ninc")

    def __init__(self, stream, seq, is_dma=False, ninc=1):
        self.stream = stream
        self.seq = seq
        self.is_dma = is_dma
        self.ninc = ninc


class Prog:

    N_DMA_SEMS = 14

    def __init__(self, nc):
        self.nc = nc
        self.ops = {s: [] for s in COMPUTE_STREAMS}
        self.dma_slot_rr = {}
        self.auto_deps = []

    def barrier(self, toks):
        self.auto_deps = list(toks)

    def op(self, stream, fn, deps=()):
        lst = self.ops[stream]
        t = Tok(stream, len(lst))
        lst.append(dict(fn=fn, deps=list(deps) + self.auto_deps, marked=False,
                        dma=None, tok=t))
        return t

    def dma(self, stream, transfers, deps=(), cast=False, slow=False):
        lst = self.ops[stream]
        t = Tok(stream, len(lst), is_dma=True, ninc=len(transfers))
        rr = self.dma_slot_rr.get(stream, 0)
        slot = (stream, rr % self.N_DMA_SEMS)
        self.dma_slot_rr[stream] = rr + 1
        kw = {"allow_slow_non_contiguous": True} if slow else {}
        lst.append(dict(fn=None, transfers=list(transfers),
                        deps=list(deps) + self.auto_deps,
                        marked=False, dma=slot, tok=t, kw=kw))
        return t

    def idma(self, fn, deps=()):
        lst = self.ops["gpsimd"]
        t = Tok("gpsimd", len(lst), is_dma=True, ninc=1)
        rr = self.dma_slot_rr.get("gpsimd", 0)
        slot = ("gpsimd", rr % self.N_DMA_SEMS)
        self.dma_slot_rr["gpsimd"] = rr + 1
        lst.append(dict(fn=fn, transfers=None, deps=list(deps) + self.auto_deps,
                        marked=False, dma=slot, tok=t, kw={}))
        return t

    def cc(self, fn, deps=()):
        lst = self.ops["gpsimd"]
        if not hasattr(self, "cc_count"):
            self.cc_count = 0
        self.cc_count += 1
        ta = Tok("gpsimd", len(lst))
        lst.append(dict(fn=fn, deps=list(deps) + self.auto_deps, marked=False,
                        dma=None, tok=ta, cc=self.cc_count))
        tb = Tok("gpsimd", len(lst))
        lst.append(dict(fn="relay", deps=[ta], marked=False, dma=None, tok=tb))
        return tb

    def materialize(self, ctx, final_deps):
        nc = self.nc
        def mark(tok):
            self.ops[tok.stream][tok.seq]["marked"] = True
        for s in COMPUTE_STREAMS:
            for o in self.ops[s]:
                for d in o["deps"]:
                    mark(d)
        for d in final_deps:
            mark(d)
        engsem_val = {}
        for s in COMPUTE_STREAMS:
            cum = 0
            vals = []
            for o in self.ops[s]:
                if o["dma"] is None and o["marked"] and "cc" not in o:
                    cum += 1
                vals.append(cum)
            engsem_val[s] = vals
        from collections import defaultdict
        dma_cum = defaultdict(int)
        dma_before = {}
        dma_after = {}
        for s in COMPUTE_STREAMS:
            for o in self.ops[s]:
                if o["dma"] is not None:
                    slot = o["dma"]
                    dma_before[(s, o["tok"].seq)] = dma_cum[slot]
                    dma_cum[slot] += 16 * o["tok"].ninc
                    dma_after[(s, o["tok"].seq)] = dma_cum[slot]
        esem = {s: ctx.enter_context(nc.semaphore(f"es_{s}")) for s in COMPUTE_STREAMS}
        used_slots = sorted({o["dma"] for ss in COMPUTE_STREAMS for o in self.ops[ss]
                             if o["dma"] is not None})
        dsem = {sl: ctx.enter_context(nc.semaphore(f"ds_{sl[0]}_{sl[1]}")) for sl in used_slots}

        ccsem = [None]
        relay_tile = [None]
        if getattr(self, "cc_count", 0):
            ccsem[0] = ctx.enter_context(nc.semaphore("cc_sem"))
            relay_tile[0] = ctx.enter_context(
                nc.sbuf_tensor("cc_relay", [1, 4], mybir.dt.float32))

        def dep_wait_list(stream, deps, hw):
            best = {}
            for d in deps:
                o = self.ops[d.stream][d.seq]
                if o["dma"] is not None:
                    sem = dsem[o["dma"]]
                    val = dma_after[(d.stream, d.seq)]
                    key = ("d", o["dma"])
                elif "cc" in o:
                    assert stream == "gpsimd", "cc tokens only waitable by gpsimd"
                    sem = ccsem[0]
                    val = o["cc"]
                    key = ("c",)
                else:
                    if d.stream == stream == "tensor":
                        continue
                    sem = esem[d.stream]
                    val = engsem_val[d.stream][d.seq]
                    key = ("e", d.stream)
                if key not in best or best[key][1] < val:
                    best[key] = (sem, val)
            ws = []
            for key, (sem, val) in best.items():
                if hw.get(key, -1) < val:
                    hw[key] = val
                    ws.append((sem, val))
            return ws

        block = ctx.enter_context(nc.Block())

        def make_stream_fn(s):
            def emit(eng):
                hw = {}
                for o in self.ops[s]:
                    for (sem, val) in dep_wait_list(s, o["deps"], hw):
                        eng.wait_ge(sem, val)
                    if o["dma"] is not None:
                        slot = o["dma"]
                        before = dma_before[(s, o["tok"].seq)]
                        key = ("d", slot)
                        if before > 0 and hw.get(key, -1) < before:
                            eng.wait_ge(dsem[slot], before)
                            hw[key] = before
                        if o["transfers"] is not None:
                            for (oap, iap) in o["transfers"]:
                                eng.dma_start(out=oap, in_=iap,
                                              **o.get("kw", {})).then_inc(dsem[slot], 16)
                        else:
                            o["fn"](eng).then_inc(dsem[slot], 16)
                    elif "cc" in o:
                        o["fn"](eng).then_inc(ccsem[0])
                    elif o["fn"] == "relay":
                        ins = eng.memset(relay_tile[0][:], 0.0)
                        if o["marked"]:
                            ins.then_inc(esem[s], 1)
                    else:
                        ins = o["fn"](eng)
                        if o["marked"]:
                            ins.then_inc(esem[s], 1)
                if s == "sync":
                    for d in final_deps:
                        o = self.ops[d.stream][d.seq]
                        if o["dma"] is not None:
                            sem = dsem[o["dma"]]
                            val = dma_after[(d.stream, d.seq)]
                            key = ("dF", o["dma"])
                        else:
                            sem = esem[d.stream]
                            val = engsem_val[d.stream][d.seq]
                            key = ("eF", d.stream)
                        if hw.get(key, -1) < val:
                            hw[key] = val
                            eng.wait_ge(sem, val)
            return emit

        for s in COMPUTE_STREAMS:
            if self.ops[s] or s == "sync":
                getattr(block, s)(make_stream_fn(s))



def host_consts():
    c = {}
    half = 32
    inv = 1.0 / 10000.0 ** (np.arange(half) / half)
    ang = np.arange(4096)[None, :] * inv[:, None]
    c["rope_cos"] = np.cos(ang).astype(BF)
    c["rope_sin"] = np.sin(ang).astype(BF)
    c["identity"] = np.eye(128, dtype=np.float32)
    c["identity_bf"] = np.eye(128, dtype=np.float32).astype(BF)
    i = np.arange(128)[:, None]
    j = np.arange(128)[None, :]
    c["mask_tril"] = np.where(j <= i, 0.0, -1e30).astype(np.float32)
    c["mask_triu"] = np.where(j >= i, 0.0, -1e30).astype(np.float32)
    c["mask_zero"] = np.zeros((128, 128), np.float32)
    c["mask_ninf"] = np.full((128, 128), -1e30, np.float32)
    c["ones_col"] = np.ones((128, 1), np.float32)
    c["ones_row"] = np.ones((1, 128), np.float32)
    return c


def fft_consts():
    c = {}
    W64 = np.exp(-2j * np.pi * np.outer(np.arange(64), np.arange(64)) / 64)
    c["w64_re"] = W64.real.astype(BF)
    c["w64_im"] = W64.imag.astype(BF)
    WN_aq = np.exp(-2j * np.pi * np.outer(np.arange(128), np.arange(64)) / 8192)
    W128 = np.exp(-2j * np.pi * np.outer(np.arange(128), np.arange(128)) / 128)
    F = WN_aq[:, None, :].transpose(2, 0, 1) * W128[None]
    c["F_re"] = F.real.astype(BF)
    c["F_im"] = F.imag.astype(BF)
    c["F_imn"] = (-F.imag).astype(BF)
    W128i = np.exp(2j * np.pi * np.outer(np.arange(128), np.arange(128)) / 128)
    WNi_aq = np.exp(2j * np.pi * np.outer(np.arange(128), np.arange(64)) / 8192)
    G = W128i.T[:, :, None].transpose(2, 1, 0) * 0
    G = (W128i[None, :, :] * WNi_aq.T[:, :, None]).transpose(0, 2, 1)
    c["G_re"] = G.real.astype(BF)
    c["G_im"] = G.imag.astype(BF)
    c["G_imn"] = (-G.imag).astype(BF)
    W64i = np.exp(2j * np.pi * np.outer(np.arange(64), np.arange(64)) / 64)
    c["w64i_re"] = (W64i.real / 8192.0).astype(np.float32).astype(BF)
    c["w64i_imn"] = (-W64i.imag / 8192.0).astype(np.float32).astype(BF)
    L = 4096
    t = np.arange(L, dtype=np.float64) / (L - 1)
    angf = 2.0 * math.pi * np.arange(L, dtype=np.float64) / L
    phase = 0.0001 * angf
    c["pe_t"] = np.stack([t, np.cos(phase), -np.sin(phase)], 0).astype(np.float32)
    dmax, dmin = (math.log(0.01) / 0.3, math.log(0.01) / 1.5)
    deltas = np.abs(np.linspace(dmax, dmin, 1024)).astype(np.float64)
    c["deltas"] = deltas.astype(np.float32)[:, None]
    c["t_row"] = t.astype(np.float32)[None, :]
    return c



def ceil_div(a, b):
    return (a + b - 1) // b


class LB:

    def __init__(self, name, nsw=1):
        self.nc = bass.Bass("TRN2", target_bir_lowering=False, num_swdge_queues=nsw)
        self.p = Prog(self.nc)
        from contextlib import ExitStack
        self.ctx = ExitStack()
        self.cur = self.ctx
        self.consts = {}
        self.const_arrays = {}
        self.final = []
        self.tag = ""

    def scoped(self):
        from contextlib import ExitStack, contextmanager
        lb = self

        @contextmanager
        def _s():
            old = lb.cur
            old_sid = getattr(lb, "scope_id", 0)
            lb.scope_id = getattr(lb, "scope_seq", 0) + 1
            lb.scope_seq = lb.scope_id
            lb.cur = ExitStack()
            try:
                yield
            finally:
                lb.cur.close()
                lb.cur = old
                lb.scope_id = old_sid
        return _s()

    def dram_in(self, name, shape, dt):
        return self.nc.dram_tensor(name, list(shape), dt, kind="ExternalInput").ap()

    def dram_out(self, name, shape, dt):
        return self.nc.dram_tensor(name, list(shape), dt, kind="ExternalOutput").ap()

    def dram_scratch(self, name, shape, dt, shared=False):
        if shared:
            return self.nc.dram_tensor(self.tag + name, list(shape), dt, kind="Internal",
                                       addr_space="Shared").ap()
        return self.nc.dram_tensor(self.tag + name, list(shape), dt, kind="Internal").ap()

    def sb(self, name, shape, dt):
        sid = getattr(self, "scope_id", 0)
        return self.cur.enter_context(self.nc.sbuf_tensor(
            f"{self.tag}s{sid}_{name}", list(shape), dt))

    def ps(self, name, shape, dt=F32):
        sid = getattr(self, "scope_id", 0)
        return self.cur.enter_context(self.nc.psum_tensor(
            f"{self.tag}s{sid}_{name}", list(shape), dt))

    def const(self, name, arr, dt=None, queue="sync"):
        sid = getattr(self, "scope_id", 0)
        key = (name, sid)
        if key in self.consts:
            return self.consts[key]
        arr = np.asarray(arr)
        if dt is None:
            dt = {np.dtype(np.float32): F32, BF: BF16, np.dtype(np.int32): I32}[arr.dtype]
        dname = "c_" + name
        if dname in self.const_arrays:
            d = self._const_d[dname]
        else:
            d = self.dram_in(dname, arr.shape, dt)
            self.const_arrays[dname] = arr
            if not hasattr(self, "_const_d"):
                self._const_d = {}
            self._const_d[dname] = d
        shape = list(arr.shape)
        t = self.cur.enter_context(self.nc.sbuf_tensor(f"cs{sid}_{name}", shape, dt))
        tok = self.p.dma(queue, [(t[:], d)])
        self.consts[key] = (t, tok)
        return t, tok

    def finish(self, toks):
        self.final.extend(toks)

    def materialize(self):
        self.p.materialize(self.ctx, self.final)
        self.ctx.close()
        return self.nc


def bcast_row(lb, row_ap, row_tok, psum, sbuf_out, n, deps=()):
    ones, otok = lb.const("ones_row", np.ones((1, 128), np.float32))
    t1 = lb.p.op("tensor", lambda e: e.matmul(psum[:, 0:n], ones[:], row_ap, start=True, stop=True),
                 deps=[otok, row_tok, *deps])
    t2 = lb.p.op("vector", lambda e: e.tensor_copy(sbuf_out[:, 0:n], psum[:, 0:n]), deps=[t1])
    return t2




def emit_ln_transpose(lb, x_dram, xnT, nt=32, out_bf=True):
    p = lb.p
    identbf, tok_id = lb.const("identity_bf", np.eye(128, dtype=np.float32).astype(BF))
    epsc, tok_eps = lb.const("eps_col", np.full((128, 1), 1e-5, np.float32))
    xtile = [lb.sb(f"lnx{i}", [128, 1024], F32) for i in range(2)]
    xcb = lb.sb("lnxcb", [128, 1024], BF16)
    xc = lb.sb("lnxc", [128, 1024], F32)
    sq = lb.sb("lnsq", [128, 1024], F32)
    st = lb.sb("lnst", [128, 4], F32)
    psb = [lb.ps(f"lnps{i}", [128, 128], BF16) for i in range(2)]
    done = []
    prev_tile_last = [None, None]
    prev_stats = None
    prev_evac_last = None
    prev_tr = {}
    for tb in range(nt):
        bi = tb % 2
        dl = [prev_tile_last[bi]] if prev_tile_last[bi] is not None else []
        ld = p.dma("sync", [(xtile[bi][:], x_dram[tb * 128:(tb + 1) * 128, :])], deps=dl)
        dst = [prev_stats] if prev_stats is not None else []
        t1 = p.op("vector", (lambda _x: lambda e: e.reduce_sum(st[:, 0:1], _x[:], AX.X))(xtile[bi]),
                  deps=[ld] + dst)
        t2 = p.op("vector", lambda e: e.tensor_scalar_mul(st[:, 0:1], st[:, 0:1], -1.0 / 1024.0), deps=[t1])
        t3 = p.op("vector", (lambda _x: lambda e: e.tensor_scalar_add(xc[:], _x[:], st[:, 0:1]))(xtile[bi]),
                  deps=[t2])
        t4 = p.op("scalar", lambda e: e.activation(sq[:], xc[:], AF.Square, accum_out=st[:, 1:2]), deps=[t3])
        t5 = p.op("scalar", lambda e: e.activation(st[:, 2:3], st[:, 1:2], AF.Sqrt,
                                                   scale=1.0 / 1024.0, bias=epsc[:, 0:1]),
                  deps=[t4, tok_eps])
        t6 = p.op("vector", lambda e: e.reciprocal(st[:, 3:4], st[:, 2:3]), deps=[t5])
        dxcb = [prev_evac_last] if prev_evac_last is not None else []
        t7 = p.op("vector", lambda e: e.tensor_scalar_mul(xcb[:], xc[:], st[:, 3:4]),
                  deps=[t6] + dxcb)
        prev_stats = t7
        evs = []
        for dc in range(8):
            pb = psb[dc % 2]
            dep_war = [prev_tr[dc % 2]] if (dc % 2) in prev_tr else []
            tr = p.op("tensor", (lambda _pb, _dc: lambda e: e.transpose(
                _pb[:], xcb[:, _dc * 128:(_dc + 1) * 128], identbf[:]))(pb, dc),
                deps=[t7, tok_id] + dep_war)
            ev = p.op("vector", (lambda _pb, _dc, _tb: lambda e: e.tensor_copy(
                xnT[:, _dc, _tb * 128:(_tb + 1) * 128], _pb[:]))(pb, dc, tb), deps=[tr])
            prev_tr[dc % 2] = ev
            evs.append(ev)
        prev_tile_last[bi] = t3
        prev_evac_last = evs[-1]
        done.append(evs[-1])
    return done


def emit_rope(lb, qT, nheads2, T, base_toks, tag=""):
    p = lb.p
    hc = host_consts()
    C64, tc = lb.const("rope_c64", np.concatenate([hc["rope_cos"], hc["rope_cos"]], 0))
    S64, ts_ = lb.const("rope_s64", np.concatenate([-hc["rope_sin"], hc["rope_sin"]], 0))
    key = lb.tag + "ropescratch" + str(nheads2) + "_" + str(T)
    if key not in lb.consts:
        lb.consts[key] = (lb.sb("ropeA" + key, [64, nheads2, T], BF16),)
    (tmpA,) = lb.consts[key]

    def bc(tbl):
        a = tbl[0:64, 0:T]
        return bass.AP(a.tensor, a.offset, [a.ap[0], [0, nheads2], a.ap[1]])

    t1 = p.op("vector", lambda e: e.tensor_copy(tmpA[0:32, :, :], qT[32:64, :, :]), deps=base_toks)
    t2 = p.op("vector", lambda e: e.tensor_copy(tmpA[32:64, :, :], qT[0:32, :, :]), deps=base_toks)
    t3 = p.op("vector", lambda e: e.tensor_mul(tmpA[:], tmpA[:], bc(S64)), deps=[t1, t2, ts_])
    t4 = p.op("vector", lambda e: e.tensor_mul(qT[:, :, :], qT[:, :, :], bc(C64)), deps=[t1, t2, tc])
    t5 = p.op("vector", lambda e: e.tensor_add(qT[:, :, :], qT[:, :, :], tmpA[:]), deps=[t3, t4])
    return [t5]


def emit_proj_fm(lb, wtile, wtok, xnT, xn_toks, outT, h2, tchunks, psum, kc_n=8,
                 guards=()):
    p = lb.p
    toks = []
    prev_ev = list(guards)
    for tc in range(tchunks):
        mms = []
        for kc in range(kc_n):
            t = p.op("tensor", (lambda _kc, _tc: lambda e: e.matmul(
                psum[:], wtile[:, _kc, :], xnT[:, _kc, _tc * 512:(_tc + 1) * 512],
                start=(_kc == 0), stop=(_kc == kc_n - 1)))(kc, tc),
                deps=([wtok] + xn_toks + prev_ev) if kc == 0 else [])
            mms.append(t)
        e1 = p.op("vector", (lambda _tc: lambda e: e.tensor_copy(
            outT[:, 0, _tc * 512:(_tc + 1) * 512], psum[0:64, :]))(tc), deps=[mms[-1]])
        e2 = p.op("vector", (lambda _tc: lambda e: e.tensor_copy(
            outT[:, 1, _tc * 512:(_tc + 1) * 512], psum[64:128, :]))(tc), deps=[mms[-1]])
        prev_ev = [e1, e2]
        toks += [e1, e2]
    return toks


def emit_wo_partial(lb, E, att_dram, att_done, nheads=8):
    p = lb.p
    wo_t = [lb.sb(f"wo_t{i}", [64, nheads, 128], BF16) for i in range(2)]
    attb = [lb.sb(f"woatt{i}", [64, nheads, 512], BF16) for i in range(2)]
    wev = [lb.sb(f"woev{i}", [128, 512], BF16) for i in range(2)]
    psW = lb.ps("psWo", [128, 512], F32)
    rs_in = E["rs_in"]
    toks = []
    prev_ev = {}
    prev_lda = {}
    prev_ldw = {}
    prev_mm = []
    for tc in range(8):
        ai = tc % 2
        dla = [prev_lda[ai]] if ai in prev_lda else []
        lda = p.dma("sync", [(attb[ai][:], att_dram[:, :, tc * 512:(tc + 1) * 512])],
                    deps=dla + att_done)
        last_u = None
        for oc in range(8):
            i = (oc + tc * 8) % 2
            dlw = [prev_ldw[i]] if i in prev_ldw else []
            ldw = p.dma("gpsimd", [(wo_t[i][:], E["wo"][:, :, oc * 128:(oc + 1) * 128])],
                        deps=dlw)
            mms = []
            for h in range(nheads):
                t = p.op("tensor", (lambda _h, _i, _ai: lambda e: e.matmul(
                    psW[:], wo_t[_i][:, _h, :],
                    attb[_ai][:, _h, :],
                    start=(_h == 0), stop=(_h == nheads - 1)))(h, i, ai),
                    deps=([ldw, lda] + prev_mm) if h == 0 else [])
                mms.append(t)
            prev_ldw[i] = mms[-1]
            dl = [prev_ev[i]] if i in prev_ev else []
            ev = p.op("vector", (lambda _i: lambda e: e.tensor_copy(
                wev[_i][:], psW[:]))(i), deps=[mms[-1]] + dl)
            prev_mm = [ev]
            last_u = ev
            hh = tc // 4
            st = p.dma("sync", [(rs_in[hh * 128:(hh + 1) * 128,
                                       oc * 2048 + (tc % 4) * 512:
                                       oc * 2048 + (tc % 4) * 512 + 512], wev[i][:])],
                       deps=[ev])
            prev_ev[i] = st
            toks.append(st)
        prev_lda[ai] = last_u
    return toks


def emit_l0a(lb, E):
    nc, p = lb.nc, lb.p
    wq, wk, wv = E["wq"], E["wk"], E["wv"]
    xnT, xn_toks = E["xnT"], E["xn_toks"]

    hc = host_consts()
    identbf, tok_id = lb.const("identity_bf", np.eye(128, dtype=np.float32).astype(BF))
    m512 = np.concatenate([hc["mask_triu"], np.zeros((128, 384), np.float32)], 1)
    mk512, tok_m512 = lb.const("m512", m512)
    mtril, tok_tril = lb.const("mtril", hc["mask_tril"])

    att2 = lb.sb("att2", [64, 2, 4096], BF16)
    qT = lb.sb("qT", [64, 2, 4096], BF16)
    kT = lb.sb("kT", [64, 2, 4096], BF16)
    vtok = lb.sb("vtok", [128, 32, 128], BF16)
    wq_t = lb.sb("wq_t", [128, 8, 128], BF16)
    wk_t = lb.sb("wk_t", [128, 8, 128], BF16)
    wv_t = lb.sb("wv_t", [128, 8, 128], BF16)
    Ssb = lb.sb("Ssb", [128, 640], F32)
    Pbf = lb.sb("Pbf", [128, 640], BF16)
    PT2 = [lb.sb(f"PT{i}", [128, 128], BF16) for i in range(2)]
    red = lb.sb("red", [128, 4], F32)
    psA = lb.ps("psA", [128, 512], F32)
    psB = lb.ps("psB", [128, 128], F32)
    psV = lb.ps("psV", [128, 128], F32)
    psT2 = [lb.ps(f"psT{i}", [128, 128], BF16) for i in range(2)]
    psO = lb.ps("psO", [64, 128], F32)

    pair_guard = []
    att_done_all = []
    for hp in range(4):
        csl = slice(hp * 128, (hp + 1) * 128)
        ldw = p.dma("gpsimd", [
            (wq_t[:], wq[:, csl].rearrange("(c p) n -> p c n", p=128)),
            (wk_t[:], wk[:, csl].rearrange("(c p) n -> p c n", p=128)),
            (wv_t[:], wv[:, csl].rearrange("(c p) n -> p c n", p=128)),
        ], deps=pair_guard)
        qtoks = emit_proj_fm(lb, wq_t, ldw, xnT, xn_toks, qT, 2, 8, psA, guards=pair_guard)
        ktoks = emit_proj_fm(lb, wk_t, ldw, xnT, xn_toks, kT, 2, 8, psA, guards=qtoks[-1:])
        vtoks = []
        prev_v = ktoks[-1:]
        for tb in range(32):
            mms = []
            for kc in range(8):
                t = p.op("tensor", (lambda _kc, _tb: lambda e: e.matmul(
                    psV[:], xnT[:, _kc, _tb * 128:(_tb + 1) * 128], wv_t[:, _kc, :],
                    start=(_kc == 0), stop=(_kc == 7)))(kc, tb),
                    deps=([ldw] + prev_v) if kc == 0 else [])
                mms.append(t)
            ev = p.op("vector", (lambda _tb: lambda e: e.tensor_copy(
                vtok[:, _tb, :], psV[:]))(tb), deps=[mms[-1]])
            prev_v = [ev]
            vtoks.append(ev)
        rq = emit_rope(lb, qT, 2, 4096, qtoks)
        rk = emit_rope(lb, kT, 2, 4096, ktoks + rq)
        prev_unit = []
        for h2 in range(2):
            for qb in range(32):
                I = qb
                Jlo = max(0, I - 4)
                nfull = I - Jlo
                nk = (nfull + 1) * 128
                deps0 = rq + rk + [vtoks[-1]] + prev_unit
                t_mm1 = None
                if nfull > 0:
                    t_mm1 = p.op("tensor", (lambda _h, _qb, _J, _n: lambda e: e.matmul(
                        psA[:, 0:_n * 128], qT[:, _h, _qb * 128:(_qb + 1) * 128],
                        kT[:, _h, _J * 128:(_J + _n) * 128], start=True, stop=True))(h2, qb, Jlo, nfull),
                        deps=deps0)
                t_mm2 = p.op("tensor", (lambda _h, _qb: lambda e: e.matmul(
                    psB[:], qT[:, _h, _qb * 128:(_qb + 1) * 128],
                    kT[:, _h, _qb * 128:(_qb + 1) * 128], start=True, stop=True))(h2, qb),
                    deps=deps0)
                evs = []
                if nfull > 0:
                    if I >= 4:
                        ev1 = p.op("vector", (lambda _n: lambda e: e.tensor_add(
                            Ssb[:, 0:_n * 128], psA[:, 0:_n * 128], mk512[:, 0:_n * 128]))(nfull),
                            deps=[t_mm1, tok_m512])
                    else:
                        ev1 = p.op("vector", (lambda _n: lambda e: e.tensor_copy(
                            Ssb[:, 0:_n * 128], psA[:, 0:_n * 128]))(nfull), deps=[t_mm1])
                    evs.append(ev1)
                ev2 = p.op("vector", (lambda _n: lambda e: e.tensor_add(
                    Ssb[:, _n * 128:_n * 128 + 128], psB[:], mtril[:]))(nfull),
                    deps=[t_mm2, tok_tril])
                evs.append(ev2)
                r1 = p.op("vector", (lambda _nk: lambda e: e.reduce_max(
                    red[:, 0:1], Ssb[:, 0:_nk], AX.X, negate=True))(nk), deps=evs)
                r2 = p.op("vector", lambda e: e.tensor_scalar_mul(red[:, 0:1], red[:, 0:1], 0.125),
                          deps=[r1])
                ex = p.op("scalar", (lambda _nk: lambda e: e.activation(
                    Pbf[:, 0:_nk], Ssb[:, 0:_nk], AF.Exp, bias=red[:, 0:1], scale=0.125,
                    accum_out=red[:, 1:2]))(nk), deps=[r2])
                rc = p.op("vector", lambda e: e.reciprocal(red[:, 2:3], red[:, 1:2]), deps=[ex])
                sc = p.op("scalar", (lambda _nk: lambda e: e.activation(
                    Pbf[:, 0:_nk], Pbf[:, 0:_nk], AF.Copy, scale=red[:, 2:3]))(nk),
                    deps=[rc])
                pv_mms = []
                prev_pe = {}
                for jj in range(nfull + 1):
                    J = Jlo + jj
                    bi = jj % 2
                    war = [prev_pe[bi]] if bi in prev_pe else []
                    tr = p.op("tensor", (lambda _jj, _bi: lambda e: e.transpose(
                        psT2[_bi][:], Pbf[:, _jj * 128:(_jj + 1) * 128], identbf[:]))(jj, bi),
                        deps=[sc, tok_id] + war)
                    pe = p.op("vector", (lambda _bi: lambda e: e.tensor_copy(
                        PT2[_bi][:], psT2[_bi][:]))(bi),
                        deps=[tr] + pv_mms[-2:-1])
                    prev_pe[bi] = pe
                    mv = p.op("tensor", (lambda _J, _jj, _h, _last, _bi: lambda e: e.matmul(
                        psO[:], vtok[:, _J, _h * 64:(_h + 1) * 64], PT2[_bi][:],
                        start=(_jj == 0), stop=_last))(J, jj, h2, jj == nfull, bi),
                        deps=[pe] + (prev_unit if jj == 0 else []))
                    pv_mms.append(mv)
                ao = p.op("vector", (lambda _h, _qb: lambda e: e.tensor_copy(
                    att2[:, _h, _qb * 128:(_qb + 1) * 128], psO[:]))(h2, qb),
                    deps=[pv_mms[-1]])
                prev_unit = [ao]
        sta = p.dma("sync", [(E["att_dram"][:, hp * 2 + h2, :], att2[:, h2, :])
                             for h2 in range(2)], deps=prev_unit)
        pair_guard = [sta]
        att_done_all.append(sta)
    return emit_wo_partial(lb, E, E["att_dram"], att_done_all, nheads=8)

def emit_dequant(lb, q_dram, s_dram, out_dram, K, N, tag, psb=None, guard=()):
    p = lb.p
    ones_row, tok_or = lb.const("ones_row_f32", np.ones((1, 128), np.float32))
    G = K // 128
    CC = N // 512
    qtile = [lb.sb(f"dq_q{tag}{i}", [128, 512], F32) for i in range(2)]
    scb = [lb.sb(f"dq_s{tag}{i}", [128, 512], F32) for i in range(2)]
    wdq = [lb.sb(f"dq_w{tag}{i}", [128, 512], BF16) for i in range(2)]
    srow = [lb.sb(f"dq_r{tag}{i}", [1, N], F32) for i in range(2)]
    if psb is None:
        psb = lb.ps(f"dq_ps{tag}", [128, 512], F32)
    toks = []
    prev = {}
    prev_row = {}
    pv_ev = list(guard)[-1] if guard else None
    for g in range(G):
        gi = g % 2
        rl = [prev_row[gi]] if gi in prev_row else []
        ldr = p.dma("sync", [(srow[gi][:], s_dram[g:g + 1, :])], deps=rl)
        last_mm = None
        for cc in range(CC):
            i = (g * CC + cc) % 2
            dl = [prev[i]] if i in prev else []
            ld = p.dma("gpsimd", [(qtile[i][:], q_dram[g * 128:(g + 1) * 128,
                                                        cc * 512:(cc + 1) * 512])], deps=dl)
            mmdeps = [ldr, tok_or] + ([pv_ev] if pv_ev is not None else [])
            mm = p.op("tensor", (lambda _gi, _cc: lambda e: e.matmul(
                psb[:], ones_row[:], srow[_gi][0:1, _cc * 512:(_cc + 1) * 512],
                start=True, stop=True))(gi, cc), deps=mmdeps)
            last_mm = mm
            ev = p.op("vector", (lambda _i: lambda e: e.tensor_copy(scb[_i][:], psb[:]))(i),
                      deps=[mm] + dl)
            pv_ev = ev
            t1 = p.op("vector", (lambda _i: lambda e: e.tensor_scalar_add(
                qtile[_i][:], qtile[_i][:], -128.0))(i), deps=[ld] + dl)
            t2 = p.op("vector", (lambda _i: lambda e: e.tensor_mul(
                wdq[_i][:], qtile[_i][:], scb[_i][:]))(i), deps=[t1, ev])
            st = p.dma("sync", [(out_dram[g * 128:(g + 1) * 128, cc * 512:(cc + 1) * 512],
                                 wdq[i][:])], deps=[t2])
            prev[i] = st
            toks.append(st)
        prev_row[gi] = last_mm
    return toks


def emit_l0b(lb, E):
    nc, p = lb.nc, lb.p
    x = E["x"]
    g_d, bt_d = E["g_r"], E["b_r"]
    q1_d, s1_d, b1_d = E["fc1_q"], E["fc1_s"], E["fc1_b"]
    q2_d, s2_d, b2_d = E["fc2_q"], E["fc2_s"], E["fc2_b"]
    o_x1 = E["x_out"]
    xnT, ld_xn_l = E["xnT"], E["xn_toks"]
    w1dq = lb.dram_scratch("w1dq", [1024, 4096], BF16)
    w2dq = lb.dram_scratch("w2dq", [4096, 1024], BF16)

    identbf, tok_id = lb.const("identity_bf", np.eye(128, dtype=np.float32).astype(BF))
    epsc, tok_eps = lb.const("eps_col", np.full((128, 1), 1e-5, np.float32))
    onesb, tok_ob = lb.const("ones_col_bf", np.ones((128, 1), np.float32).astype(BF))
    onesf, tok_of = lb.const("ones_col_f32", np.ones((128, 1), np.float32))

    aoutT = lb.sb("aoutT", [128, 8, 2048], BF16)
    gsb = lb.sb("gsb", [128, 8], F32)
    btsb = lb.sb("btsb", [128, 8], F32)
    b1sb = lb.sb("b1sb", [128, 32], F32)
    b2sb = lb.sb("b2sb", [128, 8], F32)
    b2h = lb.sb("b2h", [128, 8], F32)

    ld_small = p.dma("sync", [(gsb[:], g_d), (btsb[:], bt_d), (b1sb[:], b1_d),
                              (b2sb[:], b2_d)])
    t_b2h = p.op("vector", lambda e: e.tensor_scalar_mul(b2h[:], b2sb[:], 0.5), deps=[ld_small])
    ld_ao = p.dma("sync", [(aoutT[:], E["rs_out"].rearrange("p (h t) -> p h t", h=8))],
                  deps=E["rs_tok"])
    ao_toks = [ld_ao]
    ld_xn = p.op("vector", lambda e: e.engine_nop(), deps=ld_xn_l)

    psW = lb.ps("psW", [128, 512], F32)
    psR = lb.ps("psR", [1, 512], F32)
    psR2 = lb.ps("psR2", [1, 512], F32)
    psB = lb.ps("psB", [128, 512], F32)
    psY = lb.ps("psY", [128, 512], F32)
    psZ = lb.ps("psZ", [128, 512], F32)
    psbf = [lb.ps(f"psbf{i}", [128, 128], BF16) for i in range(2)]

    with lb.scoped():
        dq1 = emit_dequant(lb, q1_d, s1_d, w1dq, 1024, 4096, "1", psb=psW,
                           guard=ao_toks[-1:])
        dq2 = emit_dequant(lb, q2_d, s2_d, w2dq, 4096, 1024, "2", psb=psW,
                           guard=dq1[-1:])
    dq_done = p.op("vector", lambda e: e.engine_nop(), deps=dq1 + dq2)
    dq1 = [dq_done]
    dq2 = [dq_done]

    with lb.scoped():
        xn2 = lb.sb("xn2", [128, 8, 512], BF16)
        sqc = lb.sb("sqc", [128, 512], F32)
        rowst = lb.sb("rowst", [1, 8, 512], F32)
        mub = lb.sb("mub", [128, 512], BF16)
        rstdb = lb.sb("rstdb", [128, 512], BF16)
        w1t = [lb.sb(f"w1t{i}", [128, 8, 128], BF16) for i in range(2)]
        w2t = [lb.sb(f"w2t{i}", [128, 32, 128], BF16) for i in range(2)]
        ysb = lb.sb("ysb", [128, 32, 512], BF16)
        ztmp = lb.sb("ztmp", [128, 512], BF16)
        ones_row, tok_or = lb.const("ones_row_f32", np.ones((1, 128), np.float32))

        ffn_toks = []
        prev_w1 = {}
        prev_w2 = {}
        prev_tc = list(dq1[-1:] + dq2[-1:])
        for tc in range(4):
            tsl = slice(tc * 512, (tc + 1) * 512)
            mmsM = []
            for dc in range(8):
                t = p.op("tensor", (lambda _dc, _tc: lambda e: e.matmul(
                    psR[:], onesb[:], xnT[:, _dc, _tc * 512:(_tc + 1) * 512],
                    start=(_dc == 0), stop=(_dc == 7)))(dc, tc),
                    deps=([ld_xn, tok_ob] + prev_tc + dq1[-1:] + dq2[-1:]) if dc == 0 else [])
                mmsM.append(t)
            evM = p.op("vector", lambda e: e.tensor_scalar_mul(rowst[0:1, 0, :], psR[:], 1.0 / 1024.0),
                       deps=[mmsM[-1]] + dq1[-1:] + dq2[-1:])
            sq_toks = []
            for dc in range(8):
                s1 = p.op("scalar", (lambda _dc, _tc: lambda e: e.activation(
                    sqc[:], xnT[:, _dc, _tc * 512:(_tc + 1) * 512], AF.Square))(dc, tc),
                    deps=[ld_xn] + prev_tc + sq_toks[-1:])
                s2 = p.op("tensor", (lambda _dc: lambda e: e.matmul(
                    psR2[:], onesf[:], sqc[:], start=(_dc == 0), stop=(_dc == 7)))(dc),
                    deps=[s1, tok_of])
                sq_toks.append(s2)
            evS = p.op("vector", lambda e: e.tensor_scalar_mul(rowst[0:1, 1, :], psR2[:], 1.0 / 1024.0),
                       deps=[sq_toks[-1]])
            v1 = p.op("vector", lambda e: e.tensor_mul(rowst[0:1, 2, :], rowst[0:1, 0, :],
                                                       rowst[0:1, 0, :]), deps=[evM])
            v2 = p.op("vector", lambda e: e.tensor_sub(rowst[0:1, 2, :], rowst[0:1, 1, :],
                                                       rowst[0:1, 2, :]), deps=[v1, evS])
            v3 = p.op("scalar", lambda e: e.activation(rowst[0:1, 3, :], rowst[0:1, 2, :], AF.Sqrt,
                                                       bias=epsc[0:1, 0:1]), deps=[v2, tok_eps])
            v4 = p.op("vector", lambda e: e.reciprocal(rowst[0:1, 2, :], rowst[0:1, 3, :]), deps=[v3])
            bm = p.op("tensor", lambda e: e.matmul(psB[:], ones_row[:], rowst[0:1, 0, :],
                                                   start=True, stop=True), deps=[evM, tok_or, v1])
            bme = p.op("vector", lambda e: e.tensor_copy(mub[:], psB[:]), deps=[bm])
            br = p.op("tensor", lambda e: e.matmul(psB[:], ones_row[:], rowst[0:1, 2, :],
                                                   start=True, stop=True), deps=[v4, bme])
            bre = p.op("vector", lambda e: e.tensor_copy(rstdb[:], psB[:]), deps=[br])
            xn2_toks = []
            for dc in range(8):
                n1 = p.op("vector", (lambda _dc, _tc: lambda e: e.tensor_sub(
                    xn2[:, _dc, :], xnT[:, _dc, _tc * 512:(_tc + 1) * 512], mub[:]))(dc, tc),
                    deps=[bme] + prev_tc)
                n2 = p.op("vector", (lambda _dc: lambda e: e.tensor_mul(
                    xn2[:, _dc, :], xn2[:, _dc, :], rstdb[:]))(dc), deps=[n1, bre])
                n3 = p.op("vector", (lambda _dc: lambda e: e.tensor_scalar(
                    xn2[:, _dc, :], xn2[:, _dc, :], gsb[:, _dc:_dc + 1], btsb[:, _dc:_dc + 1],
                    ALU.mult, ALU.add))(dc), deps=[n2, ld_small])
                xn2_toks.append(n3)
            y_toks = []
            for oc in range(32):
                i = oc % 2
                wl = [prev_w1[i]] if i in prev_w1 else []
                ldw = p.dma("gpsimd", [(w1t[i][:], w1dq[:, oc * 128:(oc + 1) * 128]
                                        .rearrange("(c p) n -> p c n", p=128))],
                            deps=wl + (dq1 if (oc < 2 and tc == 0) else []))
                mms = []
                for kc in range(8):
                    t = p.op("tensor", (lambda _kc, _i: lambda e: e.matmul(
                        psY[:], w1t[_i][:, _kc, :], xn2[:, _kc, :],
                        start=(_kc == 0), stop=(_kc == 7)))(kc, i),
                        deps=([ldw] + xn2_toks + y_toks[-1:]) if kc == 0 else [])
                    mms.append(t)
                ya = p.op("scalar", (lambda _oc: lambda e: e.activation(
                    ysb[:, _oc, :], psY[:], AF.Silu, bias=b1sb[:, _oc:_oc + 1]))(oc),
                    deps=[mms[-1], ld_small])
                prev_w1[i] = ya
                y_toks.append(ya)
            for zc in range(8):
                i = zc % 2
                wl = [prev_w2[i]] if i in prev_w2 else []
                ldw = p.dma("gpsimd", [(w2t[i][:], w2dq[:, zc * 128:(zc + 1) * 128]
                                        .rearrange("(c p) n -> p c n", p=128))],
                            deps=wl + (dq2 if (zc < 2 and tc == 0) else []))
                mms = []
                for kc in range(32):
                    t = p.op("tensor", (lambda _kc, _i: lambda e: e.matmul(
                        psZ[:], w2t[_i][:, _kc, :], ysb[:, _kc, :],
                        start=(_kc == 0), stop=(_kc == 31)))(kc, i),
                        deps=([ldw] + y_toks + ffn_toks[-1:]) if kc == 0 else [])
                    mms.append(t)
                z1 = p.op("vector", (lambda _zc: lambda e: e.tensor_scalar(
                    ztmp[:], psZ[:], 0.5, b2h[:, _zc:_zc + 1], ALU.mult, ALU.add))(zc),
                    deps=[mms[-1], t_b2h])
                z2 = p.op("vector", (lambda _zc, _tc: lambda e: e.tensor_add(
                    ztmp[:], ztmp[:],
                    xnT[:, _zc, _tc * 512:(_tc + 1) * 512]))(zc, tc), deps=[z1, ld_xn])
                z3 = p.op("vector", (lambda _zc, _tc: lambda e: e.tensor_add(
                    aoutT[:, _zc, _tc * 512:(_tc + 1) * 512],
                    aoutT[:, _zc, _tc * 512:(_tc + 1) * 512], ztmp[:]))(zc, tc),
                    deps=[z2] + ao_toks)
                prev_w2[i] = z3
                ffn_toks.append(z3)
            prev_tc = [ffn_toks[-1]]

        ffn_final = ffn_toks[-1:]

    sum_toks = ffn_final
    xtile = [lb.sb(f"rx{i}", [128, 1024], F32) for i in range(2)]
    otile = [lb.sb(f"ro{i}", [128, 1024], F32) for i in range(2)]
    ttmp = lb.sb("ttmp", [128, 128], F32)
    st_toks = []
    prev_st = {}
    prev_trr = {}
    prev_tt = None
    for tb in range(16):
        i = tb % 2
        dl = [prev_st[i]] if i in prev_st else []
        ld = p.dma("sync", [(xtile[i][:], x[tb * 128:(tb + 1) * 128, :])],
                   deps=dl + (ffn_final if tb < 2 else []))
        adds = []
        for dc in range(8):
            pb = psbf[dc % 2]
            war = [prev_trr[dc % 2]] if (dc % 2) in prev_trr else []
            tr = p.op("tensor", (lambda _dc, _tb, _pb: lambda e: e.transpose(
                _pb[:], aoutT[:, _dc, _tb * 128:(_tb + 1) * 128], identbf[:]))(dc, tb, pb),
                deps=sum_toks + [tok_id] + war)
            cp = p.op("vector", (lambda _pb: lambda e: e.tensor_copy(ttmp[:], _pb[:]))(pb),
                      deps=[tr] + ([prev_tt] if prev_tt is not None else []))
            ad = p.op("vector", (lambda _dc, _i: lambda e: e.tensor_add(
                otile[_i][:, _dc * 128:(_dc + 1) * 128],
                xtile[_i][:, _dc * 128:(_dc + 1) * 128], ttmp[:]))(dc, i),
                deps=[cp, ld] + dl)
            prev_trr[dc % 2] = cp
            prev_tt = ad
            adds.append(ad)
        st = p.dma("sync", [(o_x1[tb * 128:(tb + 1) * 128, :], otile[i][:])], deps=adds)
        prev_st[i] = st
        st_toks.append(st)
    return st_toks

def build_l1a():
    raise NotImplementedError


def emit_l1a(lb, E):
    nc, p = lb.nc, lb.p
    wq, wk, wv = E["wq"], E["wk"], E["wv"]
    xnT, xn_toks = E["xnT"], E["xn_toks"]

    hc = host_consts()
    identbf, tok_id = lb.const("identity_bf", np.eye(128, dtype=np.float32).astype(BF))
    mtril, tok_tril = lb.const("mtril", hc["mask_tril"])

    att2 = lb.sb("att2", [64, 2, 4096], BF16)
    qT = lb.sb("qT", [64, 2, 4096], BF16)
    kT = lb.sb("kT", [64, 2, 4096], BF16)
    vtok = lb.sb("vtok", [128, 32, 128], BF16)
    wq_t = lb.sb("wq_t", [128, 8, 128], BF16)
    wk_t = lb.sb("wk_t", [128, 8, 128], BF16)
    wv_t = lb.sb("wv_t", [128, 8, 128], BF16)
    Ssb = lb.sb("Ssb", [128, 4096], F32)
    Pbf = lb.sb("Pbf", [128, 4096], BF16)
    PT2 = [lb.sb(f"PT{i}", [128, 128], BF16) for i in range(2)]
    red = lb.sb("red", [128, 4], F32)
    psA = lb.ps("psA", [128, 512], F32)
    psA2 = lb.ps("psA2", [128, 512], F32)
    psB = lb.ps("psB", [128, 128], F32)
    psV = lb.ps("psV", [128, 128], F32)
    psT2 = [lb.ps(f"psT{i}", [128, 128], BF16) for i in range(2)]
    psO = lb.ps("psO", [64, 128], F32)

    pair_guard = []
    att_done_all = []
    for hp in range(4):
        csl = slice(hp * 128, (hp + 1) * 128)
        ldw = p.dma("gpsimd", [
            (wq_t[:], wq[:, csl].rearrange("(c p) n -> p c n", p=128)),
            (wk_t[:], wk[:, csl].rearrange("(c p) n -> p c n", p=128)),
            (wv_t[:], wv[:, csl].rearrange("(c p) n -> p c n", p=128)),
        ], deps=pair_guard)
        qtoks = emit_proj_fm(lb, wq_t, ldw, xnT, xn_toks, qT, 2, 8, psA, guards=pair_guard)
        ktoks = emit_proj_fm(lb, wk_t, ldw, xnT, xn_toks, kT, 2, 8, psA, guards=qtoks[-1:])
        vtoks = []
        prev_v = ktoks[-1:]
        for tb in range(32):
            mms = []
            for kc in range(8):
                t = p.op("tensor", (lambda _kc, _tb: lambda e: e.matmul(
                    psV[:], xnT[:, _kc, _tb * 128:(_tb + 1) * 128], wv_t[:, _kc, :],
                    start=(_kc == 0), stop=(_kc == 7)))(kc, tb),
                    deps=([ldw] + prev_v) if kc == 0 else [])
                mms.append(t)
            ev = p.op("vector", (lambda _tb: lambda e: e.tensor_copy(
                vtok[:, _tb, :], psV[:]))(tb), deps=[mms[-1]])
            prev_v = [ev]
            vtoks.append(ev)
        rq = emit_rope(lb, qT, 2, 4096, qtoks)
        rk = emit_rope(lb, kT, 2, 4096, ktoks + rq)
        prev_unit = []
        for h2 in range(2):
            for qb in range(32):
                I = qb
                nk = (I + 1) * 128
                deps0 = rq + rk + [vtoks[-1]] + prev_unit
                evs = []
                pos = 0
                prev_evA = {}
                nch = 0
                while pos < I:
                    w = min(4, I - pos)
                    pa = [psA, psA2][nch % 2]
                    dmm = (deps0 if pos == 0 else []) + (
                        [prev_evA[nch % 2]] if (nch % 2) in prev_evA else [])
                    mm = p.op("tensor", (lambda _h, _qb, _pos, _w, _pa: lambda e: e.matmul(
                        _pa[:, 0:_w * 128], qT[:, _h, _qb * 128:(_qb + 1) * 128],
                        kT[:, _h, _pos * 128:(_pos + _w) * 128], start=True, stop=True))(
                        h2, qb, pos, w, pa), deps=dmm)
                    ev = p.op("vector", (lambda _pos, _w, _pa: lambda e: e.tensor_copy(
                        Ssb[:, _pos * 128:(_pos + _w) * 128], _pa[:, 0:_w * 128]))(pos, w, pa),
                        deps=[mm])
                    prev_evA[nch % 2] = ev
                    nch += 1
                    evs.append(ev)
                    pos += w
                mm2 = p.op("tensor", (lambda _h, _qb: lambda e: e.matmul(
                    psB[:], qT[:, _h, _qb * 128:(_qb + 1) * 128],
                    kT[:, _h, _qb * 128:(_qb + 1) * 128], start=True, stop=True))(h2, qb),
                    deps=deps0)
                ev2 = p.op("vector", (lambda _I: lambda e: e.tensor_add(
                    Ssb[:, _I * 128:(_I + 1) * 128], psB[:], mtril[:]))(I),
                    deps=[mm2, tok_tril])
                evs.append(ev2)
                r1 = p.op("vector", (lambda _nk: lambda e: e.reduce_max(
                    red[:, 0:1], Ssb[:, 0:_nk], AX.X, negate=True))(nk), deps=evs)
                r2 = p.op("vector", lambda e: e.tensor_scalar_mul(red[:, 0:1], red[:, 0:1], 0.125),
                          deps=[r1])
                ex = p.op("scalar", (lambda _nk: lambda e: e.activation(
                    Pbf[:, 0:_nk], Ssb[:, 0:_nk], AF.Exp, bias=red[:, 0:1], scale=0.125,
                    accum_out=red[:, 1:2]))(nk), deps=[r2])
                rc = p.op("vector", lambda e: e.reciprocal(red[:, 2:3], red[:, 1:2]), deps=[ex])
                sc = p.op("scalar", (lambda _nk: lambda e: e.activation(
                    Pbf[:, 0:_nk], Pbf[:, 0:_nk], AF.Copy, scale=red[:, 2:3]))(nk),
                    deps=[rc])
                pv_mms = []
                prev_pe = {}
                for J in range(I + 1):
                    bi = J % 2
                    war = [prev_pe[bi]] if bi in prev_pe else []
                    tr = p.op("tensor", (lambda _J, _bi: lambda e: e.transpose(
                        psT2[_bi][:], Pbf[:, _J * 128:(_J + 1) * 128], identbf[:]))(J, bi),
                        deps=[sc, tok_id] + war)
                    pe = p.op("vector", (lambda _bi: lambda e: e.tensor_copy(
                        PT2[_bi][:], psT2[_bi][:]))(bi),
                        deps=[tr] + pv_mms[-2:-1])
                    prev_pe[bi] = pe
                    mv = p.op("tensor", (lambda _J, _h, _last, _bi: lambda e: e.matmul(
                        psO[:], vtok[:, _J, _h * 64:(_h + 1) * 64], PT2[_bi][:],
                        start=(_J == 0), stop=_last))(J, h2, J == I, bi),
                        deps=[pe] + (prev_unit if J == 0 else []))
                    pv_mms.append(mv)
                ao = p.op("vector", (lambda _h, _qb: lambda e: e.tensor_copy(
                    att2[:, _h, _qb * 128:(_qb + 1) * 128], psO[:]))(h2, qb),
                    deps=[pv_mms[-1]])
                prev_unit = [ao]
        sta = p.dma("sync", [(E["att_dram"][:, hp * 2 + h2, :], att2[:, h2, :])
                             for h2 in range(2)], deps=prev_unit)
        pair_guard = [sta]
        att_done_all.append(sta)
    return emit_wo_partial(lb, E, E["att_dram"], att_done_all, nheads=8)

from concourse.bass_core_rust import IndirectOffsetOnAxis

CAP = 5120
NSEG = 16
WCH = 37
NLOC = 128 * WCH


def emit_l1route(lb, E):
    nc, p = lb.nc, lb.p
    xnT, xn_toks = E["xnT"], E["xn_toks"]
    wg_d = E["w_g"]
    o_xt = E["xn_tok"]

    identbf, tok_id = lb.const("identity_bf", np.eye(128, dtype=np.float32).astype(BF))
    identf, tok_idf = lb.const("identity_f32", np.eye(128, dtype=np.float32))
    ident8, tok_id8 = lb.const("ident8", np.eye(8, dtype=np.float32))
    iota8, tok_io = lb.const(
        "iota8_row", np.broadcast_to(np.arange(8, dtype=np.float32)[None, :], (128, 8)).copy())
    ones_row, tok_or = lb.const("ones_row_f32", np.ones((1, 128), np.float32))
    ones8f, tok_o8 = lb.const("ones8_col", np.ones((8, 1), np.float32))
    e5120c, tok_e5 = lb.const("e5120_col", (np.arange(8, dtype=np.float32) * CAP)[:, None])
    tlt8, tok_tlt = lb.const("tril8_lt", np.tril(np.ones((8, 8), np.float32), -1).T.copy())

    wg_sb = lb.sb("wg_sb", [128, 8, 8], BF16)
    ld_wg = p.dma("gpsimd", [(wg_sb[:], wg_d.rearrange("(c p) n -> p c n", p=128))])
    ld_xn = p.op("vector", lambda e: e.engine_nop(), deps=xn_toks)

    psL = lb.ps("psL", [128, 8], F32)
    psT = lb.ps("psT", [128, 128], BF16)
    psTf = lb.ps("psTf", [128, 128], F32)
    psF = lb.ps("psF", [1, 512], F32)
    psC = lb.ps("psC", [128, 512], F32)

    xtile = lb.sb("xtile", [128, 1024], BF16)
    logit = lb.sb("logit", [128, 8], F32)
    top8 = lb.sb("top8", [128, 8], F32)
    topi = lb.sb("topi", [128, 8], U32)
    topif = lb.sb("topif", [128, 8], F32)
    pcols = lb.sb("pcols", [128, 2, 16], F32)
    oh_tok = lb.sb("oh_tok", [128, 2, 8], F32)
    ohT = lb.sb("ohT", [8, 2, 2048], F32)
    rTs = lb.sb("rTs", [8, 2, 2048], F32)
    red2 = lb.sb("red2", [128, 2], F32)

    ztile = lb.sb("ztile", [128, 8192], BF16)
    zm = p.op("vector", lambda e: e.memset(ztile[:], 0.0))
    zd_toks = []
    mdisp = E["mdisp"]
    for zz in range(40):
        zd = p.dma("sync", [(mdisp[zz * 1024:(zz + 1) * 1024, :].rearrange(
            "(a p) n -> p a n", p=128),
            ztile[:].rearrange("p (a n) -> p a n", a=8))], deps=[zm])
        zd_toks.append(zd)

    prev_blk = []
    prev_tr = None
    prev_trh = None
    st_toks = []
    for tb in range(16):
        tsl = slice(tb * 128, (tb + 1) * 128)
        trs = []
        for dc in range(8):
            tr = p.op("tensor", (lambda _dc, _tb: lambda e: e.transpose(
                psT[:], xnT[:, _dc, _tb * 128:(_tb + 1) * 128], identbf[:]))(dc, tb),
                deps=[ld_xn, tok_id] + ([prev_tr] if prev_tr is not None else []))
            ev = p.op("vector", (lambda _dc: lambda e: e.tensor_copy(
                xtile[:, _dc * 128:(_dc + 1) * 128], psT[:]))(dc),
                deps=[tr] + prev_blk)
            prev_tr = ev
            trs.append(ev)
        stx = p.dma("sync", [(o_xt[tsl, :], xtile[:])], deps=trs)
        st_toks.append(stx)
        mms = []
        for kc in range(8):
            t = p.op("tensor", (lambda _kc, _tb: lambda e: e.matmul(
                psL[:], xnT[:, _kc, _tb * 128:(_tb + 1) * 128], wg_sb[:, _kc, :],
                start=(_kc == 0), stop=(_kc == 7)))(kc, tb),
                deps=([ld_xn, ld_wg] + prev_blk) if kc == 0 else [])
            mms.append(t)
        lv = p.op("vector", lambda e: e.tensor_copy(logit[:], psL[:]), deps=[mms[-1]] + prev_blk)
        m8 = p.op("vector", lambda e: e.max(top8[:], logit[:]), deps=[lv])
        mi = p.op("vector", lambda e: e.max_index(topi[:], top8[:], logit[:]), deps=[m8])
        mif = p.op("vector", lambda e: e.tensor_copy(topif[:], topi[:]), deps=[mi])
        d1 = p.op("vector", lambda e: e.tensor_sub(red2[:, 0:1], top8[:, 1:2], top8[:, 0:1]),
                  deps=[m8] + prev_blk)
        sg = p.op("scalar", lambda e: e.activation(red2[:, 1:2], red2[:, 0:1], AF.Exp),
                  deps=[d1] + prev_blk)
        sg2 = p.op("vector", lambda e: e.tensor_scalar_add(red2[:, 1:2], red2[:, 1:2], 1.0),
                   deps=[sg])
        sg3 = p.op("vector", (lambda _tb: lambda e: e.reciprocal(
            pcols[:, 0, _tb:_tb + 1], red2[:, 1:2]))(tb), deps=[sg2])
        sg4 = p.op("vector", (lambda _tb: lambda e: e.tensor_scalar(
            pcols[:, 1, _tb:_tb + 1], pcols[:, 0, _tb:_tb + 1], -1.0, 1.0,
            ALU.mult, ALU.add))(tb), deps=[sg3])
        khs = []
        for k in range(2):
            oh = p.op("vector", (lambda _k: lambda e: e.tensor_single_scalar(
                oh_tok[:, _k, :], iota8[:], topif[:, _k:_k + 1], ALU.is_equal))(k),
                deps=[mif, tok_io] + prev_blk)
            trh = p.op("tensor", (lambda _k: lambda e: e.transpose(
                psTf[0:8, :], oh_tok[:, _k, :], identf[:]))(k),
                deps=[oh, tok_idf] + ([prev_trh] if prev_trh is not None else []))
            evh = p.op("vector", (lambda _k, _tb: lambda e: e.tensor_copy(
                ohT[:, _k, _tb * 128:(_tb + 1) * 128], psTf[0:8, :]))(k, tb), deps=[trh])
            prev_trh = evh
            khs.append(evh)
        prev_blk = khs + [sg4, stx]

    ones8s = lb.sb("ones8s", [8, 1], F32)
    o8 = p.op("vector", lambda e: e.memset(ones8s[:], 1.0))
    rt_toks = []
    cn = lb.sb("cn", [8, 2], F32)
    for k in range(2):
        d0 = bass.AP(ones8s[:].tensor, 0, [[1, 8], [0, 2048]])
        sc = p.op("vector", (lambda _k, _d0: lambda e: e.tensor_tensor_scan(
            rTs[:, _k, :], _d0, ohT[:, _k, :], 0.0, ALU.mult, ALU.add))(k, d0),
            deps=prev_blk + [o8])
        ex = p.op("vector", (lambda _k: lambda e: e.tensor_sub(
            rTs[:, _k, :], rTs[:, _k, :], ohT[:, _k, :]))(k), deps=[sc])
        tt = p.op("vector", (lambda _k: lambda e: e.tensor_add(
            cn[:, _k:_k + 1], rTs[:, _k, 2047:2048], ohT[:, _k, 2047:2048]))(k), deps=[ex])
        rt_toks += [ex, tt]

    st_cn = p.dma("sync", [(E["cn_ag_in"], cn[:])], deps=rt_toks)
    cc_cn = p.cc(lambda e: e.collective_compute(
        "AllGather", ALU.bypass, replica_groups=[[0, 1, 2, 3, 4, 5, 6, 7]],
        ins=[E["cn_ag_in"].opt()], outs=[E["cn_ag_out"].opt()]), deps=[st_cn])
    cnall = lb.sb("cnall", [64, 2], F32)
    mlt = lb.sb("mlt", [64, 8], F32)
    mall = lb.sb("mall", [64, 8], F32)
    ld_cn = p.dma("sync", [(cnall[:], E["cn_ag_out"]), (mlt[:], E["mlt"]),
                           (mall[:], E["mall"])], deps=[cc_cn])
    psP = lb.ps("psP", [8, 8], F32)
    part = lb.sb("part", [8, 2], F32)
    full = lb.sb("full", [8, 2], F32)
    praw = lb.sb("praw", [8, 2], F32)
    kept = lb.sb("kept", [8, 2], F32)
    p37t = lb.sb("p37t", [8, 2], F32)
    p37i = lb.sb("p37i", [8, 2], I32)
    loff = lb.sb("loff", [8, 2], F32)
    goff = lb.sb("goff", [8, 2], F32)
    tmp82 = lb.sb("tmp82", [8, 2], F32)
    mm1 = p.op("tensor", lambda e: e.matmul(psP[:, 0:2], mlt[:], cnall[:], start=True, stop=True),
               deps=[ld_cn])
    ev1 = p.op("vector", lambda e: e.tensor_copy(part[:], psP[:, 0:2]), deps=[mm1])
    mm2 = p.op("tensor", lambda e: e.matmul(psP[:, 0:2], mall[:], cnall[:], start=True, stop=True),
               deps=[ev1])
    ev2 = p.op("vector", lambda e: e.tensor_copy(full[:], psP[:, 0:2]), deps=[mm2])
    q1 = p.op("vector", lambda e: e.tensor_copy(praw[:, 0:1], part[:, 0:1]), deps=[ev1, ev2])
    q2 = p.op("vector", lambda e: e.tensor_add(praw[:, 1:2], full[:, 0:1], part[:, 1:2]),
              deps=[ev1, ev2])
    k1 = p.op("vector", lambda e: e.tensor_scalar(kept[:], praw[:], -1.0, float(CAP),
                                                  ALU.mult, ALU.add), deps=[q1, q2])
    k2 = p.op("vector", lambda e: e.tensor_single_scalar(kept[:], kept[:], 0.0, ALU.max),
              deps=[k1])
    k3 = p.op("vector", lambda e: e.tensor_tensor(kept[:], kept[:], cn[:], ALU.min),
              deps=[k2] + rt_toks)
    f1 = p.op("vector", lambda e: e.tensor_scalar(p37t[:], kept[:], 1.0 / WCH,
                                                  (WCH - 1.0) / WCH, ALU.mult, ALU.add),
              deps=[k3])
    f2 = p.op("vector", lambda e: e.tensor_copy(p37i[:], p37t[:]), deps=[f1])
    f3 = p.op("vector", lambda e: e.tensor_copy(tmp82[:], p37i[:]), deps=[f2])
    f4 = p.op("vector", lambda e: e.tensor_tensor(goff[:], tmp82[:], p37t[:], ALU.is_gt),
              deps=[f3])
    f5 = p.op("vector", lambda e: e.tensor_sub(tmp82[:], tmp82[:], goff[:]), deps=[f4])
    f6 = p.op("vector", lambda e: e.tensor_scalar_mul(p37t[:], tmp82[:], float(WCH)),
              deps=[f5])
    mm3 = p.op("tensor", lambda e: e.matmul(psP[:, 0:2], tlt8[:, 0:8], p37t[:],
                                            start=True, stop=True), deps=[f6, tok_tlt])
    ev3 = p.op("vector", lambda e: e.tensor_copy(tmp82[:], psP[:, 0:2]), deps=[mm3, f5])
    l1_ = p.op("vector", lambda e: e.tensor_add(loff[:, 0:1], tmp82[:, 0:1], tmp82[:, 1:2]),
               deps=[ev3])
    l2_ = p.op("vector", lambda e: e.tensor_add(loff[:, 1:2], loff[:, 0:1], p37t[:, 0:1]),
               deps=[l1_, f6])
    g1 = p.op("vector", lambda e: e.tensor_scalar_add(goff[:, 0:1], praw[:, 0:1],
                                                      e5120c[:, 0:1]), deps=[q1, f4, tok_e5])
    g2 = p.op("vector", lambda e: e.tensor_scalar_add(goff[:, 1:2], praw[:, 1:2],
                                                      e5120c[:, 0:1]), deps=[q2, f4, tok_e5])
    off_done = [l2_, g1, g2, k3]

    scr = lb.sb("scr", [8, 2, 2048], F32)
    rowS = lb.sb("rowS", [1, 2, 2048], F32)
    rowK = lb.sb("rowK", [1, 2, 2048], F32)
    rowL = rowK
    drop = lb.sb("drop", [1, 2, 2048], F32)
    p0row = lb.sb("p0row", [1, 2048], F32)
    slotp = E["slotpos"]
    wrow = E["wrow"]

    def rowsum(dst, colv, colv_deps, k, extra=0.0):
        t1 = p.op("vector", (lambda _k: lambda e: e.tensor_scalar_add(
            scr[:, _k, :], rTs[:, _k, :], colv[:, k:k + 1]))(k),
            deps=rt_toks + colv_deps + prev_blk)
        t2 = p.op("vector", (lambda _k: lambda e: e.tensor_mul(
            scr[:, _k, :], scr[:, _k, :], ohT[:, _k, :]))(k), deps=[t1])
        out = []
        for ff in range(4):
            m1 = p.op("tensor", (lambda _k, _f: lambda e: e.matmul(
                psF[:], ones8f[:], scr[:, _k, _f * 512:(_f + 1) * 512],
                start=True, stop=True))(k, ff), deps=[t2, tok_o8] + out[-1:])
            f0 = p.op("vector", (lambda _k, _f, _d: lambda e: e.tensor_copy(
                _d[0:1, _k, _f * 512:(_f + 1) * 512], psF[0:1, :]))(k, ff, dst), deps=[m1])
            out.append(f0)
        return out[-1:]

    row_toks = []
    drop_toks = {}
    for k in range(2):
        row_toks += rowsum(rowS, goff, off_done, k)
        kt = rowsum(rowK, praw, off_done, k)
        dd0 = p.op("vector", (lambda _k: lambda e: e.tensor_single_scalar(
            drop[0:1, _k, :], rowK[0:1, _k, :], CAP - 0.5, ALU.is_gt))(k), deps=kt)
        drop_toks[k] = dd0
        row_toks += rowsum(rowL, loff, off_done + [dd0], k)
    sidx_toks = []
    for k in range(2):
        dd = drop_toks[k]
        s1 = p.op("vector", (lambda _k: lambda e: e.scalar_tensor_tensor(
            rowS[0:1, _k, :], drop[0:1, _k, :], 1e8, rowS[0:1, _k, :],
            ALU.mult, ALU.add))(k), deps=[dd] + row_toks)
        s2 = p.op("vector", (lambda _k: lambda e: e.tensor_scalar(
            slotp[0:1, _k, :], rowL[0:1, _k, :], -1.0, float(NLOC - 1),
            ALU.mult, ALU.add))(k), deps=row_toks + E.get("slot_war", []))
        s3 = p.op("vector", (lambda _k: lambda e: e.tensor_mul(
            slotp[0:1, _k, :], slotp[0:1, _k, :], drop[0:1, _k, :]))(k), deps=[s2, dd])
        s4 = p.op("vector", (lambda _k: lambda e: e.tensor_add(
            slotp[0:1, _k, :], slotp[0:1, _k, :], rowL[0:1, _k, :]))(k), deps=[s3])
        sidx_toks += [s1, s4]
        row_toks = row_toks + [dd, s1]
    st_pc = p.dma("sync", [(E["pc_bounce"].rearrange("k j p -> p k j"), pcols[:])],
                  deps=prev_blk, slow=True)
    ld_p0 = p.dma("sync", [(p0row[:], E["pc_bounce"].rearrange(
        "k j p -> k (j p)")[0:1, :])], deps=[st_pc])
    w_toks = []
    for k in range(2):
        w1 = p.op("vector", (lambda _k: lambda e: e.tensor_scalar(
            wrow[0:1, _k, :], drop[0:1, _k, :], -1.0, 1.0, ALU.mult, ALU.add))(k),
            deps=sidx_toks + [ld_p0] + E.get("w_war", []))
        if k == 0:
            w2 = p.op("vector", lambda e: e.tensor_mul(
                wrow[0:1, 0, :], wrow[0:1, 0, :], p0row[0:1, :]), deps=[w1])
        else:
            w2a = p.op("vector", lambda e: e.tensor_scalar(
                p0row[0:1, :], p0row[0:1, :], -1.0, 1.0, ALU.mult, ALU.add), deps=[w1] + w_toks)
            w2 = p.op("vector", lambda e: e.tensor_mul(
                wrow[0:1, 1, :], wrow[0:1, 1, :], p0row[0:1, :]), deps=[w2a])
        w_toks.append(w2)

    rl16 = lb.sb("rl16", [1, 16], F32)
    rd16 = lb.sb("rd16", [1, 16], F32)
    rdd = lb.sb("rdd", [8, 2], F32)
    gcol = lb.sb("gcol", [128, 1], F32)
    gtmp = lb.sb("gtmp", [128, 2], F32)
    gidx = E["gidx"]
    gidxf = lb.sb("gidxf", [128, 1], F32)
    c37, tok_c37 = lb.const("c37_col", (WCH * np.arange(128, dtype=np.float32))[:, None])
    dd1 = p.op("vector", lambda e: e.tensor_sub(rdd[:], goff[:], loff[:]), deps=off_done)
    r16_toks = []
    for k in range(2):
        mmr = p.op("tensor", (lambda _k: lambda e: e.matmul(
            psP[0:1, 0:8], loff[:, _k:_k + 1], ident8[:], start=True, stop=True))(k),
            deps=[l2_, tok_id8] + r16_toks)
        evr = p.op("vector", (lambda _k: lambda e: e.tensor_copy(
            rl16[0:1, _k:16:2], psP[0:1, 0:8]))(k), deps=[mmr])
        mmd = p.op("tensor", (lambda _k: lambda e: e.matmul(
            psP[0:1, 0:8], rdd[:, _k:_k + 1], ident8[:], start=True, stop=True))(k),
            deps=[dd1, tok_id8, evr])
        evd = p.op("vector", (lambda _k: lambda e: e.tensor_copy(
            rd16[0:1, _k:16:2], psP[0:1, 0:8]))(k), deps=[mmd])
        r16_toks += [evr, evd]
    diff_toks = []
    prevd = r16_toks
    for sj in range(15, 0, -1):
        dj = p.op("vector", (lambda _s: lambda e: e.tensor_sub(
            rd16[0:1, _s:_s + 1], rd16[0:1, _s:_s + 1], rd16[0:1, _s - 1:_s]))(sj),
            deps=prevd)
        prevd = [dj]
        diff_toks.append(dj)
    rlh = p.op("vector", lambda e: e.tensor_scalar_add(rl16[:], rl16[:], -0.5),
               deps=r16_toks)
    rld2 = lb.sb("rld2", [1, 32], F32)
    il1 = p.op("vector", lambda e: e.tensor_copy(rld2[0:1, 0:32:2], rl16[0:1, :]),
               deps=[rlh])
    il2 = p.op("vector", lambda e: e.tensor_copy(rld2[0:1, 1:32:2], rd16[0:1, :]),
               deps=[il1] + diff_toks)
    gi0 = p.op("vector", lambda e: e.tensor_copy(gidxf[:], c37[:]),
               deps=[tok_c37] + E.get("g_war", []))
    gprev = [gi0, il2]
    for sj in range(16):
        bm = p.op("tensor", (lambda _s: lambda e: e.matmul(
            psC[:, 0:2], ones_row[:], rld2[0:1, 2 * _s:2 * _s + 2],
            start=True, stop=True))(sj), deps=gprev + [tok_or])
        evb = p.op("vector", lambda e: e.tensor_copy(gtmp[:], psC[:, 0:2]), deps=[bm])
        ge2 = p.op("vector", lambda e: e.tensor_tensor(
            gcol[:], c37[:], gtmp[:, 0:1], ALU.is_gt), deps=[evb] + gprev)
        ac = p.op("vector", lambda e: e.tensor_mul(gcol[:], gcol[:], gtmp[:, 1:2]),
                  deps=[ge2])
        ad = p.op("vector", lambda e: e.tensor_add(gidxf[:], gidxf[:], gcol[:]),
                  deps=[ac])
        gprev = [ad]
    gcast = p.op("vector", lambda e: e.tensor_copy(gidx[:], gidxf[:]), deps=gprev)

    st_row = p.dma("sync", [(E["row_b0"], rowS[0:1, 0, :]), (E["row_b1"], rowS[0:1, 1, :])],
                   deps=sidx_toks)
    sidxc = lb.sb("sidxc", [128, 2, 16], I32)
    ld_sc = p.dma("gpsimd", [(sidxc[:, 0, :], E["row_b0"].rearrange("(j p) -> p j", p=128)),
                             (sidxc[:, 1, :], E["row_b1"].rearrange("(j p) -> p j", p=128))],
                  deps=[st_row], slow=True)
    xt2 = [lb.sb(f"xt2_{i}", [128, 1024], BF16) for i in range(2)]
    sc_toks = []
    prev_g = {}
    for tb in range(16):
        i = tb % 2
        dl = [prev_g[i]] if i in prev_g else []
        ldx = p.dma("sync", [(xt2[i][:], E["xn_tok"][tb * 128:(tb + 1) * 128, :])],
                    deps=dl + [st_toks[tb]])
        for k in range(2):
            g = p.idma((lambda _i, _tb, _k: lambda e: e.indirect_dma_start(
                out=mdisp, out_offset=IndirectOffsetOnAxis(
                    sidxc[:, _k, _tb:_tb + 1], 0),
                in_=xt2[_i][:], in_offset=None,
                bounds_check=8 * CAP, oob_is_err=False))(i, tb, k),
                deps=[ldx, ld_sc] + zd_toks)
            prev_g[i] = g
            sc_toks.append(g)
    cmb_ready = [gcast] + w_toks + sidx_toks
    return sc_toks, cmb_ready


def emit_l1exp(lb, E):
    nc, p = lb.nc, lb.p
    identbf, tok_id = lb.const("identity_bf", np.eye(128, dtype=np.float32).astype(BF))
    cfc_d, cpr_d = E["c_fc"], E["c_proj"]
    mbatch, o_oe = E["mbatch"], E["oe_out"]

    cfc = lb.sb("cfc", [128, 8, 4096], BF16)
    cprb = [lb.sb(f"cprb{i}", [128, 32, 128], BF16) for i in range(2)]
    ldw1 = p.dma("gpsimd", [(cfc[:], cfc_d.rearrange("(c p) n -> p c n", p=128))])
    psT = lb.ps("psT", [128, 128], BF16)
    psY = lb.ps("psY", [128, 512], F32)
    psZ = lb.ps("psZ", [128, 512], F32)

    xe_tok = lb.sb("xe_tok", [128, 4, 1024], BF16)
    xeT = lb.sb("xeT", [128, 8, 512], BF16)
    hT = lb.sb("hT", [128, 32, 512], BF16)
    oeT = lb.sb("oeT", [128, 8, 512], BF16)
    oe_tok = lb.sb("oe_tok", [128, 4, 1024], BF16)
    prev_chunk = []
    prev_tr2 = None
    prev_cpr = {}
    all_st = []
    for ct in range(10):
        gts = []
        for j in range(4):
            g = p.dma("sync", [(xe_tok[:, j, :],
                               mbatch[ct * 512 + j * 128: ct * 512 + (j + 1) * 128, :])],
                      deps=E["rs_tok"] + prev_chunk)
            gts.append(g)
        tr_toks = []
        for j in range(4):
            for dc in range(8):
                tr = p.op("tensor", (lambda _j, _dc: lambda e: e.transpose(
                    psT[:], xe_tok[:, _j, _dc * 128:(_dc + 1) * 128], identbf[:]))(j, dc),
                    deps=[gts[j], tok_id] + ([prev_tr2] if prev_tr2 else [])),
                tr = tr[0] if isinstance(tr, tuple) else tr
                ev = p.op("vector", (lambda _j, _dc: lambda e: e.tensor_copy(
                    xeT[:, _dc, _j * 128:(_j + 1) * 128], psT[:]))(j, dc), deps=[tr] + prev_chunk)
                prev_tr2 = ev
                tr_toks.append(ev)
        y_toks = []
        for oc in range(32):
            mms = []
            for kc in range(8):
                t = p.op("tensor", (lambda _kc, _oc: lambda e: e.matmul(
                    psY[:], cfc[:, _kc, _oc * 128:(_oc + 1) * 128], xeT[:, _kc, :],
                    start=(_kc == 0), stop=(_kc == 7)))(kc, oc),
                    deps=([ldw1] + tr_toks + y_toks[-1:]) if kc == 0 else [])
                mms.append(t)
            ya = p.op("scalar", (lambda _oc: lambda e: e.activation(
                hT[:, _oc, :], psY[:], AF.Gelu))(oc), deps=[mms[-1]] + prev_chunk)
            y_toks.append(ya)
        z_toks = []
        for zc in range(8):
            zi = zc % 2
            wl = [prev_cpr[zi]] if zi in prev_cpr else []
            ldw2 = p.dma("gpsimd", [(cprb[zi][:], cpr_d[:, zc * 128:(zc + 1) * 128]
                                     .rearrange("(c p) n -> p c n", p=128))], deps=wl)
            mms = []
            for kc in range(32):
                t = p.op("tensor", (lambda _kc, _zc, _zi: lambda e: e.matmul(
                    psZ[:], cprb[_zi][:, _kc, :], hT[:, _kc, :],
                    start=(_kc == 0), stop=(_kc == 31)))(kc, zc, zi),
                    deps=([ldw2] + y_toks + z_toks[-1:]) if kc == 0 else [])
                mms.append(t)
            ze = p.op("vector", (lambda _zc: lambda e: e.tensor_copy(
                oeT[:, _zc, :], psZ[:]))(zc), deps=[mms[-1]] + prev_chunk)
            prev_cpr[zi] = ze
            z_toks.append(ze)
        st_toks2 = []
        for j in range(4):
            evs = []
            for dc in range(8):
                tr = p.op("tensor", (lambda _j, _dc: lambda e: e.transpose(
                    psT[:], oeT[:, _dc, _j * 128:(_j + 1) * 128], identbf[:]))(j, dc),
                    deps=z_toks + [tok_id] + ([prev_tr2] if prev_tr2 else []))
                ev = p.op("vector", (lambda _j, _dc: lambda e: e.tensor_copy(
                    oe_tok[:, _j, _dc * 128:(_dc + 1) * 128], psT[:]))(j, dc),
                    deps=[tr] + prev_chunk)
                prev_tr2 = ev
                evs.append(ev)
            st = p.dma("sync", [(o_oe[ct * 512 + j * 128: ct * 512 + (j + 1) * 128, :],
                                 oe_tok[:, j, :])], deps=evs)
            st_toks2.append(st)
        prev_chunk = st_toks2
        all_st += st_toks2
    return all_st


def emit_l1cmb(lb, E):
    nc, p = lb.nc, lb.p
    identbf, tok_id = lb.const("identity_bf", np.eye(128, dtype=np.float32).astype(BF))
    ones_row, tok_or = lb.const("ones_row_f32", np.ones((1, 128), np.float32))
    cW, tok_cW = lb.const("c37_col", (WCH * np.arange(128, dtype=np.float32))[:, None])
    x, o_x2 = E["x"], E["x_out"]

    soe = lb.sb("soe", [128, WCH * 1024], BF16)
    aoutT = lb.sb("aoutT", [128, 8, 2048], BF16)
    Eb = lb.sb("Eb", [128, 2, 2048], F32)
    wb = lb.sb("wb", [128, 2, 2048], F32)
    psC = lb.ps("psCb", [128, 512], F32)

    zs = p.op("vector", lambda e: e.memset(soe[:], 0.0))
    gidx2 = lb.sb("gidx2", [128, 1], I32)
    g2c = p.op("vector", lambda e: e.tensor_single_scalar(
        gidx2[:], E["gidx"][:], 19, ALU.add), deps=E["cmb_ready"])
    gat1 = p.idma(lambda e: e.indirect_dma_start(
        out=soe[:, 0:19 * 1024], out_offset=None, in_=E["oe_all"],
        in_offset=IndirectOffsetOnAxis(E["gidx"][:], 0),
        bounds_check=8 * CAP, oob_is_err=False),
        deps=[zs] + E["ag_tok"] + E["cmb_ready"])
    gat = p.idma(lambda e: e.indirect_dma_start(
        out=soe[:, 19 * 1024:37 * 1024], out_offset=None, in_=E["oe_all"],
        in_offset=IndirectOffsetOnAxis(gidx2[:], 0),
        bounds_check=8 * CAP, oob_is_err=False),
        deps=[gat1, g2c])
    ld_ao = p.dma("sync", [(aoutT[:], E["rs_out"].rearrange("p (h t) -> p h t", h=8))],
                  deps=E["rs_tok"])

    bc_toks = []
    prev_bc = []
    for k in range(2):
        for ff in range(4):
            m1 = p.op("tensor", (lambda _k, _f: lambda e: e.matmul(
                psC[:], ones_row[:], E["slotpos"][0:1, _k, _f * 512:(_f + 1) * 512],
                start=True, stop=True))(k, ff), deps=E["cmb_ready"] + [tok_or] + prev_bc)
            e1 = p.op("vector", (lambda _k, _f: lambda e: e.tensor_copy(
                Eb[:, _k, _f * 512:(_f + 1) * 512], psC[:]))(k, ff), deps=[m1])
            m2 = p.op("tensor", (lambda _k, _f: lambda e: e.matmul(
                psC[:], ones_row[:], E["wrow"][0:1, _k, _f * 512:(_f + 1) * 512],
                start=True, stop=True))(k, ff), deps=[e1])
            e2 = p.op("vector", (lambda _k, _f: lambda e: e.tensor_copy(
                wb[:, _k, _f * 512:(_f + 1) * 512], psC[:]))(k, ff), deps=[m2])
            prev_bc = [e2]
            bc_toks += [e1, e2]
    cWb = [bass.AP(cW[:].tensor, 0, [cW[:].ap[0], [0, 2048]])]
    sub_toks = []
    for k in range(2):
        t = p.op("vector", (lambda _k: lambda e: e.tensor_sub(
            Eb[:, _k, :], Eb[:, _k, :], cWb[0]))(k), deps=bc_toks + [tok_cW])
        sub_toks.append(t)

    mk = lb.sb("mk", [128, 256], F32)
    t1m = lb.sb("t1m", [128, 256], F32)
    Pj = lb.sb("Pj", [128, 256], BF16)
    psMo = [lb.ps(f"psMo{i}", [128, 1024], F32) for i in range(2)]
    psbf = [lb.ps(f"psbf{i}", [128, 128], BF16) for i in range(2)]
    xtile = [lb.sb(f"rx{i}", [128, 1024], F32) for i in range(2)]
    otile = [lb.sb(f"ro{i}", [128, 1024], F32) for i in range(2)]
    ttmp = lb.sb("ttmp", [128, 128], F32)
    st_toks = []
    prev_st = {}
    prev_trr = {}
    prev_tt = None
    prev_grp_mm = []
    prev_pev = None
    for grp in range(8):
        gsl = slice(grp * 256, (grp + 1) * 256)
        mm_last = []
        for j in range(WCH):
            e0 = p.op("vector", (lambda _j, _g: lambda e: e.tensor_single_scalar(
                mk[:], Eb[:, 0, _g * 256:(_g + 1) * 256], float(_j), ALU.is_equal))(j, grp),
                deps=sub_toks + ([prev_pev] if prev_pev else []))
            p0_ = p.op("vector", (lambda _g: lambda e: e.tensor_mul(
                Pj[:], mk[:], wb[:, 0, _g * 256:(_g + 1) * 256]))(grp), deps=[e0])
            e1_ = p.op("vector", (lambda _j, _g: lambda e: e.tensor_single_scalar(
                mk[:], Eb[:, 1, _g * 256:(_g + 1) * 256], float(_j), ALU.is_equal))(j, grp),
                deps=[p0_])
            t1_ = p.op("vector", (lambda _g: lambda e: e.tensor_mul(
                t1m[:], mk[:], wb[:, 1, _g * 256:(_g + 1) * 256]))(grp), deps=[e1_])
            pv = p.op("vector", lambda e: e.tensor_add(Pj[:], Pj[:], t1m[:]), deps=[t1_])
            prev_pev = pv
            for t2 in range(2):
                for ob in range(2):
                    mmd = ([gat, pv] + prev_grp_mm) if j == 0 else [pv]
                    mm = p.op("tensor", (lambda _j, _t2, _ob: lambda e: e.matmul(
                        psMo[_t2][:, _ob * 512:(_ob + 1) * 512],
                        Pj[:, _t2 * 128:(_t2 + 1) * 128],
                        soe[:, _j * 1024 + _ob * 512:_j * 1024 + (_ob + 1) * 512],
                        start=(_j == 0), stop=(_j == WCH - 1)))(j, t2, ob),
                        deps=mmd)
                    if j == WCH - 1:
                        mm_last.append(mm)
        grp_evs = []
        for t2 in range(2):
            tb = grp * 2 + t2
            i = tb % 2
            dl = [prev_st[i]] if i in prev_st else []
            ld = p.dma("sync", [(xtile[i][:], x[tb * 128:(tb + 1) * 128, :])], deps=dl)
            adds = []
            for dc in range(8):
                pb = psbf[dc % 2]
                war = [prev_trr[dc % 2]] if (dc % 2) in prev_trr else []
                tr = p.op("tensor", (lambda _dc, _tb, _pb: lambda e: e.transpose(
                    _pb[:], aoutT[:, _dc, _tb * 128:(_tb + 1) * 128], identbf[:]))(dc, tb, pb),
                    deps=[ld_ao, tok_id] + war + mm_last)
                cp = p.op("vector", (lambda _pb: lambda e: e.tensor_copy(ttmp[:], _pb[:]))(pb),
                          deps=[tr] + ([prev_tt] if prev_tt is not None else []))
                ad = p.op("vector", (lambda _dc, _i: lambda e: e.tensor_add(
                    otile[_i][:, _dc * 128:(_dc + 1) * 128],
                    xtile[_i][:, _dc * 128:(_dc + 1) * 128], ttmp[:]))(dc, i),
                    deps=[cp, ld] + dl)
                prev_trr[dc % 2] = cp
                prev_tt = ad
                adds.append(ad)
            am = p.op("vector", (lambda _i, _t2: lambda e: e.tensor_add(
                otile[_i][:], otile[_i][:], psMo[_t2][:]))(i, t2),
                deps=adds + mm_last)
            st = p.dma("sync", [(o_x2[tb * 128:(tb + 1) * 128, :], otile[i][:])],
                       deps=[am])
            prev_st[i] = st
            st_toks.append(st)
            grp_evs.append(am)
        prev_grp_mm = grp_evs
    return st_toks

def fft_dev_consts():
    c = {}
    b_, q_ = np.meshgrid(np.arange(64), np.arange(64), indexing="ij")
    ang = 2 * np.pi * b_ * q_ / 64.0
    w64dup = np.concatenate([np.cos(ang), -np.sin(ang)], 1)
    c["w64dup"] = np.concatenate([w64dup, w64dup], 0).astype(BF)
    al = np.arange(128)[:, None]
    r_ = np.arange(128)[None, :]
    F = np.zeros((64, 3, 128, 128), np.float32)
    for q in range(64):
        fq = np.exp(-2j * np.pi * al * q / 8192.0) * np.exp(-2j * np.pi * al * r_ / 128.0)
        F[q, 0] = fq.real
        F[q, 1] = fq.imag
        F[q, 2] = -fq.imag
    c["Ftab"] = np.ascontiguousarray(F.transpose(2, 0, 1, 3)).astype(BF)
    G = np.zeros((64, 3, 128, 128), np.float32)
    W128i = np.exp(2j * np.pi * al * r_ / 128.0)
    for q in range(64):
        T = np.exp(-2j * np.pi * 4095.0 * (q + 64 * np.arange(128)) / 8192.0)
        gq = (W128i * np.exp(2j * np.pi * al * q / 8192.0)).T * T[:, None]
        G[q, 0] = gq.real
        G[q, 1] = gq.imag
        G[q, 2] = -gq.imag
    c["Gtab"] = np.ascontiguousarray(G.transpose(2, 0, 1, 3)).astype(BF)
    q_b = np.exp(2j * np.pi * np.outer(np.arange(64), np.arange(32)) / 64.0) / 8192.0
    wi = np.concatenate([q_b.real.astype(np.float32), -q_b.imag.astype(np.float32)], 1)
    c["w64idup"] = np.concatenate([wi, wi], 0).astype(BF)
    L = 4096
    t = (np.arange(L) / (L - 1)).astype(np.float32)
    c["trows"] = np.broadcast_to(t[None, :], (64, L)).copy().astype(BF)
    dmax, dmin = (math.log(0.01) / 0.3, math.log(0.01) / 1.5)
    deltas = np.abs(np.linspace(dmax, dmin, 1024)).astype(np.float32)
    c["ndel_full"] = -deltas
    ang2 = 2.0 * math.pi * np.arange(L, dtype=np.float64) / L
    ph = 0.0001 * ang2
    c["peT"] = np.stack([t.astype(np.float64), np.cos(ph), -np.sin(ph)], 0).astype(np.float32)
    return c


def emit_l2a(lb, E):
    nc, p = lb.nc, lb.p
    wina_d, winb_d = E["win_a"], E["win_b"]
    bia_d, bib_d = E["bia"], E["bib"]
    shw0_d, shb0_d = E["shw_0"], E["shb_0"]
    shw1_d, shb1_d = E["shw_1"], E["shb_1"]
    shwb_d, shbb_d = E["shw_b"], E["shb_b"]
    W1_d, b1f_d, W2_d = E["W1"], E["b1f"], E["W2"]
    W3r_d, b3c_d, ndel_d = E["W3r"], E["b3c"], E["ndel"]
    fb_d = E["fbrow"]

    with lb.scoped():
        fc = fft_dev_consts()
        identbf, tok_id = lb.const("identity_bf", np.eye(128, dtype=np.float32).astype(BF))
        w64dup, tok_w64 = lb.const("w64dup", fc["w64dup"])
        w64idup, tok_w64i = lb.const("w64idup", fc["w64idup"])
        trows, tok_tr = lb.const("trows", fc["trows"])
        peT_d = lb.dram_in("c_peT", [3, 4096], F32)
        lb.const_arrays["c_peT"] = fc["peT"]
        negpi, tok_np = lb.const("negpi_col", np.full((128, 1), -math.pi, np.float32))
        ones_row, tok_or = lb.const("ones_row_f32", np.ones((1, 128), np.float32))
        Ftab_d = lb.dram_in("c_Ftab", [128, 64, 3, 128], BF16)
        lb.const_arrays["c_Ftab"] = fc["Ftab"]
        Gtab_d = lb.dram_in("c_Gtab", [128, 64, 3, 128], BF16)
        lb.const_arrays["c_Gtab"] = fc["Gtab"]

        xnT, xn_toks = E["xnT"], E["xn_toks"]
        ystg = [lb.sb("ystg%d" % i, [64, 128], BF16) for i in range(2)]
        smalls = {}
        tr_list = [("bia", bia_d, [128, 16]), ("bib", bib_d, [64, 16]),
                   ("shw0", shw0_d, [64, 16, 3]), ("shb0", shb0_d, [64, 16]),
                   ("shw1", shw1_d, [64, 16, 3]), ("shb1", shb1_d, [64, 16]),
                   ("shwb", shwb_d, [64, 16, 3]), ("shbb", shbb_d, [64, 16]),
                   ("W1", W1_d, [3, 64]), ("b1f", b1f_d, [64, 4]),
                   ("W2", W2_d, [64, 64]), ("W3r", W3r_d, [64, 8, 64]),
                   ("b3c", b3c_d, [64, 8]), ("ndel", ndel_d, [64, 8]),
                   ("fbrow", fb_d, [1, 512])]
        for nm, d, shp in tr_list:
            smalls[nm] = lb.sb("sm_" + nm, shp, F32)
        ld_small = p.dma("sync", [(smalls[nm][:], d) for nm, d, shp in tr_list])
        wina = lb.sb("wina", [128, 8, 128], BF16)
        winb = lb.sb("winb", [128, 8, 64], BF16)

        psA = lb.ps("psA", [128, 512], F32)
        psB = lb.ps("psB", [64, 512], F32)
        psT = lb.ps("psT", [128, 128], BF16)
        psS = lb.ps("psS", [128, 128], F32)
        psXr = lb.ps("psXr", [128, 64], F32)
        psXi = lb.ps("psXi", [128, 64], F32)
        psI2 = lb.ps("psI2", [128, 32], F32)

        s2_dram = lb.dram_scratch("s2_dram", [64, 4096], F32)
        _mlp_scope = lb.scoped()
        _mlp_scope.__enter__()
        s1 = lb.sb("s1", [64, 4096], F32)
        srf = lb.sb("srf", [64, 512], F32)
        srfI = lb.sb("srfI", [64, 512], I32)
        srfF = lb.sb("srfF", [64, 512], F32)
        srfG = lb.sb("srfG", [64, 512], F32)
        peTb = [lb.sb(f"peTb{i}", [3, 512], F32) for i in range(2)]
        b1f = smalls["b1f"]
        mlp_toks = []
        prev_ev = []
        prev_pe = {}
        for tc in range(8):
            pi = tc % 2
            pl = [prev_pe[pi]] if pi in prev_pe else []
            ld_pe = p.dma("sync", [(peTb[pi][:], peT_d[:, tc * 512:(tc + 1) * 512])], deps=pl)
            m = p.op("tensor", (lambda _tc, _pi: lambda e: e.matmul(
                psB[:], smalls["W1"][:], peTb[_pi][:],
                start=True, stop=True))(tc, pi), deps=[ld_small, ld_pe] + prev_ev + xn_toks[-1:])
            prev_pe[pi] = m
            t1 = p.op("vector", lambda e: e.tensor_scalar(
                srf[:], psB[:], b1f[:, 0:1], b1f[:, 1:2], ALU.mult, ALU.add), deps=[m] + prev_ev)
            c1 = p.op("vector", lambda e: e.tensor_copy(srfI[:], srf[:]), deps=[t1])
            c2 = p.op("vector", lambda e: e.tensor_copy(srfF[:], srfI[:]), deps=[c1])
            c3 = p.op("vector", lambda e: e.tensor_tensor(srfG[:], srfF[:], srf[:], ALU.is_gt),
                      deps=[c2])
            c4 = p.op("vector", lambda e: e.tensor_sub(srf[:], srf[:], srfF[:]), deps=[c3])
            t2 = p.op("vector", lambda e: e.tensor_add(srf[:], srf[:], srfG[:]), deps=[c4])
            t3 = p.op("scalar", (lambda _tc: lambda e: e.activation(
                s1[:, _tc * 512:(_tc + 1) * 512], srf[:], AF.Sin, bias=negpi[0:64, 0:1],
                scale=2.0 * math.pi))(tc), deps=[t2, tok_np])
            prev_ev = [t3]
            mlp_toks.append(t3)
        prev_ev2 = [mlp_toks[-1]]
        mlp2_toks = []
        for tc in range(8):
            m = p.op("tensor", (lambda _tc: lambda e: e.matmul(
                psB[:], smalls["W2"][:], s1[:, _tc * 512:(_tc + 1) * 512],
                start=True, stop=True))(tc), deps=[ld_small] + mlp_toks + prev_ev2)
            t1 = p.op("vector", lambda e: e.tensor_scalar(
                srf[:], psB[:], b1f[:, 2:3], b1f[:, 3:4], ALU.mult, ALU.add), deps=[m] + prev_ev2)
            c1 = p.op("vector", lambda e: e.tensor_copy(srfI[:], srf[:]), deps=[t1])
            c2 = p.op("vector", lambda e: e.tensor_copy(srfF[:], srfI[:]), deps=[c1])
            c3 = p.op("vector", lambda e: e.tensor_tensor(srfG[:], srfF[:], srf[:], ALU.is_gt),
                      deps=[c2])
            c4 = p.op("vector", lambda e: e.tensor_sub(srf[:], srf[:], srfF[:]), deps=[c3])
            t2 = p.op("vector", lambda e: e.tensor_add(srf[:], srf[:], srfG[:]), deps=[c4])
            t3 = p.op("scalar", (lambda _tc: lambda e: e.activation(
                srf[:], srf[:], AF.Sin, bias=negpi[0:64, 0:1],
                scale=2.0 * math.pi))(tc), deps=[t2, tok_np])
            t4 = p.dma("sync", [(s2_dram[:, tc * 512:(tc + 1) * 512], srf[:])], deps=[t3])
            prev_ev2 = [t4]
            mlp2_toks.append(t4)
        _mlp_scope.__exit__(None, None, None)
        s2b = [lb.sb(f"s2b{i}", [64, 512], F32) for i in range(2)]

        fbb = lb.sb("fbb", [128, 512], F32)
        b1_ = p.op("tensor", lambda e: e.matmul(psA[:], ones_row[:], smalls["fbrow"][0:1, :],
                                                start=True, stop=True),
                   deps=[ld_small, tok_or] + xn_toks[-1:] + mlp2_toks[-1:])
        b2_ = p.op("vector", lambda e: e.tensor_copy(fbb[:], psA[:]), deps=[b1_])
        fb_toks = [b2_]

        x0u = lb.sb("x0u", [64, 4096], BF16)
        x1u = lb.sb("x1u", [64, 4096], BF16)
        vu = lb.sb("vu", [64, 4096], BF16)
        upad = lb.sb("upad", [64, 4098], BF16)
        vtmf = lb.sb("vtmf", [128, 8192], BF16)
        vtm3 = vtmf[:, 0:4096].rearrange("p (a b c) -> p a b c", a=32, b=2)
        B2f = lb.sb("B2f", [128, 4096], BF16)
        B2 = B2f[:].rearrange("p (a b) -> p a b", a=32)
        Pw = B2f[:].rearrange("p (a b) -> p a b", a=64)
        HAf = lb.sb("HAf", [128, 8192], BF16)
        Hview = HAf[:].rearrange("p (c q) -> p c q", c=64)
        Ablkv = HAf[:].rearrange("p (pr ri hf q) -> p pr ri hf q", pr=32, ri=2, hf=2)
        XVr = lb.sb("XVr", [128, 64, 64], BF16)
        XVi = lb.sb("XVi", [128, 64, 64], BF16)
        XKr = lb.sb("XKr", [128, 64, 64], BF16)
        XKi = lb.sb("XKi", [128, 64, 64], BF16)
        Ft = [lb.sb(f"Ft{i}", [128, 3, 128], BF16) for i in range(2)]
        Gt = [lb.sb(f"Gt{i}", [128, 3, 128], BF16) for i in range(2)]
        ycv = lb.sb("ycv", [128, 32, 64], BF16)
        x0tm = lb.sb("x0tm", [128, 32, 64], BF16)

        zt = p.op("vector", lambda e: e.memset(vtmf[:], 0.0), deps=mlp2_toks[-1:])


        def fwd_fft(src, src_toks, XR, XI, guards):
            toks = list(guards)
            t1s = []
            prev_t1 = None
            for J in range(32):
                war = [prev_t1] if prev_t1 else []
                tr = p.op("tensor", (lambda _J: lambda e: e.transpose(
                    psT[0:128, 0:64], src[:, _J * 128:(_J + 1) * 128], identbf[0:64, 0:64]))(J),
                    deps=src_toks + [tok_id] + war + toks)
                ev = p.op("vector", (lambda _J: lambda e: e.tensor_copy(
                    vtm3[:, :, :, _J], psT[0:128, 0:64]))(J), deps=[tr] + guards)
                prev_t1 = ev
                t1s.append(ev)
            t2s = []
            for pr in range(32):
                inap = vtm3[:, pr, :, :].rearrange("p a b -> p (a b)")
                tr = p.op("tensor", (lambda _in: lambda e: e.transpose(
                    psT[:], _in, identbf[:]))(inap), deps=t1s + [tok_id] + t2s[-1:])
                ev = p.op("vector", (lambda _pr: lambda e: e.tensor_copy(
                    B2[:, _pr, :], psT[:]))(pr), deps=[tr])
                t2s.append(ev)
            s1s = []
            for ch in range(64):
                pr, hf = ch // 2, ch % 2
                mm = p.op("tensor", (lambda _pr, _hf: lambda e: e.matmul(
                    psS[:], B2[_hf * 64:(_hf + 1) * 64, _pr, :],
                    w64dup[_hf * 64:(_hf + 1) * 64, :], start=True, stop=True))(pr, hf),
                    deps=t2s + [tok_w64] + s1s[-1:])
                ev = p.op("vector", (lambda _ch: lambda e: e.tensor_copy(
                    Hview[:, _ch, :], psS[:]))(ch), deps=[mm])
                s1s.append(ev)
            prev_ld = {}
            prev_x = []
            for q in range(64):
                i = q % 2
                wl = [prev_ld[i]] if i in prev_ld else []
                ldf = p.dma("sync", [(Ft[i][:], Ftab_d[:, q, :, :])],
                            deps=wl + (list(guards) if q < 2 else []))
                Hr = Hview[:, :, q]
                Hi = Hview[:, :, 64 + q]
                mm1 = p.op("tensor", (lambda _i, _Hr: lambda e: e.matmul(
                    psXr[:], Ft[_i][:, 0, :], _Hr, start=True, stop=False))(i, Hr),
                    deps=[ldf] + s1s + prev_x)
                mm2 = p.op("tensor", (lambda _i, _Hi: lambda e: e.matmul(
                    psXr[:], Ft[_i][:, 2, :], _Hi, start=False, stop=True))(i, Hi), deps=[])
                mm3 = p.op("tensor", (lambda _i, _Hi: lambda e: e.matmul(
                    psXi[:], Ft[_i][:, 0, :], _Hi, start=True, stop=False))(i, Hi), deps=[])
                mm4 = p.op("tensor", (lambda _i, _Hr: lambda e: e.matmul(
                    psXi[:], Ft[_i][:, 1, :], _Hr, start=False, stop=True))(i, Hr), deps=[])
                e1 = p.op("vector", (lambda _q: lambda e: e.tensor_copy(XR[:, _q, :], psXr[:]))(q),
                          deps=[mm2])
                e2 = p.op("vector", (lambda _q: lambda e: e.tensor_copy(XI[:, _q, :], psXi[:]))(q),
                          deps=[mm4])
                prev_ld[i] = e2
                prev_x = [e1, e2]
            return prev_x

        st_toks = []
        prev_chg = [zt] + fb_toks
        for chg in range(8):
            ld_win = p.dma("gpsimd", [
                (wina[:], wina_d[:, chg, :].rearrange("(c p) n -> p c n", p=128)),
                (winb[:], winb_d[:, chg, :].rearrange("(c p) n -> p c n", p=128))],
                deps=prev_chg)
            u_toks = []
            for tcc in range(8):
                mmsA = []
                for kc in range(8):
                    tA = p.op("tensor", (lambda _kc, _tcc, _chg: lambda e: e.matmul(
                        psA[:], wina[:, _kc, :], xnT[:, _kc, _tcc * 512:(_tcc + 1) * 512],
                        start=(_kc == 0), stop=(_kc == 7)))(kc, tcc, chg),
                        deps=([ld_win] + xn_toks + prev_chg + u_toks[-1:]) if kc == 0 else [])
                    mmsA.append(tA)
                mmsB = []
                for kc in range(8):
                    tB = p.op("tensor", (lambda _kc, _tcc, _chg: lambda e: e.matmul(
                        psB[:], winb[:, _kc, :], xnT[:, _kc, _tcc * 512:(_tcc + 1) * 512],
                        start=(_kc == 0), stop=(_kc == 7)))(kc, tcc, chg),
                        deps=[mmsA[-1]] + u_toks[-1:])
                    mmsB.append(tB)
                eA0 = p.op("vector", (lambda _tcc, _chg: lambda e: e.tensor_scalar_add(
                    x0u[:, _tcc * 512:(_tcc + 1) * 512], psA[0:64, :],
                    smalls["bia"][0:64, _chg:_chg + 1]))(tcc, chg),
                    deps=[mmsA[-1], ld_small] + prev_chg)
                eA1 = p.op("vector", (lambda _tcc, _chg: lambda e: e.tensor_scalar_add(
                    x1u[:, _tcc * 512:(_tcc + 1) * 512], psA[64:128, :],
                    smalls["bia"][64:128, _chg:_chg + 1]))(tcc, chg), deps=[mmsA[-1]] + prev_chg)
                eB = p.op("vector", (lambda _tcc, _chg: lambda e: e.tensor_scalar_add(
                    vu[:, _tcc * 512:(_tcc + 1) * 512], psB[:, :],
                    smalls["bib"][:, _chg:_chg + 1]))(tcc, chg), deps=[mmsB[-1]] + prev_chg)
                u_toks += [eA0, eA1, eB]
            sc_toks = []
            for (buf, wnm, bnm, r0) in ((x0u, "shw0", "shb0", 0), (x1u, "shw1", "shb1", 0),
                                        (vu, "shwb", "shbb", 0)):
                rs = slice(0, 64)
                z1 = p.op("vector", lambda e: e.memset(upad[:, 0:2], 0.0),
                          deps=u_toks + sc_toks[-1:])
                z2 = p.op("vector", (lambda _b: lambda e: e.tensor_copy(upad[:, 2:4098], _b[:]))(buf),
                          deps=[z1] + u_toks)
                wsl = smalls[wnm]
                bsl = smalls[bnm]
                c1 = p.op("vector", (lambda _b, _w, _chg, _r: lambda e: e.tensor_scalar_mul(
                    _b[:], upad[:, 0:4096], _w[_r, _chg, 0:1]))(buf, wsl, chg, rs), deps=[z2])
                c2 = p.op("vector", (lambda _b, _w, _chg, _r: lambda e: e.scalar_tensor_tensor(
                    _b[:], upad[:, 1:4097], _w[_r, _chg, 1:2], _b[:], ALU.mult, ALU.add))(
                    buf, wsl, chg, rs), deps=[c1])
                c3 = p.op("vector", (lambda _b, _w, _chg, _r: lambda e: e.scalar_tensor_tensor(
                    _b[:], upad[:, 2:4098], _w[_r, _chg, 2:3], _b[:], ALU.mult, ALU.add))(
                    buf, wsl, chg, rs), deps=[c2])
                c4 = p.op("vector", (lambda _b, _bs, _chg, _r: lambda e: e.tensor_scalar_add(
                    _b[:], _b[:], _bs[_r, _chg:_chg + 1]))(buf, bsl, chg, rs), deps=[c3])
                sc_toks.append(c4)
            g1 = p.op("vector", lambda e: e.tensor_mul(vu[:], vu[:], x1u[:]), deps=sc_toks)
            d1 = p.op("scalar", (lambda _chg: lambda e: e.activation(
                upad[:, 0:4096], trows[:], AF.Exp, scale=smalls["ndel"][:, _chg:_chg + 1]))(chg),
                deps=[tok_tr, ld_small] + prev_chg + sc_toks + [g1])
            k_toks = []
            prev_s2 = {}
            for tcc in range(8):
                s2i = tcc % 2
                sl = [prev_s2[s2i]] if s2i in prev_s2 else []
                ld_s2 = p.dma("sync", [(s2b[s2i][:], s2_dram[:, tcc * 512:(tcc + 1) * 512])],
                              deps=sl + mlp2_toks)
                m = p.op("tensor", (lambda _tcc, _chg, _s2i: lambda e: e.matmul(
                    psB[:], smalls["W3r"][:, _chg, :], s2b[_s2i][:],
                    start=True, stop=True))(tcc, chg, s2i),
                    deps=[ld_s2, ld_small] + k_toks[-1:] + u_toks + [g1])
                prev_s2[s2i] = m
                ev = p.op("vector", (lambda _tcc, _chg: lambda e: e.tensor_scalar(
                    x1u[:, _tcc * 512:(_tcc + 1) * 512], psB[:], -1.0,
                    smalls["b3c"][:, _chg:_chg + 1], ALU.mult, ALU.add))(tcc, chg), deps=[m])
                k_toks.append(ev)
            k2 = p.op("vector", lambda e: e.tensor_mul(x1u[:], x1u[:], upad[:, 0:4096]),
                      deps=k_toks + [d1])
            z0 = p.op("vector", lambda e: e.memset(vtmf[:], 0.0), deps=prev_chg + [g1] + k_toks + [k2])
            xv_done = fwd_fft(vu, [g1, z0], XVr, XVi, prev_chg)
            xk_done = fwd_fft(x1u, [k2], XKr, XKi, xv_done)
            pw1 = p.op("vector", lambda e: e.tensor_mul(Pw[:], XVr[:], XKr[:]),
                       deps=xv_done + xk_done)
            pw2 = p.op("vector", lambda e: e.tensor_mul(XVr[:], XVr[:], XKi[:]), deps=[pw1])
            pw3 = p.op("vector", lambda e: e.tensor_mul(XKi[:], XVi[:], XKi[:]), deps=[pw2])
            pw4 = p.op("vector", lambda e: e.tensor_mul(XVi[:], XVi[:], XKr[:]), deps=[pw3])
            pw5 = p.op("vector", lambda e: e.tensor_add(Pw[:], Pw[:], XKi[:]), deps=[pw4])
            pw6 = p.op("vector", lambda e: e.tensor_sub(XVi[:], XVi[:], XVr[:]), deps=[pw5])
            pw_done = [pw5, pw6]
            prev_gld = {}
            prev_a = []
            for q in range(64):
                i = q % 2
                wl = [prev_gld[i]] if i in prev_gld else []
                ldg = p.dma("sync", [(Gt[i][:], Gtab_d[:, q, :, :])],
                            deps=wl + (pw_done if q < 2 else []))
                Pr = Pw[:, q, :]
                Pi = XVi[:, q, :]
                mm1 = p.op("tensor", (lambda _i, _P: lambda e: e.matmul(
                    psXr[:], Gt[_i][:, 0, :], _P, start=True, stop=False))(i, Pr),
                    deps=[ldg] + pw_done + prev_a)
                mm2 = p.op("tensor", (lambda _i, _P: lambda e: e.matmul(
                    psXr[:], Gt[_i][:, 2, :], _P, start=False, stop=True))(i, Pi), deps=[])
                mm3 = p.op("tensor", (lambda _i, _P: lambda e: e.matmul(
                    psXi[:], Gt[_i][:, 0, :], _P, start=True, stop=False))(i, Pi), deps=[])
                mm4 = p.op("tensor", (lambda _i, _P: lambda e: e.matmul(
                    psXi[:], Gt[_i][:, 1, :], _P, start=False, stop=True))(i, Pr), deps=[])
                e1 = p.op("vector", (lambda _q: lambda e: e.tensor_copy(
                    Ablkv[:, :, 0, :, _q], psXr[:].rearrange("p (a b) -> p a b", a=32)))(q),
                    deps=[mm2])
                e2 = p.op("vector", (lambda _q: lambda e: e.tensor_copy(
                    Ablkv[:, :, 1, :, _q], psXi[:].rearrange("p (a b) -> p a b", a=32)))(q),
                    deps=[mm4])
                prev_gld[i] = e2
                prev_a = [e1, e2]
            atr_toks = []
            prev_at = None
            for pr in range(32):
                for ri in range(2):
                    inap = Ablkv[:, pr, ri, :, :].rearrange("p a b -> p (a b)")
                    war = [prev_at] if prev_at else []
                    tr = p.op("tensor", (lambda _in: lambda e: e.transpose(
                        psT[:], _in, identbf[:]))(inap), deps=prev_a + [tok_id] + war)
                    ev = p.op("vector", (lambda _pr, _ri: lambda e: e.tensor_copy(
                        vtm3[:, _pr, _ri, :].rearrange("p (a b) -> p a b", a=2) if False else
                        vtmf[:, (2 * _pr + _ri) * 128:(2 * _pr + _ri + 1) * 128], psT[:]))(pr, ri),
                        deps=[tr])
                    prev_at = ev
                    atr_toks.append(ev)
            i2_toks = []
            for ch in range(64):
                pr, hf = ch // 2, ch % 2
                mm1 = p.op("tensor", (lambda _pr, _hf: lambda e: e.matmul(
                    psI2[:], vtmf[_hf * 64:(_hf + 1) * 64, (2 * _pr) * 128:(2 * _pr + 1) * 128],
                    w64idup[_hf * 64:(_hf + 1) * 64, 0:32], start=True, stop=False))(pr, hf),
                    deps=atr_toks + [tok_w64i] + i2_toks[-1:])
                mm2 = p.op("tensor", (lambda _pr, _hf: lambda e: e.matmul(
                    psI2[:], vtmf[_hf * 64:(_hf + 1) * 64, (2 * _pr + 1) * 128:(2 * _pr + 2) * 128],
                    w64idup[_hf * 64:(_hf + 1) * 64, 32:64], start=False, stop=True))(pr, hf),
                    deps=[])
                ev = p.op("vector", (lambda _ch, _chg: lambda e: e.tensor_scalar_add(
                    ycv[:, :, _ch], psI2[:], fbb[:, _chg * 64 + _ch:_chg * 64 + _ch + 1]))(ch, chg),
                    deps=[mm2] + fb_toks)
                i2_toks.append(ev)
            x0_toks = []
            prev_x0 = None
            for J in range(32):
                war = [prev_x0] if prev_x0 else []
                tr = p.op("tensor", (lambda _J: lambda e: e.transpose(
                    psT[0:128, 0:64], x0u[:, _J * 128:(_J + 1) * 128], identbf[0:64, 0:64]))(J),
                    deps=sc_toks + [tok_id] + war + i2_toks[-1:])
                ev = p.op("vector", (lambda _J: lambda e: e.tensor_copy(
                    x0tm[:, _J, :], psT[0:128, 0:64]))(J), deps=[tr] + st_toks[-1:])
                prev_x0 = ev
                x0_toks.append(ev)
            if globals().get("L2_TAP7", False) and chg == 7:
                tapy = lb.dram_out("L2_ycv7", [128, 32, 64], BF16)
                tapx = lb.dram_out("L2_x0tm7", [128, 32, 64], BF16)
                tp = p.dma("sync", [(tapy, ycv[:]), (tapx, x0tm[:])],
                           deps=x0_toks + i2_toks)
                x0_toks = x0_toks + [tp]
            gt = p.op("vector", lambda e: e.tensor_mul(ycv[:], ycv[:], x0tm[:]),
                      deps=x0_toks + i2_toks)
            yt_last = None
            prev_sty = {}
            for beta in range(32):
                bi = beta % 2
                war = [yt_last] if yt_last else []
                trY = p.op("tensor", (lambda _b: lambda e: e.transpose(
                    psT[0:64, 0:128], ycv[:, _b, :], identbf[:]))(beta),
                    deps=[gt, tok_id] + war + st_toks[-1:])
                evY = p.op("vector", (lambda _bi: lambda e: e.tensor_copy(
                    ystg[_bi][:], psT[0:64, 0:128]))(bi),
                    deps=[trY] + ([prev_sty[bi]] if bi in prev_sty else []))
                yt_last = evY
                sty = p.dma("sync", [(E["yt_dram"][chg * 64:(chg + 1) * 64,
                                                   beta * 128:(beta + 1) * 128],
                                      ystg[bi][:])], deps=[evY])
                prev_sty[bi] = sty
                st_toks.append(sty)
            prev_chg = [st_toks[-1]]

    wout_sb = lb.sb("wout_sb", [128, 4, 1024], BF16)
    ld_wo2 = p.dma("gpsimd", [(wout_sb[:], E["wout_own"].rearrange(
        "(g p) n -> p g n", p=128))], deps=st_toks)
    psW2 = lb.ps("psW2", [128, 512], F32)
    aev = [lb.sb(f"l2aev{i}", [128, 1024], BF16) for i in range(2)]
    ytb = [lb.sb(f"l2ytb{i}", [128, 4, 128], BF16) for i in range(2)]
    ytv = E["yt_dram"].rearrange("(g p) t -> p g t", p=128)
    rs_in = E["rs_in"]
    wo_toks = []
    prev_ev2 = {}
    prev_ldy = {}
    prev_mm2 = list(st_toks)
    for tb in range(32):
        i = tb % 2
        dly = [prev_ldy[i]] if i in prev_ldy else []
        ldy = p.dma("sync", [(ytb[i][:], ytv[:, :, tb * 128:(tb + 1) * 128])],
                    deps=dly + st_toks)
        evd = [prev_ev2[i]] if i in prev_ev2 else []
        last_mm = None
        for ob in range(2):
            mms = []
            for kc in range(4):
                t = p.op("tensor", (lambda _kc, _i, _ob: lambda e: e.matmul(
                    psW2[:], ytb[_i][:, _kc, :],
                    wout_sb[:, _kc, _ob * 512:(_ob + 1) * 512],
                    start=(_kc == 0), stop=(_kc == 3)))(kc, i, ob),
                    deps=([ld_wo2, ldy] + prev_mm2) if kc == 0 else [])
                mms.append(t)
            ev = p.op("vector", (lambda _i, _ob: lambda e: e.tensor_copy(
                aev[_i][:, _ob * 512:(_ob + 1) * 512], psW2[:]))(i, ob),
                deps=[mms[-1]] + evd)
            prev_mm2 = [ev]
            evd = [ev]
            last_mm = ev
        prev_ldy[i] = last_mm
        st = p.dma("sync", [(rs_in[tb * 128:(tb + 1) * 128, :], aev[i][:])],
                   deps=evd)
        prev_ev2[i] = st
        wo_toks.append(st)
    return wo_toks

def emit_l2b(lb, E):
    nc, p = lb.nc, lb.p
    x = E["x"]
    wf_d, a1_d, a2_d = E["wf"], E["a1"], E["a2"]
    bf2_d, boutr_d = E["bf2"], E["bout_row"]
    o_x3 = E["x_out"]
    xnT = E["xnT"]

    identbf, tok_id = lb.const("identity_bf", np.eye(128, dtype=np.float32).astype(BF))
    ones_row, tok_or = lb.const("ones_row_f32", np.ones((1, 128), np.float32))
    hA = lb.sb("hA", [128, 8, 2048], BF16)
    hB = lb.sb("hB", [128, 8, 2048], BF16)
    bsb = lb.sb("bsb", [128, 2, 8], F32)
    brow = lb.sb("brow", [1, 1024], F32)
    boutb = lb.sb("boutb", [128, 1024], F32)
    wt = [lb.sb(f"wt{i}", [128, 8, 128], BF16) for i in range(2)]
    psW = lb.ps("psW", [128, 512], F32)
    psT = lb.ps("psT", [128, 128], BF16)
    psbf = [lb.ps(f"psbf{i}", [128, 128], BF16) for i in range(2)]

    ld_xn = p.op("vector", lambda e: e.engine_nop(), deps=E["xn_toks"])
    ld_b = p.dma("sync", [(bsb[:, 1, :], bf2_d), (brow[:], boutr_d)])
    bb1 = p.op("tensor", lambda e: e.matmul(psW[:], ones_row[:], brow[0:1, 0:512],
                                            start=True, stop=True), deps=[ld_b, tok_or])
    bb2 = p.op("vector", lambda e: e.tensor_copy(boutb[:, 0:512], psW[:]), deps=[bb1])
    bb3 = p.op("tensor", lambda e: e.matmul(psW[:], ones_row[:], brow[0:1, 512:1024],
                                            start=True, stop=True), deps=[bb2])
    bb4 = p.op("vector", lambda e: e.tensor_copy(boutb[:, 512:1024], psW[:]), deps=[bb3])
    bias_toks = [bb4]

    def matmul_fm(w_dram, src, src_toks, dst, act=None, bias_idx=None, guards=()):
        toks = []
        prev_w = {}
        prev_ev = list(guards)
        for oc in range(8):
            i = oc % 2
            wl = [prev_w[i]] if i in prev_w else []
            ldw = p.dma("gpsimd", [(wt[i][:], w_dram[:, oc * 128:(oc + 1) * 128]
                                    .rearrange("(c p) n -> p c n", p=128))], deps=wl + list(guards))
            for tc in range(4):
                mms = []
                for kc in range(8):
                    t = p.op("tensor", (lambda _kc, _tc, _i, _s: lambda e: e.matmul(
                        psW[:], wt[_i][:, _kc, :], _s[:, _kc, _tc * 512:(_tc + 1) * 512],
                        start=(_kc == 0), stop=(_kc == 7)))(kc, tc, i, src),
                        deps=([ldw] + src_toks + prev_ev) if kc == 0 else [])
                    mms.append(t)
                if act == "tanh":
                    ev = p.op("scalar", (lambda _oc, _tc, _d: lambda e: e.activation(
                        _d[:, _oc, _tc * 512:(_tc + 1) * 512], psW[:], AF.Tanh,
                        bias=bsb[:, 1, _oc:_oc + 1]))(oc, tc, dst), deps=[mms[-1], ld_b])
                elif bias_idx is not None:
                    ev = p.op("vector", (lambda _oc, _tc, _d, _bi: lambda e: e.tensor_scalar_add(
                        _d[:, _oc, _tc * 512:(_tc + 1) * 512], psW[:],
                        bsb[:, _bi, _oc:_oc + 1]))(oc, tc, dst, bias_idx), deps=[mms[-1], ld_b])
                else:
                    ev = p.op("vector", (lambda _oc, _tc, _d: lambda e: e.tensor_copy(
                        _d[:, _oc, _tc * 512:(_tc + 1) * 512], psW[:]))(oc, tc, dst),
                        deps=[mms[-1]])
                prev_ev = [ev]
                toks.append(ev)
            prev_w[i] = prev_ev[0]
        return toks

    L2B_PART = globals().get("L2B_PART", 4)
    if L2B_PART >= 1:
        h1_toks = matmul_fm(wf_d, xnT, [ld_xn], hA, guards=bias_toks)
    else:
        h1_toks = [p.op("vector", lambda e: e.memset(hA[:], 0.0), deps=bias_toks)]
    if L2B_PART >= 2:
        h2_toks = matmul_fm(a1_d, hA, h1_toks, hB, guards=h1_toks[-1:])
    else:
        h2_toks = h1_toks
    if L2B_PART >= 4:
        h3_toks = matmul_fm(a2_d, hB, h2_toks, hA, act="tanh", guards=h2_toks)
    elif L2B_PART == 3:
        h3_toks = matmul_fm(a2_d, hB, h2_toks, hA, bias_idx=1, guards=h2_toks)
    else:
        h3_toks = h2_toks

    xtile = [lb.sb(f"rx{i}", [128, 1024], F32) for i in range(2)]
    atile = [lb.sb(f"ra{i}", [128, 1024], F32) for i in range(2)]
    otile = [lb.sb(f"ro{i}", [128, 1024], F32) for i in range(2)]
    ttmp = lb.sb("ttmp", [128, 128], F32)
    st_toks = []
    prev_st = {}
    prev_trr = {}
    prev_tt = None
    for tb in range(16):
        i = tb % 2
        dl = [prev_st[i]] if i in prev_st else []
        ld = p.dma("sync", [(xtile[i][:], x[tb * 128:(tb + 1) * 128, :])],
                   deps=dl + E["rs_tok"])
        lda2 = p.dma("gpsimd", [(atile[i][:], E["rs_out"][tb * 128:(tb + 1) * 128, :])],
                     deps=dl + E["rs_tok"])
        adds = []
        for dc in range(8):
            pb = psbf[dc % 2]
            war = [prev_trr[dc % 2]] if (dc % 2) in prev_trr else []
            tr = p.op("tensor", (lambda _dc, _tb, _pb: lambda e: e.transpose(
                _pb[:], hA[:, _dc, _tb * 128:(_tb + 1) * 128], identbf[:]))(dc, tb, pb),
                deps=h3_toks + [tok_id] + war)
            cp = p.op("vector", (lambda _pb: lambda e: e.tensor_copy(ttmp[:], _pb[:]))(pb),
                      deps=[tr] + ([prev_tt] if prev_tt is not None else []))
            ad = p.op("vector", (lambda _dc, _i: lambda e: e.tensor_add(
                otile[_i][:, _dc * 128:(_dc + 1) * 128],
                xtile[_i][:, _dc * 128:(_dc + 1) * 128], ttmp[:]))(dc, i),
                deps=[cp, ld] + dl)
            prev_trr[dc % 2] = cp
            prev_tt = ad
            adds.append(ad)
        a1t = p.op("vector", (lambda _i: lambda e: e.tensor_add(
            otile[_i][:], otile[_i][:], atile[_i][:]))(i), deps=adds + [ld, lda2])
        a2t = p.op("vector", (lambda _i: lambda e: e.tensor_add(
            otile[_i][:], otile[_i][:], boutb[:]))(i), deps=[a1t] + bias_toks)
        st = p.dma("sync", [(o_x3[tb * 128:(tb + 1) * 128, :], otile[i][:])],
                   deps=[a2t])
        prev_st[i] = st
        st_toks.append(st)
    return st_toks

def emit_l3a(lb, E):
    nc, p = lb.nc, lb.p
    wk_d, wv_d, wr_d = E["wk"], E["wv"], E["wr"]
    td_d, tf_d = E["td"], E["tf"]

    identbf, tok_id = lb.const("identity_bf", np.eye(128, dtype=np.float32).astype(BF))
    jr = np.arange(16, dtype=np.float32)
    c_j7, tok_j7 = lb.const("jrow7", np.broadcast_to((jr - 7.5)[None, :], (64, 16)).copy())
    c_j7n, tok_j7n = lb.const("jrow7n", np.broadcast_to((7.5 - jr)[None, :], (64, 16)).copy())
    c_d16, tok_d16 = lb.const("d16row", np.broadcast_to(
        (16.0 * np.arange(8, dtype=np.float32))[None, :], (64, 8)).copy())
    c_tau, tok_tau = lb.const("taurow", np.broadcast_to(
        (np.arange(128, dtype=np.float32) + 1.0)[None, :], (64, 128)).copy())
    c_k127, tok_k127 = lb.const("krow127", np.broadcast_to(
        (127.0 - np.arange(128, dtype=np.float32))[None, :], (64, 128)).copy())
    c_128, tok_c128 = lb.const("c128col", np.full((64, 1), 128.0, np.float32))
    ii = np.arange(128)[:, None]
    jj = np.arange(128)[None, :]
    masks = []
    for d in range(8):
        m = ((ii // 16 - jj // 16) == d) & (ii >= jj)
        mk, tokm = lb.const(f"smask{d}", m.astype(np.uint8), dt=U8)
        masks.append((mk, tokm))
    zeroc, tok_z = lb.const("zero128", np.zeros((128, 128), np.float32))

    xnT, xn_toks = E["xnT"], E["xn_toks"]
    ypair = lb.sb("ypair", [64, 2, 4096], BF16)
    att_done_all = []
    tdsb = lb.sb("tdsb", [64, 8], F32)
    tfsb = lb.sb("tfsb", [64, 8], F32)
    ld_tdf = p.dma("sync", [(tdsb[:], td_d), (tfsb[:], tf_d)])

    kTB = lb.sb("kTB", [64, 2, 4096], BF16)
    rrA = lb.sb("rrA", [64, 2, 4096], BF16)
    Rtil = lb.sb("Rtil", [64, 2, 4096], BF16)
    vtok = lb.sb("vtok", [128, 32, 128], BF16)
    ktok = lb.sb("ktok", [128, 32, 128], BF16)
    wq_t = lb.sb("wq_t", [128, 8, 128], BF16)
    wk_t = lb.sb("wk_t", [128, 8, 128], BF16)
    wv_t = lb.sb("wv_t", [128, 8, 128], BF16)

    lnD = lb.sb("lnD", [64, 2], F32)
    P7 = lb.sb("P7", [64, 2, 16], BF16)
    Pb7 = lb.sb("Pb7", [64, 2, 16], BF16)
    D16 = lb.sb("D16", [64, 2, 8], F32)
    Ptau = lb.sb("Ptau", [64, 2, 128], BF16)
    Pkfm = lb.sb("Pkfm", [64, 128], F32)
    Pktok = lb.sb("Pktok", [128, 2, 64], BF16)
    D128 = lb.sb("D128", [64, 2], F32)
    St = lb.sb("St", [64, 64], F32)
    Stb = lb.sb("Stb", [64, 64], BF16)
    Bd2 = [lb.sb(f"Bd{i}", [64, 128], BF16) for i in range(2)]
    Sacc = lb.sb("Sacc", [128, 128], F32)
    Sbf = lb.sb("Sbf", [128, 128], BF16)
    PT = lb.sb("PT", [128, 128], BF16)

    psA = lb.ps("psA", [128, 512], F32)
    psV = lb.ps("psV", [128, 128], F32)
    psY = lb.ps("psY", [64, 128], F32)
    psS = lb.ps("psS", [128, 128], F32)
    psC = lb.ps("psC", [64, 64], F32)
    psT = lb.ps("psT", [128, 128], BF16)
    psTf = lb.ps("psTf", [128, 64], F32)

    def proj_tok(wtile, wtok, dst, scale_col=None, guards=()):
        toks = []
        prev = list(guards)
        for tb in range(32):
            mms = []
            for kc in range(8):
                t = p.op("tensor", (lambda _kc, _tb: lambda e: e.matmul(
                    psV[:], xnT[:, _kc, _tb * 128:(_tb + 1) * 128], wtile[:, _kc, :],
                    start=(_kc == 0), stop=(_kc == 7)))(kc, tb),
                    deps=([wtok] + xn_toks + prev) if kc == 0 else [])
                mms.append(t)
            ev = p.op("vector", (lambda _tb: lambda e: e.tensor_copy(
                dst[:, _tb, :], psV[:]))(tb), deps=[mms[-1]])
            prev = [ev]
            toks.append(ev)
        return toks

    carry_state = []
    pair_guard = []
    for hp in range(4):
        csl = slice(hp * 128, (hp + 1) * 128)
        ldw = p.dma("gpsimd", [
            (wk_t[:], wk_d[:, csl].rearrange("(c p) n -> p c n", p=128)),
            (wv_t[:], wv_d[:, csl].rearrange("(c p) n -> p c n", p=128)),
            (wq_t[:], wr_d[:, csl].rearrange("(c p) n -> p c n", p=128)),
        ], deps=pair_guard)
        k_toks = []
        r_toks = []
        prev = list(pair_guard)
        for h2 in range(2):
            hh = hp * 2 + h2
            for tc in range(8):
                mms = []
                for kc in range(8):
                    t = p.op("tensor", (lambda _kc, _tc, _h2: lambda e: e.matmul(
                        psA[:, 0:512], wk_t[:, _kc, _h2 * 64:(_h2 + 1) * 64].rearrange("p n -> p n")
                        if False else wk_t[:, _kc, :],
                        xnT[:, _kc, _tc * 512:(_tc + 1) * 512],
                        start=(_kc == 0), stop=(_kc == 7)))(kc, tc, h2),
                        deps=([ldw, ld_tdf] + xn_toks + prev) if kc == 0 else [])
                    mms.append(t)
                if h2 == 0 and tc < 8:
                    pass
                e1 = p.op("vector", (lambda _tc, _hp: lambda e: e.tensor_scalar_mul(
                    kTB[:, 0, _tc * 512:(_tc + 1) * 512], psA[0:64, :],
                    tfsb[:, 2 * _hp:2 * _hp + 1]))(tc, hp), deps=[mms[-1], ld_tdf])
                e2 = p.op("vector", (lambda _tc, _hp: lambda e: e.tensor_scalar_mul(
                    kTB[:, 1, _tc * 512:(_tc + 1) * 512], psA[64:128, :],
                    tfsb[:, 2 * _hp + 1:2 * _hp + 2]))(tc, hp), deps=[mms[-1]])
                prev = [e1, e2]
                k_toks += [e1, e2]
            break
        for tc in range(8):
            mms = []
            for kc in range(8):
                t = p.op("tensor", (lambda _kc, _tc: lambda e: e.matmul(
                    psA[:, 0:512], wq_t[:, _kc, :],
                    xnT[:, _kc, _tc * 512:(_tc + 1) * 512],
                    start=(_kc == 0), stop=(_kc == 7)))(kc, tc),
                    deps=([ldw] + xn_toks + prev) if kc == 0 else [])
                mms.append(t)
            e1 = p.op("scalar", (lambda _tc: lambda e: e.activation(
                rrA[:, 0, _tc * 512:(_tc + 1) * 512], psA[0:64, :], AF.Sigmoid))(tc),
                deps=[mms[-1]] + prev)
            e2 = p.op("scalar", (lambda _tc: lambda e: e.activation(
                rrA[:, 1, _tc * 512:(_tc + 1) * 512], psA[64:128, :], AF.Sigmoid))(tc),
                deps=[mms[-1]] + prev)
            prev = [e1, e2]
            r_toks += [e1, e2]
        vt_toks = proj_tok(wv_t, ldw, vtok, guards=prev)
        kt_toks = proj_tok(wk_t, ldw, ktok, guards=vt_toks[-1:])
        tbl_toks = []
        for h2 in range(2):
            hh = hp * 2 + h2
            l1 = p.op("scalar", (lambda _h2, _hh: lambda e: e.activation(
                lnD[:, _h2:_h2 + 1], tdsb[:, _hh:_hh + 1], AF.Ln))(h2, hh),
                deps=[ld_tdf] + pair_guard + tbl_toks)
            a1 = p.op("scalar", (lambda _h2: lambda e: e.activation(
                P7[:, _h2, :], c_j7[:], AF.Exp, scale=lnD[:, _h2:_h2 + 1]))(h2),
                deps=[l1, tok_j7])
            a2 = p.op("scalar", (lambda _h2: lambda e: e.activation(
                Pb7[:, _h2, :], c_j7n[:], AF.Exp, scale=lnD[:, _h2:_h2 + 1]))(h2),
                deps=[l1, tok_j7n])
            a3 = p.op("scalar", (lambda _h2: lambda e: e.activation(
                D16[:, _h2, :], c_d16[:], AF.Exp, scale=lnD[:, _h2:_h2 + 1]))(h2),
                deps=[l1, tok_d16])
            a4 = p.op("scalar", (lambda _h2: lambda e: e.activation(
                Ptau[:, _h2, :], c_tau[:], AF.Exp, scale=lnD[:, _h2:_h2 + 1]))(h2),
                deps=[l1, tok_tau])
            a5 = p.op("scalar", (lambda _h2: lambda e: e.activation(
                Pkfm[:], c_k127[:], AF.Exp, scale=lnD[:, _h2:_h2 + 1]))(h2),
                deps=[l1, tok_k127] + tbl_toks)
            a6 = p.op("scalar", (lambda _h2: lambda e: e.activation(
                D128[:, _h2:_h2 + 1], c_128[:], AF.Exp, scale=lnD[:, _h2:_h2 + 1]))(h2),
                deps=[l1, tok_c128])
            pkb = p.op("vector", (lambda: lambda e: e.tensor_copy(Bd2[0][:], Pkfm[:]))(),
                       deps=[a5] + carry_state)
            trp = p.op("tensor", (lambda: lambda e: e.transpose(
                psT[:, 0:64], Bd2[0][:], identbf[0:64, 0:64]))(), deps=[pkb, tok_id] + tbl_toks)
            evp = p.op("vector", (lambda _h2: lambda e: e.tensor_copy(
                Pktok[:, _h2, :], psT[:, 0:64]))(h2), deps=[trp])
            tbl_toks += [a1, a2, a3, a4, a6, evp]
        sc_toks = []
        ap_P7 = [None, None]
        for h2 in range(2):
            pa = Ptau[0:64, h2, :]
            rt = p.op("vector", (lambda _h2, _a: lambda e: e.tensor_mul(
                Rtil[:, _h2, :], rrA[:, _h2, :],
                bass.AP(_a.tensor, _a.offset, [_a.ap[0], [0, 32], [1, 128]])))(h2, pa),
                deps=r_toks + tbl_toks + pair_guard)
            p7a = P7[0:64, h2, :]
            ra = p.op("vector", (lambda _h2, _a: lambda e: e.tensor_mul(
                rrA[:, _h2, :], rrA[:, _h2, :],
                bass.AP(_a.tensor, _a.offset, [_a.ap[0], [0, 256], [1, 16]])))(h2, p7a),
                deps=[rt] + r_toks + tbl_toks)
            pb = Pb7[0:64, h2, :]
            kb = p.op("vector", (lambda _h2, _a: lambda e: e.tensor_mul(
                kTB[:, _h2, :], kTB[:, _h2, :],
                bass.AP(_a.tensor, _a.offset, [_a.ap[0], [0, 256], [1, 16]])))(h2, pb),
                deps=k_toks + tbl_toks + [ra])
            pk = Pktok[0:128, h2, :]
            kt = p.op("vector", (lambda _h2, _a: lambda e: e.tensor_mul(
                ktok[:, :, _h2 * 64:(_h2 + 1) * 64], ktok[:, :, _h2 * 64:(_h2 + 1) * 64],
                bass.AP(_a.tensor, _a.offset, [_a.ap[0], [0, 32], [1, 64]])))(h2, pk),
                deps=kt_toks + tbl_toks + [kb])
            sc_toks += [rt, ra, kb, kt]
        for h2 in range(2):
            z1 = p.op("vector", lambda e: e.memset(St[:], 0.0),
                      deps=sc_toks + pair_guard + carry_state)
            prev_y = None
            prev_state = [z1]
            for ch in range(32):
                csl2 = slice(ch * 128, (ch + 1) * 128)
                sb1 = p.op("scalar", lambda e: e.activation(Stb[:], St[:], AF.Copy),
                           deps=prev_state)
                mmI = p.op("tensor", (lambda _h2, _c: lambda e: e.matmul(
                    psY[:], Stb[:], Rtil[:, _h2, _c * 128:(_c + 1) * 128],
                    start=True, stop=False))(h2, ch),
                    deps=[sb1] + ([prev_y] if prev_y else []) + sc_toks)
                sel_prev = None
                prev_mmS = {}
                prev_sel2 = {}
                for d in range(8):
                    di = d % 2
                    _ps = psS if di == 0 else psV
                    bd = p.op("vector", (lambda _h2, _c, _d, _di: lambda e: e.tensor_scalar_mul(
                        Bd2[_di][:], kTB[:, _h2, _c * 128:(_c + 1) * 128],
                        D16[:, _h2, _d:_d + 1]))(h2, ch, d, di),
                        deps=[mmI] + ([prev_mmS[di]] if di in prev_mmS else [])
                        + sc_toks + tbl_toks)
                    mmS = p.op("tensor", (lambda _h2, _c, _di, _p: lambda e: e.matmul(
                        _p[:], rrA[:, _h2, _c * 128:(_c + 1) * 128], Bd2[_di][:],
                        start=True, stop=True))(h2, ch, di, _ps),
                        deps=[bd] + ([prev_sel2[di]] if di in prev_sel2 else []))
                    prev_mmS[di] = mmS
                    sel = p.op("vector", (lambda _d, _p: lambda e: e.select(
                        Sacc[:], masks[_d][0][:], _p[:],
                        zeroc[:] if _d == 0 else Sacc[:], add_drain=True))(d, _ps),
                        deps=[mmS, masks[d][1], tok_z] + ([sel_prev] if sel_prev else []))
                    prev_sel2[di] = sel
                    sel_prev = sel
                cb = p.op("vector", lambda e: e.tensor_copy(Sbf[:], Sacc[:]), deps=[sel_prev])
                trS = p.op("tensor", (lambda: lambda e: e.transpose(
                    psT[:], Sbf[:], identbf[:]))(), deps=[cb, tok_id])
                evS = p.op("vector", lambda e: e.tensor_copy(PT[:], psT[:]), deps=[trS])
                mmP = p.op("tensor", (lambda _h2, _c: lambda e: e.matmul(
                    psY[:], vtok[:, _c, _h2 * 64:(_h2 + 1) * 64], PT[:],
                    start=False, stop=True))(h2, ch), deps=[evS])
                evY = p.op("vector", (lambda _h2, _c: lambda e: e.tensor_copy(
                    ypair[:, _h2, _c * 128:(_c + 1) * 128], psY[:]))(h2, ch),
                    deps=[mmP])
                prev_y = evY
                mmC = p.op("tensor", (lambda _h2, _c: lambda e: e.matmul(
                    psC[:], ktok[:, _c, _h2 * 64:(_h2 + 1) * 64],
                    vtok[:, _c, _h2 * 64:(_h2 + 1) * 64],
                    start=True, stop=True))(h2, ch), deps=[sb1] + sc_toks)
                u1 = p.op("vector", (lambda _h2: lambda e: e.tensor_scalar_mul(
                    St[:], St[:], D128[:, _h2:_h2 + 1]))(h2), deps=[mmI, sb1] + tbl_toks)
                u2 = p.op("vector", lambda e: e.tensor_add(St[:], St[:], psC[:]), deps=[u1, mmC])
                prev_state = [u2]
            carry_state = list(prev_state) + [prev_y]
        sta = p.dma("sync", [(E["att_dram"][:, hp * 2 + h2, :], ypair[:, h2, :])
                             for h2 in range(2)], deps=[prev_y, prev_state[0]])
        pair_guard = [sta]
        att_done_all.append(sta)
    return emit_wo_partial(lb, E, E["att_dram"], att_done_all, nheads=8)

def emit_l3b(lb, E):
    nc, p = lb.nc, lb.p
    x = E["x"]
    qg_d, qu_d, qd_d = E["qgT"], E["quT"], E["qdT"]
    o_x4 = E["x_out"]
    xnT = E["xnT"]

    identbf, tok_id = lb.const("identity_bf", np.eye(128, dtype=np.float32).astype(BF))
    aoutT = lb.sb("aoutT", [128, 8, 2048], BF16)
    ld_xn = p.op("vector", lambda e: e.engine_nop(), deps=E["xn_toks"])
    ld_ao = p.dma("sync", [(aoutT[:], E["rs_out"].rearrange("p (h t) -> p h t", h=8))],
                  deps=E["rs_tok"])
    ao_toks = [ld_ao]

    psU = lb.ps("psU", [128, 512], F32)
    psZ = lb.ps("psZ", [128, 512], F32)
    psbf = [lb.ps(f"psbf{i}", [128, 128], BF16) for i in range(2)]

    wgt = [lb.sb(f"wgt{i}", [128, 8, 128], BF16) for i in range(2)]
    wut = [lb.sb(f"wut{i}", [128, 8, 128], BF16) for i in range(2)]
    wdt = [lb.sb(f"wdt{i}", [128, 24, 128], BF16) for i in range(2)]
    hT = lb.sb("hT", [128, 24, 512], BF16)
    ztmp = lb.sb("ztmp", [128, 512], BF16)
    ffn_toks = []
    prev_wg = {}
    prev_wd = {}
    prev_tc2 = []
    for tc in range(4):
        tsl = slice(tc * 512, (tc + 1) * 512)
        h_toks = []
        for oc in range(24):
            i = oc % 2
            wl = [prev_wg[i]] if i in prev_wg else []
            ldg = p.dma("gpsimd", [
                (wgt[i][:], qg_d[:, oc * 128:(oc + 1) * 128].rearrange("(c p) n -> p c n", p=128)),
                (wut[i][:], qu_d[:, oc * 128:(oc + 1) * 128].rearrange("(c p) n -> p c n", p=128)),
            ], deps=wl)
            mmsG = []
            for kc in range(8):
                t = p.op("tensor", (lambda _kc, _i, _tc: lambda e: e.matmul(
                    psU[:], wgt[_i][:, _kc, :], xnT[:, _kc, _tc * 512:(_tc + 1) * 512],
                    start=(_kc == 0), stop=(_kc == 7)))(kc, i, tc),
                    deps=([ldg, ld_xn] + prev_tc2 + h_toks[-1:]) if kc == 0 else [])
                mmsG.append(t)
            sg = p.op("scalar", (lambda _oc: lambda e: e.activation(
                ztmp[:], psU[:], AF.Silu, scale=1.0 / 127.0))(oc), deps=[mmsG[-1]] + prev_tc2)
            mmsU = []
            for kc in range(8):
                t = p.op("tensor", (lambda _kc, _i, _tc: lambda e: e.matmul(
                    psU[:], wut[_i][:, _kc, :], xnT[:, _kc, _tc * 512:(_tc + 1) * 512],
                    start=(_kc == 0), stop=(_kc == 7)))(kc, i, tc),
                    deps=[sg])
                mmsU.append(t)
            hm = p.op("vector", (lambda _oc: lambda e: e.tensor_mul(
                hT[:, _oc, :], ztmp[:], psU[:]))(oc), deps=[mmsU[-1], sg] + prev_tc2)
            prev_wg[i] = hm
            h_toks.append(hm)
        for zc in range(8):
            i = zc % 2
            wl = [prev_wd[i]] if i in prev_wd else []
            ldd = p.dma("gpsimd", [(wdt[i][:], qd_d[:, zc * 128:(zc + 1) * 128]
                                    .rearrange("(c p) n -> p c n", p=128))], deps=wl)
            mms = []
            for kc in range(24):
                t = p.op("tensor", (lambda _kc, _i: lambda e: e.matmul(
                    psZ[:], wdt[_i][:, _kc, :], hT[:, _kc, :],
                    start=(_kc == 0), stop=(_kc == 23)))(kc, i),
                    deps=([ldd] + h_toks + ffn_toks[-1:]) if kc == 0 else [])
                mms.append(t)
            z1 = p.op("vector", (lambda _zc, _tc: lambda e: e.tensor_scalar_mul(
                ztmp[:], psZ[:], 1.0 / 16129.0))(zc, tc),
                deps=[mms[-1]] + h_toks)
            z2 = p.op("vector", (lambda _zc, _tc: lambda e: e.tensor_add(
                aoutT[:, _zc, _tc * 512:(_tc + 1) * 512],
                aoutT[:, _zc, _tc * 512:(_tc + 1) * 512], ztmp[:]))(zc, tc),
                deps=[z1] + ao_toks)
            prev_wd[i] = z2
            ffn_toks.append(z2)
        prev_tc2 = [ffn_toks[-1]]

    xtile = [lb.sb(f"rx{i}", [128, 1024], F32) for i in range(2)]
    otile = [lb.sb(f"ro{i}", [128, 1024], F32) for i in range(2)]
    ttmp = lb.sb("ttmp", [128, 128], F32)
    st_toks = []
    prev_st = {}
    prev_trr = {}
    prev_tt = None
    for tb in range(16):
        i = tb % 2
        dl = [prev_st[i]] if i in prev_st else []
        ld = p.dma("sync", [(xtile[i][:], x[tb * 128:(tb + 1) * 128, :])], deps=dl)
        adds = []
        for dc in range(8):
            pb = psbf[dc % 2]
            war = [prev_trr[dc % 2]] if (dc % 2) in prev_trr else []
            tr = p.op("tensor", (lambda _dc, _tb, _pb: lambda e: e.transpose(
                _pb[:], aoutT[:, _dc, _tb * 128:(_tb + 1) * 128], identbf[:]))(dc, tb, pb),
                deps=ffn_toks + [tok_id] + war)
            cp = p.op("vector", (lambda _pb: lambda e: e.tensor_copy(ttmp[:], _pb[:]))(pb),
                      deps=[tr] + ([prev_tt] if prev_tt is not None else []))
            ad = p.op("vector", (lambda _dc, _i: lambda e: e.tensor_add(
                otile[_i][:, _dc * 128:(_dc + 1) * 128],
                xtile[_i][:, _dc * 128:(_dc + 1) * 128], ttmp[:]))(dc, i),
                deps=[cp, ld] + dl)
            prev_trr[dc % 2] = cp
            prev_tt = ad
            adds.append(ad)
        st = p.dma("sync", [(o_x4[tb * 128:(tb + 1) * 128, :], otile[i][:])], deps=adds)
        prev_st[i] = st
        st_toks.append(st)
    return st_toks



PAIRS = [[0, 1], [2, 3], [4, 5], [6, 7]]
ALL8 = [[0, 1, 2, 3, 4, 5, 6, 7]]


def build_fused(debug_taps=False, upto=4, lfrom=0):
    lb = LB("full")
    nc, p = lb.nc, lb.p

    def din(name, shape, dt=F32):
        return lb.dram_in(name, shape, dt)

    x0 = din("x0_own", [2048, 1024])
    o_x4 = lb.dram_out("x4_own", [2048, 1024], F32)
    W = {}
    for i in (0, 1):
        W[f"l{i}_wq"] = din(f"l{i}_wq", [1024, 512])
        W[f"l{i}_wk"] = din(f"l{i}_wk", [1024, 512])
        W[f"l{i}_wv"] = din(f"l{i}_wv", [1024, 512])
        W[f"l{i}_wo"] = din(f"l{i}_wo", [64, 8, 1024])
    W["l0_g_r"] = din("l0_g_r", [128, 8])
    W["l0_b_r"] = din("l0_b_r", [128, 8])
    W["l0_fc1_q"] = din("l0_fc1_q", [1024, 4096], I32)
    W["l0_fc1_s"] = din("l0_fc1_s", [8, 4096])
    W["l0_fc1_b"] = din("l0_fc1_b", [128, 32])
    W["l0_fc2_q"] = din("l0_fc2_q", [4096, 1024], I32)
    W["l0_fc2_s"] = din("l0_fc2_s", [32, 1024])
    W["l0_fc2_b"] = din("l0_fc2_b", [128, 8])
    W["l1_wg"] = din("l1_wg", [1024, 8])
    W["l1_mlt"] = din("l1_mlt", [64, 8])
    W["l1_mall"] = din("l1_mall", [64, 8])
    W["l1_cfc"] = din("l1_cfc", [1024, 4096])
    W["l1_cpr"] = din("l1_cpr", [4096, 1024])
    W["l2_win_a"] = din("l2_win_a", [1024, 16, 128])
    W["l2_win_b"] = din("l2_win_b", [1024, 16, 64])
    W["l2_bia"] = din("l2_bia", [128, 16])
    W["l2_bib"] = din("l2_bib", [64, 16])
    for nm in ("shw_0", "shb_0", "shw_1", "shb_1", "shw_b", "shb_b"):
        shp = [64, 16, 3] if nm.startswith("shw") else [64, 16]
        W["l2_" + nm] = din("l2_" + nm, shp)
    W["l2_W1"] = din("l2_W1", [3, 64])
    W["l2_b1f"] = din("l2_b1f", [64, 4])
    W["l2_W2"] = din("l2_W2", [64, 64])
    W["l2_W3r"] = din("l2_W3r", [64, 8, 64])
    W["l2_b3c"] = din("l2_b3c", [64, 8])
    W["l2_ndel"] = din("l2_ndel", [64, 8])
    W["l2_fbrow"] = din("l2_fbrow", [1, 512])
    W["l2_wout_own"] = din("l2_wout_own", [512, 1024])
    W["l2_bout_row"] = din("l2_bout_row", [1, 1024])
    W["l2_wf"] = din("l2_wf", [1024, 1024])
    W["l2_a1"] = din("l2_a1", [1024, 1024])
    W["l2_a2"] = din("l2_a2", [1024, 1024])
    W["l2_bf2"] = din("l2_bf2", [128, 8])
    W["l3_wk"] = din("l3_wk", [1024, 512])
    W["l3_wv"] = din("l3_wv", [1024, 512])
    W["l3_wr"] = din("l3_wr", [1024, 512])
    W["l3_td"] = din("l3_td", [64, 8])
    W["l3_tf"] = din("l3_tf", [64, 8])
    W["l3_wo"] = din("l3_wo", [64, 8, 1024])
    W["l3_qgT"] = din("l3_qgT", [1024, 3072], I32)
    W["l3_quT"] = din("l3_quT", [1024, 3072], I32)
    W["l3_qdT"] = din("l3_qdT", [3072, 1024], I32)

    def mkout(name, shape, dt):
        if debug_taps:
            return lb.dram_out(name, shape, dt)
        return lb.dram_scratch(name, shape, dt)
    mids = [mkout("x1_own", [2048, 1024], F32), mkout("x2_own", [2048, 1024], F32),
            mkout("x3_own", [2048, 1024], F32)]
    xs = [None, mids[0], mids[1], mids[2], o_x4]
    xs[lfrom] = x0


    final_toks = []
    for li in range(lfrom, upto):
        lb.tag = f"L{li}_"
        ln_in = lb.dram_scratch("ln_in", [128, 16384], BF16)
        ln_out = lb.dram_scratch("ln_out", [256, 16384], BF16)
        if li == 2:
            rs_in = lb.dram_scratch("rs_in", [4096, 1024], BF16)
            rs_out = lb.dram_scratch("rs_out", [2048, 1024], BF16)
            att_dram = lb.dram_scratch("yt_dram", [512, 4096], BF16)
        else:
            rs_in = lb.dram_scratch("rs_in", [256, 16384], BF16)
            rs_out = lb.dram_scratch("rs_out", [128, 16384], BF16)
            att_dram = lb.dram_scratch("att_dram", [64, 8, 4096], BF16)
        with lb.scoped():
            with lb.scoped():
                xn_ownT = lb.sb("xnOT", [128, 8, 2048], BF16)
                ln_toks = emit_ln_transpose(lb, xs[li], xn_ownT, nt=16)
                st_ln = p.dma("sync", [(ln_in, xn_ownT[:].rearrange("p h t -> p (h t)"))],
                              deps=ln_toks)
            cc1 = p.cc((lambda _i, _o: lambda e: e.collective_compute(
                "AllGather", ALU.bypass, replica_groups=PAIRS,
                ins=[_i.opt()], outs=[_o.opt()]))(ln_in, ln_out), deps=[st_ln])
            p.barrier([cc1])

            def own_xn():
                t = lb.sb("xnOT2", [128, 8, 2048], BF16)
                ld = p.dma("sync", [(t[:], ln_in.rearrange("p (h t) -> p h t", h=8))],
                           deps=[cc1])
                return t, [ld]
            with lb.scoped():
                xnT = lb.sb("xnTf", [128, 8, 4096], BF16)
                ld1 = p.dma("sync", [
                    (xnT[:, :, 0:2048], ln_out[0:128, :].rearrange("p (h t) -> p h t", h=8)),
                    (xnT[:, :, 2048:4096], ln_out[128:256, :].rearrange("p (h t) -> p h t", h=8)),
                ], deps=[cc1])
                E = dict(xnT=xnT, xn_toks=[ld1], rs_in=rs_in, att_dram=att_dram,
                         yt_dram=att_dram)
                if li == 0:
                    E.update(wq=W["l0_wq"], wk=W["l0_wk"], wv=W["l0_wv"], wo=W["l0_wo"])
                    mix_toks = emit_l0a(lb, E)
                elif li == 1:
                    E.update(wq=W["l1_wq"], wk=W["l1_wk"], wv=W["l1_wv"], wo=W["l1_wo"])
                    mix_toks = emit_l1a(lb, E)
                elif li == 2:
                    for nm in ("win_a", "win_b", "bia", "bib", "shw_0", "shb_0", "shw_1",
                               "shb_1", "shw_b", "shb_b", "W1", "b1f", "W2", "W3r", "b3c",
                               "ndel", "fbrow", "wout_own"):
                        E[nm] = W["l2_" + nm]
                    mix_toks = emit_l2a(lb, E)
                else:
                    E.update(wk=W["l3_wk"], wv=W["l3_wv"], wr=W["l3_wr"],
                             td=W["l3_td"], tf=W["l3_tf"], wo=W["l3_wo"])
                    mix_toks = emit_l3a(lb, E)
            cc2 = p.cc((lambda _i, _o: lambda e: e.collective_compute(
                "ReduceScatter", ALU.add, replica_groups=PAIRS,
                ins=[_i.opt()], outs=[_o.opt()]))(rs_in, rs_out), deps=mix_toks)
            if debug_taps:
                tap = lb.dram_out(f"L{li}_rs_tap",
                                  [2048, 1024] if li == 2 else [128, 16384], BF16)
                cc2 = p.dma("sync", [(tap, rs_out[:, :] if li == 2 else rs_out)],
                            deps=[cc2])
                if li == 2:
                    tap2 = lb.dram_out("L2_yt_tap", [512, 4096], BF16)
                    tap3 = lb.dram_out("L2_ln_tap", [256, 16384], BF16)
                    cc2 = p.dma("sync", [(tap2, att_dram[:, :]), (tap3, ln_out)],
                                deps=[cc2] + mix_toks)
            p.barrier([cc2])
            if li == 2 and globals().get("L2_SKIP_B", False):
                st = [p.dma("sync", [(xs[3], xs[2])], deps=[cc2])]
            elif li != 1:
                with lb.scoped():
                    xnb, xnb_toks = own_xn()
                    E2 = dict(x=xs[li], rs_out=rs_out, rs_tok=[cc2],
                              xnT=xnb, xn_toks=xnb_toks, x_out=xs[li + 1])
                    if li == 0:
                        for nm in ("g_r", "b_r", "fc1_q", "fc1_s", "fc1_b",
                                   "fc2_q", "fc2_s", "fc2_b"):
                            E2[nm] = W["l0_" + nm]
                        st = emit_l0b(lb, E2)
                    elif li == 2:
                        for nm in ("wf", "a1", "a2", "bf2", "bout_row"):
                            E2[nm] = W["l2_" + nm]
                        st = emit_l2b(lb, E2)
                    else:
                        for nm in ("qgT", "quT", "qdT"):
                            E2[nm] = W["l3_" + nm]
                        st = emit_l3b(lb, E2)
            else:
                xn_tok = lb.dram_scratch("xn_tok", [2048, 1024], BF16)
                mdisp = lb.dram_scratch("mdisp", [8 * CAP, 1024], BF16)
                mbatch = lb.dram_scratch("mbatch", [CAP, 1024], BF16)
                oe_ag = lb.dram_scratch("oe_ag", [CAP, 1024], BF16)
                oe_all = lb.dram_scratch("oe_all", [8 * CAP + 64, 1024], BF16)
                cn_ag_in = lb.dram_scratch("cn_ag_in", [8, 2], F32)
                cn_ag_out = lb.dram_scratch("cn_ag_out", [64, 2], F32)
                pc_b = lb.dram_scratch("pc_b", [2, 16, 128], F32)
                row_b0 = lb.dram_scratch("row_b0", [2048], F32)
                row_b1 = lb.dram_scratch("row_b1", [2048], F32)
                slotpos = lb.sb("slotpos", [1, 2, 2048], F32)
                wrow = lb.sb("wrow", [1, 2, 2048], F32)
                gidx = lb.sb("gidx", [128, 1], I32)
                with lb.scoped():
                    xnb, xnb_toks = own_xn()
                    E3 = dict(xnT=xnb, xn_toks=xnb_toks, w_g=W["l1_wg"],
                              mlt=W["l1_mlt"], mall=W["l1_mall"],
                              cn_ag_in=cn_ag_in, cn_ag_out=cn_ag_out,
                              mdisp=mdisp, xn_tok=xn_tok, pc_bounce=pc_b,
                              row_b0=row_b0, row_b1=row_b1,
                              slotpos=slotpos, wrow=wrow, gidx=gidx)
                    sc_toks, cmb_ready = emit_l1route(lb, E3)
                cc3 = p.cc((lambda _i, _o: lambda e: e.collective_compute(
                    "ReduceScatter", ALU.add, replica_groups=ALL8,
                    ins=[_i.opt()], outs=[_o.opt()]))(mdisp, mbatch),
                    deps=sc_toks)
                p.barrier([cc3] + cmb_ready)
                with lb.scoped():
                    E4 = dict(mbatch=mbatch, rs_tok=[cc3], c_fc=W["l1_cfc"],
                              c_proj=W["l1_cpr"], oe_out=oe_ag)
                    exp_st = emit_l1exp(lb, E4)
                cc4 = p.cc((lambda _i, _o: lambda e: e.collective_compute(
                    "AllGather", ALU.bypass, replica_groups=ALL8,
                    ins=[_i.opt()], outs=[_o[0:8 * CAP, :].opt()]))(oe_ag, oe_all),
                    deps=exp_st)
                p.barrier([cc4])
                with lb.scoped():
                    E5 = dict(oe_all=oe_all, ag_tok=[cc4], gidx=gidx,
                              cmb_ready=cmb_ready, slotpos=slotpos, wrow=wrow,
                              rs_out=rs_out, rs_tok=[cc2], x=xs[1], x_out=xs[2])
                    st = emit_l1cmb(lb, E5)
            p.barrier(st)
            final_toks = st
        lb.tag = ""
    lb.finish(final_toks)
    return lb

PERM64 = np.concatenate([np.arange(0, 64, 2), np.arange(1, 64, 2)])

_BUILD_CACHE = {}


def _get_prog(name, build_fn):
    if name not in _BUILD_CACHE:
        lb = build_fn()
        consts = dict(lb.const_arrays)
        nc = lb.materialize()
        _BUILD_CACHE[name] = (nc, consts)
    return _BUILD_CACHE[name]


def _attn_w_slices(wq_rows, wk_rows, wv_rows, half, rope=True):
    h0 = half * 8
    if rope:
        wq = np.concatenate([wq_rows[h * 64:(h + 1) * 64][PERM64] for h in range(h0, h0 + 8)], 0).T
        wk = np.concatenate([wk_rows[h * 64:(h + 1) * 64][PERM64] for h in range(h0, h0 + 8)], 0).T
    else:
        wq = wq_rows[h0 * 64:(h0 + 8) * 64].T
        wk = wk_rows[h0 * 64:(h0 + 8) * 64].T
    wv = wv_rows[h0 * 64:(h0 + 8) * 64].T
    return (np.ascontiguousarray(wq), np.ascontiguousarray(wk), np.ascontiguousarray(wv))


def fused_in_map(inp, c):
    z = inp
    b, half = c // 2, c % 2
    h0 = half * 8
    own = slice(half * 2048, (half + 1) * 2048)
    x0 = np.asarray(z["L0_attn_x"], np.float32)
    im = {}
    im["x0_own"] = np.ascontiguousarray(x0[b][own])

    wqkv = np.asarray(z["L0_attn_w_qkv"])
    im["l0_wq"] = np.ascontiguousarray(np.concatenate(
        [wqkv[:, h * 64:(h + 1) * 64][:, PERM64] for h in range(h0, h0 + 8)], 1))
    im["l0_wk"] = np.ascontiguousarray(np.concatenate(
        [wqkv[:, 1024 + h * 64:1024 + (h + 1) * 64][:, PERM64] for h in range(h0, h0 + 8)], 1))
    im["l0_wv"] = np.ascontiguousarray(np.concatenate(
        [wqkv[:, 2048 + h * 64:2048 + (h + 1) * 64] for h in range(h0, h0 + 8)], 1))
    im["l0_wo"] = np.ascontiguousarray(np.asarray(z["L0_attn_w_o"])
                                       .reshape(16, 64, 1024).transpose(1, 0, 2)[:, h0:h0 + 8, :])
    im["l0_g_r"] = np.ascontiguousarray(np.asarray(z["L0_ffn_ln_gamma"]).reshape(8, 128).T)
    im["l0_b_r"] = np.ascontiguousarray(np.asarray(z["L0_ffn_ln_beta"]).reshape(8, 128).T)
    im["l0_fc1_q"] = np.asarray(z["L0_ffn_fc1_q"])
    im["l0_fc1_s"] = np.asarray(z["L0_ffn_fc1_scales"])
    im["l0_fc1_b"] = np.ascontiguousarray(np.asarray(z["L0_ffn_fc1_bias"]).reshape(32, 128).T)
    im["l0_fc2_q"] = np.asarray(z["L0_ffn_fc2_q"])
    im["l0_fc2_s"] = np.asarray(z["L0_ffn_fc2_scales"])
    im["l0_fc2_b"] = np.ascontiguousarray(np.asarray(z["L0_ffn_fc2_bias"]).reshape(8, 128).T)

    wq, wk, wv = _attn_w_slices(np.asarray(z["L1_attn_Wq"]), np.asarray(z["L1_attn_Wk"]),
                                np.asarray(z["L1_attn_Wv"]), half, rope=True)
    im["l1_wq"], im["l1_wk"], im["l1_wv"] = wq, wk, wv
    im["l1_wo"] = np.ascontiguousarray(np.asarray(z["L1_attn_Wo"]).T
                                       .reshape(16, 64, 1024).transpose(1, 0, 2)[:, h0:h0 + 8, :])
    im["l1_wg"] = np.ascontiguousarray(np.asarray(z["L1_ffn_w_g"]))
    mlt = np.zeros((64, 8), np.float32)
    mall = np.zeros((64, 8), np.float32)
    for cc in range(8):
        for e in range(8):
            mall[cc * 8 + e, e] = 1.0
            if cc < c:
                mlt[cc * 8 + e, e] = 1.0
    im["l1_mlt"] = mlt
    im["l1_mall"] = mall
    im["l1_cfc"] = np.ascontiguousarray(np.asarray(z["L1_ffn_c_fc"])[c])
    im["l1_cpr"] = np.ascontiguousarray(np.asarray(z["L1_ffn_c_proj"])[c])

    ch0 = half * 512
    Wi = np.asarray(z["L2_attn_W_in"])
    bi = np.asarray(z["L2_attn_b_in"])
    sw = np.asarray(z["L2_attn_short_w"])
    sb_ = np.asarray(z["L2_attn_short_b"])
    win_a = np.zeros((1024, 16, 128), np.float32)
    win_b = np.zeros((1024, 16, 64), np.float32)
    bia = np.zeros((128, 16), np.float32)
    bib = np.zeros((64, 16), np.float32)
    shw_0 = np.zeros((64, 16, 3), np.float32)
    shb_0 = np.zeros((64, 16), np.float32)
    shw_1 = np.zeros((64, 16, 3), np.float32)
    shb_1 = np.zeros((64, 16), np.float32)
    shw_b = np.zeros((64, 16, 3), np.float32)
    shb_b = np.zeros((64, 16), np.float32)
    W3r = np.zeros((64, 8, 64), np.float32)
    b3c = np.zeros((64, 8), np.float32)
    ndel = np.zeros((64, 8), np.float32)
    fcd = fft_dev_consts()
    for chg in range(8):
        cs = ch0 + chg * 64
        win_a[:, chg, 0:64] = Wi[:, cs:cs + 64]
        win_a[:, chg, 64:128] = Wi[:, 1024 + cs:1024 + cs + 64]
        win_b[:, chg, :] = Wi[:, 2048 + cs:2048 + cs + 64]
        bia[0:64, chg] = bi[cs:cs + 64]
        bia[64:128, chg] = bi[1024 + cs:1024 + cs + 64]
        bib[:, chg] = bi[2048 + cs:2048 + cs + 64]
        shw_0[:, chg] = sw[cs:cs + 64]
        shw_1[:, chg] = sw[1024 + cs:1024 + cs + 64]
        shb_0[:, chg] = sb_[cs:cs + 64]
        shb_1[:, chg] = sb_[1024 + cs:1024 + cs + 64]
        shw_b[:, chg] = sw[2048 + cs:2048 + cs + 64]
        shb_b[:, chg] = sb_[2048 + cs:2048 + cs + 64]
        W3r[:, chg, :] = np.asarray(z["L2_attn_W3"])[:, cs:cs + 64]
        b3c[:, chg] = np.asarray(z["L2_attn_b3"])[cs:cs + 64]
        ndel[:, chg] = fcd["ndel_full"][cs:cs + 64]
    f1 = np.asarray(z["L2_attn_f1"])
    b1 = np.asarray(z["L2_attn_b1"])
    f2 = np.asarray(z["L2_attn_f2"])
    b2 = np.asarray(z["L2_attn_b2"])
    b1f = np.stack([f1 / (2 * np.pi), f1 * b1 / (2 * np.pi),
                    -f2 / (2 * np.pi), f2 * b2 / (2 * np.pi)], 1).astype(np.float32)
    im["l2_win_a"] = win_a
    im["l2_win_b"] = win_b
    im["l2_bia"] = bia
    im["l2_bib"] = bib
    im["l2_shw_0"] = shw_0
    im["l2_shb_0"] = shb_0
    im["l2_shw_1"] = shw_1
    im["l2_shb_1"] = shb_1
    im["l2_shw_b"] = shw_b
    im["l2_shb_b"] = shb_b
    im["l2_W1"] = np.ascontiguousarray(np.asarray(z["L2_attn_W1"], np.float32))
    im["l2_b1f"] = b1f
    im["l2_W2"] = np.ascontiguousarray(np.asarray(z["L2_attn_W2"], np.float32))
    im["l2_W3r"] = W3r
    im["l2_b3c"] = b3c
    im["l2_ndel"] = ndel
    im["l2_fbrow"] = np.ascontiguousarray(
        np.asarray(z["L2_attn_filter_bias"], np.float32)[ch0:ch0 + 512][None, :])
    im["l2_wout_own"] = np.ascontiguousarray(np.asarray(z["L2_attn_W_out"])[ch0:ch0 + 512, :])
    im["l2_bout_row"] = np.ascontiguousarray(np.asarray(z["L2_attn_b_out"])[None, :])
    im["l2_wf"] = np.ascontiguousarray(np.asarray(z["L2_ffn_weight"]))
    im["l2_a1"] = np.ascontiguousarray(np.asarray(z["L2_ffn_aux_weight1"]))
    im["l2_a2"] = np.ascontiguousarray(np.asarray(z["L2_ffn_aux_weight2"]))
    im["l2_bf2"] = np.ascontiguousarray(np.asarray(z["L2_ffn_bias"]).reshape(8, 128).T)

    rows = slice(h0 * 64, (h0 + 8) * 64)
    im["l3_wk"] = np.ascontiguousarray(np.asarray(z["L3_attn_Wk"])[rows].T)
    im["l3_wv"] = np.ascontiguousarray(np.asarray(z["L3_attn_Wv"])[rows].T)
    im["l3_wr"] = np.ascontiguousarray(np.asarray(z["L3_attn_Wr"])[rows].T)
    im["l3_td"] = np.ascontiguousarray(np.asarray(z["L3_attn_time_decay"])[h0:h0 + 8].T)
    im["l3_tf"] = np.ascontiguousarray(np.asarray(z["L3_attn_time_first"])[h0:h0 + 8].T)
    im["l3_wo"] = np.ascontiguousarray(np.asarray(z["L3_attn_Wo"]).T
                                       .reshape(16, 64, 1024).transpose(1, 0, 2)[:, h0:h0 + 8, :])
    im["l3_qgT"] = np.ascontiguousarray(np.asarray(z["L3_ffn_w_gate"]).T)
    im["l3_quT"] = np.ascontiguousarray(np.asarray(z["L3_ffn_w_up"]).T)
    im["l3_qdT"] = np.ascontiguousarray(np.asarray(z["L3_ffn_w_down"]).T)
    return im


def kernel(**inp):
    nc, consts = _get_prog("fused", build_fused)
    in_maps = []
    for c in range(8):
        im = fused_in_map(inp, c)
        im.update(consts)
        in_maps.append(im)
    res = run_bass_kernel_spmd(nc, in_maps, core_ids=list(range(8)))
    x4 = np.stack([np.concatenate([np.asarray(res.results[2 * b]["x4_own"]),
                                   np.asarray(res.results[2 * b + 1]["x4_own"])], 0)
                   for b in range(4)], 0)
    return x4.astype(np.float32)


# version 16
# speedup vs baseline: 1.1737x; 1.1737x over previous
import math
import numpy as np
import ml_dtypes
import concourse.bass as bass
import concourse.mybir as mybir
from concourse.bass_utils import run_bass_kernel_spmd

F32 = mybir.dt.float32
BF16 = mybir.dt.bfloat16
I32 = mybir.dt.int32
U32 = mybir.dt.uint32
U8 = mybir.dt.uint8
AF = mybir.ActivationFunctionType
ALU = mybir.AluOpType
AX = mybir.AxisListType

BF = np.dtype(ml_dtypes.bfloat16)


COMPUTE_STREAMS = ("tensor", "vector", "scalar", "gpsimd", "sync")


class Tok:
    __slots__ = ("stream", "seq", "is_dma", "ninc")

    def __init__(self, stream, seq, is_dma=False, ninc=1):
        self.stream = stream
        self.seq = seq
        self.is_dma = is_dma
        self.ninc = ninc


class Prog:

    N_DMA_SEMS = 14

    def __init__(self, nc):
        self.nc = nc
        self.ops = {s: [] for s in COMPUTE_STREAMS}
        self.dma_slot_rr = {}
        self.auto_deps = []

    def barrier(self, toks):
        self.auto_deps = list(toks)

    def op(self, stream, fn, deps=()):
        lst = self.ops[stream]
        t = Tok(stream, len(lst))
        lst.append(dict(fn=fn, deps=list(deps) + self.auto_deps, marked=False,
                        dma=None, tok=t))
        return t

    def dma(self, stream, transfers, deps=(), cast=False, slow=False):
        lst = self.ops[stream]
        t = Tok(stream, len(lst), is_dma=True, ninc=len(transfers))
        rr = self.dma_slot_rr.get(stream, 0)
        slot = (stream, rr % self.N_DMA_SEMS)
        self.dma_slot_rr[stream] = rr + 1
        kw = {"allow_slow_non_contiguous": True} if slow else {}
        lst.append(dict(fn=None, transfers=list(transfers),
                        deps=list(deps) + self.auto_deps,
                        marked=False, dma=slot, tok=t, kw=kw))
        return t

    def idma(self, fn, deps=()):
        lst = self.ops["gpsimd"]
        t = Tok("gpsimd", len(lst), is_dma=True, ninc=1)
        rr = self.dma_slot_rr.get("gpsimd", 0)
        slot = ("gpsimd", rr % self.N_DMA_SEMS)
        self.dma_slot_rr["gpsimd"] = rr + 1
        lst.append(dict(fn=fn, transfers=None, deps=list(deps) + self.auto_deps,
                        marked=False, dma=slot, tok=t, kw={}))
        return t

    def cc_issue(self, fn, deps=()):
        lst = self.ops["gpsimd"]
        if not hasattr(self, "cc_count"):
            self.cc_count = 0
        self.cc_count += 1
        ta = Tok("gpsimd", len(lst))
        lst.append(dict(fn=fn, deps=list(deps) + self.auto_deps, marked=False,
                        dma=None, tok=ta, cc=self.cc_count))
        return ta

    def cc_wait(self, ta):
        lst = self.ops["gpsimd"]
        tb = Tok("gpsimd", len(lst))
        lst.append(dict(fn="relay", deps=[ta], marked=False, dma=None, tok=tb))
        return tb

    def cc(self, fn, deps=()):
        return self.cc_wait(self.cc_issue(fn, deps))

    def materialize(self, ctx, final_deps):
        nc = self.nc
        def mark(tok):
            self.ops[tok.stream][tok.seq]["marked"] = True
        for s in COMPUTE_STREAMS:
            for o in self.ops[s]:
                for d in o["deps"]:
                    mark(d)
        for d in final_deps:
            mark(d)
        engsem_val = {}
        for s in COMPUTE_STREAMS:
            cum = 0
            vals = []
            for o in self.ops[s]:
                if o["dma"] is None and o["marked"] and "cc" not in o:
                    cum += 1
                vals.append(cum)
            engsem_val[s] = vals
        from collections import defaultdict
        dma_cum = defaultdict(int)
        dma_before = {}
        dma_after = {}
        for s in COMPUTE_STREAMS:
            for o in self.ops[s]:
                if o["dma"] is not None:
                    slot = o["dma"]
                    dma_before[(s, o["tok"].seq)] = dma_cum[slot]
                    dma_cum[slot] += 16 * o["tok"].ninc
                    dma_after[(s, o["tok"].seq)] = dma_cum[slot]
        esem = {s: ctx.enter_context(nc.semaphore(f"es_{s}")) for s in COMPUTE_STREAMS}
        used_slots = sorted({o["dma"] for ss in COMPUTE_STREAMS for o in self.ops[ss]
                             if o["dma"] is not None})
        dsem = {sl: ctx.enter_context(nc.semaphore(f"ds_{sl[0]}_{sl[1]}")) for sl in used_slots}

        ccsem = [None]
        relay_tile = [None]
        if getattr(self, "cc_count", 0):
            ccsem[0] = ctx.enter_context(nc.semaphore("cc_sem"))
            relay_tile[0] = ctx.enter_context(
                nc.sbuf_tensor("cc_relay", [1, 4], mybir.dt.float32))

        def dep_wait_list(stream, deps, hw):
            best = {}
            for d in deps:
                o = self.ops[d.stream][d.seq]
                if o["dma"] is not None:
                    sem = dsem[o["dma"]]
                    val = dma_after[(d.stream, d.seq)]
                    key = ("d", o["dma"])
                elif "cc" in o:
                    assert stream == "gpsimd", "cc tokens only waitable by gpsimd"
                    sem = ccsem[0]
                    val = o["cc"]
                    key = ("c",)
                else:
                    if d.stream == stream == "tensor":
                        continue
                    sem = esem[d.stream]
                    val = engsem_val[d.stream][d.seq]
                    key = ("e", d.stream)
                if key not in best or best[key][1] < val:
                    best[key] = (sem, val)
            ws = []
            for key, (sem, val) in best.items():
                if hw.get(key, -1) < val:
                    hw[key] = val
                    ws.append((sem, val))
            return ws

        block = ctx.enter_context(nc.Block())

        def make_stream_fn(s):
            def emit(eng):
                hw = {}
                for o in self.ops[s]:
                    for (sem, val) in dep_wait_list(s, o["deps"], hw):
                        eng.wait_ge(sem, val)
                    if o["dma"] is not None:
                        slot = o["dma"]
                        before = dma_before[(s, o["tok"].seq)]
                        key = ("d", slot)
                        if before > 0 and hw.get(key, -1) < before:
                            eng.wait_ge(dsem[slot], before)
                            hw[key] = before
                        if o["transfers"] is not None:
                            for (oap, iap) in o["transfers"]:
                                eng.dma_start(out=oap, in_=iap,
                                              **o.get("kw", {})).then_inc(dsem[slot], 16)
                        else:
                            o["fn"](eng).then_inc(dsem[slot], 16)
                    elif "cc" in o:
                        o["fn"](eng).then_inc(ccsem[0])
                    elif o["fn"] == "relay":
                        ins = eng.memset(relay_tile[0][:], 0.0)
                        if o["marked"]:
                            ins.then_inc(esem[s], 1)
                    else:
                        ins = o["fn"](eng)
                        if o["marked"]:
                            ins.then_inc(esem[s], 1)
                if s == "sync":
                    for d in final_deps:
                        o = self.ops[d.stream][d.seq]
                        if o["dma"] is not None:
                            sem = dsem[o["dma"]]
                            val = dma_after[(d.stream, d.seq)]
                            key = ("dF", o["dma"])
                        else:
                            sem = esem[d.stream]
                            val = engsem_val[d.stream][d.seq]
                            key = ("eF", d.stream)
                        if hw.get(key, -1) < val:
                            hw[key] = val
                            eng.wait_ge(sem, val)
            return emit

        for s in COMPUTE_STREAMS:
            if self.ops[s] or s == "sync":
                getattr(block, s)(make_stream_fn(s))



def host_consts():
    c = {}
    half = 32
    inv = 1.0 / 10000.0 ** (np.arange(half) / half)
    ang = np.arange(4096)[None, :] * inv[:, None]
    c["rope_cos"] = np.cos(ang).astype(BF)
    c["rope_sin"] = np.sin(ang).astype(BF)
    c["identity"] = np.eye(128, dtype=np.float32)
    c["identity_bf"] = np.eye(128, dtype=np.float32).astype(BF)
    i = np.arange(128)[:, None]
    j = np.arange(128)[None, :]
    c["mask_tril"] = np.where(j <= i, 0.0, -1e30).astype(np.float32)
    c["mask_triu"] = np.where(j >= i, 0.0, -1e30).astype(np.float32)
    c["mask_zero"] = np.zeros((128, 128), np.float32)
    c["mask_ninf"] = np.full((128, 128), -1e30, np.float32)
    c["ones_col"] = np.ones((128, 1), np.float32)
    c["ones_row"] = np.ones((1, 128), np.float32)
    return c


def fft_consts():
    c = {}
    W64 = np.exp(-2j * np.pi * np.outer(np.arange(64), np.arange(64)) / 64)
    c["w64_re"] = W64.real.astype(BF)
    c["w64_im"] = W64.imag.astype(BF)
    WN_aq = np.exp(-2j * np.pi * np.outer(np.arange(128), np.arange(64)) / 8192)
    W128 = np.exp(-2j * np.pi * np.outer(np.arange(128), np.arange(128)) / 128)
    F = WN_aq[:, None, :].transpose(2, 0, 1) * W128[None]
    c["F_re"] = F.real.astype(BF)
    c["F_im"] = F.imag.astype(BF)
    c["F_imn"] = (-F.imag).astype(BF)
    W128i = np.exp(2j * np.pi * np.outer(np.arange(128), np.arange(128)) / 128)
    WNi_aq = np.exp(2j * np.pi * np.outer(np.arange(128), np.arange(64)) / 8192)
    G = W128i.T[:, :, None].transpose(2, 1, 0) * 0
    G = (W128i[None, :, :] * WNi_aq.T[:, :, None]).transpose(0, 2, 1)
    c["G_re"] = G.real.astype(BF)
    c["G_im"] = G.imag.astype(BF)
    c["G_imn"] = (-G.imag).astype(BF)
    W64i = np.exp(2j * np.pi * np.outer(np.arange(64), np.arange(64)) / 64)
    c["w64i_re"] = (W64i.real / 8192.0).astype(np.float32).astype(BF)
    c["w64i_imn"] = (-W64i.imag / 8192.0).astype(np.float32).astype(BF)
    L = 4096
    t = np.arange(L, dtype=np.float64) / (L - 1)
    angf = 2.0 * math.pi * np.arange(L, dtype=np.float64) / L
    phase = 0.0001 * angf
    c["pe_t"] = np.stack([t, np.cos(phase), -np.sin(phase)], 0).astype(np.float32)
    dmax, dmin = (math.log(0.01) / 0.3, math.log(0.01) / 1.5)
    deltas = np.abs(np.linspace(dmax, dmin, 1024)).astype(np.float64)
    c["deltas"] = deltas.astype(np.float32)[:, None]
    c["t_row"] = t.astype(np.float32)[None, :]
    return c



def ceil_div(a, b):
    return (a + b - 1) // b


class LB:

    def __init__(self, name, nsw=1):
        self.nc = bass.Bass("TRN2", target_bir_lowering=False, num_swdge_queues=nsw)
        self.p = Prog(self.nc)
        from contextlib import ExitStack
        self.ctx = ExitStack()
        self.cur = self.ctx
        self.consts = {}
        self.const_arrays = {}
        self.final = []
        self.tag = ""

    def scoped(self):
        from contextlib import ExitStack, contextmanager
        lb = self

        @contextmanager
        def _s():
            old = lb.cur
            old_sid = getattr(lb, "scope_id", 0)
            lb.scope_id = getattr(lb, "scope_seq", 0) + 1
            lb.scope_seq = lb.scope_id
            lb.cur = ExitStack()
            try:
                yield
            finally:
                lb.cur.close()
                lb.cur = old
                lb.scope_id = old_sid
        return _s()

    def dram_in(self, name, shape, dt):
        return self.nc.dram_tensor(name, list(shape), dt, kind="ExternalInput").ap()

    def dram_out(self, name, shape, dt):
        return self.nc.dram_tensor(name, list(shape), dt, kind="ExternalOutput").ap()

    def dram_scratch(self, name, shape, dt, shared=False):
        if shared:
            return self.nc.dram_tensor(self.tag + name, list(shape), dt, kind="Internal",
                                       addr_space="Shared").ap()
        return self.nc.dram_tensor(self.tag + name, list(shape), dt, kind="Internal").ap()

    def sb(self, name, shape, dt):
        sid = getattr(self, "scope_id", 0)
        return self.cur.enter_context(self.nc.sbuf_tensor(
            f"{self.tag}s{sid}_{name}", list(shape), dt))

    def ps(self, name, shape, dt=F32):
        sid = getattr(self, "scope_id", 0)
        return self.cur.enter_context(self.nc.psum_tensor(
            f"{self.tag}s{sid}_{name}", list(shape), dt))

    def const(self, name, arr, dt=None, queue="sync"):
        sid = getattr(self, "scope_id", 0)
        key = (name, sid)
        if key in self.consts:
            return self.consts[key]
        arr = np.asarray(arr)
        if dt is None:
            dt = {np.dtype(np.float32): F32, BF: BF16, np.dtype(np.int32): I32}[arr.dtype]
        dname = "c_" + name
        if dname in self.const_arrays:
            d = self._const_d[dname]
        else:
            d = self.dram_in(dname, arr.shape, dt)
            self.const_arrays[dname] = arr
            if not hasattr(self, "_const_d"):
                self._const_d = {}
            self._const_d[dname] = d
        shape = list(arr.shape)
        t = self.cur.enter_context(self.nc.sbuf_tensor(f"cs{sid}_{name}", shape, dt))
        tok = self.p.dma(queue, [(t[:], d)])
        self.consts[key] = (t, tok)
        return t, tok

    def finish(self, toks):
        self.final.extend(toks)

    def materialize(self):
        self.p.materialize(self.ctx, self.final)
        self.ctx.close()
        return self.nc


def bcast_row(lb, row_ap, row_tok, psum, sbuf_out, n, deps=()):
    ones, otok = lb.const("ones_row", np.ones((1, 128), np.float32))
    t1 = lb.p.op("tensor", lambda e: e.matmul(psum[:, 0:n], ones[:], row_ap, start=True, stop=True),
                 deps=[otok, row_tok, *deps])
    t2 = lb.p.op("vector", lambda e: e.tensor_copy(sbuf_out[:, 0:n], psum[:, 0:n]), deps=[t1])
    return t2




def emit_ln_transpose(lb, x_dram, xnT, nt=32, out_bf=True):
    p = lb.p
    identbf, tok_id = lb.const("identity_bf", np.eye(128, dtype=np.float32).astype(BF))
    epsc, tok_eps = lb.const("eps_col", np.full((128, 1), 1e-5, np.float32))
    xtile = [lb.sb(f"lnx{i}", [128, 1024], F32) for i in range(2)]
    xcb = lb.sb("lnxcb", [128, 1024], BF16)
    xc = lb.sb("lnxc", [128, 1024], F32)
    sq = lb.sb("lnsq", [128, 1024], F32)
    st = lb.sb("lnst", [128, 4], F32)
    psb = [lb.ps(f"lnps{i}", [128, 128], BF16) for i in range(2)]
    done = []
    prev_tile_last = [None, None]
    prev_stats = None
    prev_evac_last = None
    prev_tr = {}
    for tb in range(nt):
        bi = tb % 2
        dl = [prev_tile_last[bi]] if prev_tile_last[bi] is not None else []
        ld = p.dma("sync", [(xtile[bi][:], x_dram[tb * 128:(tb + 1) * 128, :])], deps=dl)
        dst = [prev_stats] if prev_stats is not None else []
        t1 = p.op("vector", (lambda _x: lambda e: e.reduce_sum(st[:, 0:1], _x[:], AX.X))(xtile[bi]),
                  deps=[ld] + dst)
        t2 = p.op("vector", lambda e: e.tensor_scalar_mul(st[:, 0:1], st[:, 0:1], -1.0 / 1024.0), deps=[t1])
        t3 = p.op("vector", (lambda _x: lambda e: e.tensor_scalar_add(xc[:], _x[:], st[:, 0:1]))(xtile[bi]),
                  deps=[t2])
        t4 = p.op("scalar", lambda e: e.activation(sq[:], xc[:], AF.Square, accum_out=st[:, 1:2]), deps=[t3])
        t5 = p.op("scalar", lambda e: e.activation(st[:, 2:3], st[:, 1:2], AF.Sqrt,
                                                   scale=1.0 / 1024.0, bias=epsc[:, 0:1]),
                  deps=[t4, tok_eps])
        t6 = p.op("vector", lambda e: e.reciprocal(st[:, 3:4], st[:, 2:3]), deps=[t5])
        dxcb = [prev_evac_last] if prev_evac_last is not None else []
        t7 = p.op("vector", lambda e: e.tensor_scalar_mul(xcb[:], xc[:], st[:, 3:4]),
                  deps=[t6] + dxcb)
        prev_stats = t7
        evs = []
        for dc in range(8):
            pb = psb[dc % 2]
            dep_war = [prev_tr[dc % 2]] if (dc % 2) in prev_tr else []
            tr = p.op("tensor", (lambda _pb, _dc: lambda e: e.transpose(
                _pb[:], xcb[:, _dc * 128:(_dc + 1) * 128], identbf[:]))(pb, dc),
                deps=[t7, tok_id] + dep_war)
            ev = p.op("vector", (lambda _pb, _dc, _tb: lambda e: e.tensor_copy(
                xnT[:, _dc, _tb * 128:(_tb + 1) * 128], _pb[:]))(pb, dc, tb), deps=[tr])
            prev_tr[dc % 2] = ev
            evs.append(ev)
        prev_tile_last[bi] = t3
        prev_evac_last = evs[-1]
        done.append(evs[-1])
    return done


def emit_rope(lb, qT, nheads2, T, base_toks, tag=""):
    p = lb.p
    hc = host_consts()
    C64, tc = lb.const("rope_c64", np.concatenate([hc["rope_cos"], hc["rope_cos"]], 0))
    S64, ts_ = lb.const("rope_s64", np.concatenate([-hc["rope_sin"], hc["rope_sin"]], 0))
    key = lb.tag + "ropescratch" + str(nheads2) + "_" + str(T)
    if key not in lb.consts:
        lb.consts[key] = (lb.sb("ropeA" + key, [64, nheads2, T], BF16),)
    (tmpA,) = lb.consts[key]

    def bc(tbl):
        a = tbl[0:64, 0:T]
        return bass.AP(a.tensor, a.offset, [a.ap[0], [0, nheads2], a.ap[1]])

    t1 = p.op("vector", lambda e: e.tensor_copy(tmpA[0:32, :, :], qT[32:64, :, :]), deps=base_toks)
    t2 = p.op("vector", lambda e: e.tensor_copy(tmpA[32:64, :, :], qT[0:32, :, :]), deps=base_toks)
    t3 = p.op("vector", lambda e: e.tensor_mul(tmpA[:], tmpA[:], bc(S64)), deps=[t1, t2, ts_])
    t4 = p.op("vector", lambda e: e.tensor_mul(qT[:, :, :], qT[:, :, :], bc(C64)), deps=[t1, t2, tc])
    t5 = p.op("vector", lambda e: e.tensor_add(qT[:, :, :], qT[:, :, :], tmpA[:]), deps=[t3, t4])
    return [t5]


def emit_proj_fm(lb, wtile, wtok, xnT, xn_toks, outT, h2, tchunks, psum, kc_n=8,
                 guards=()):
    p = lb.p
    toks = []
    prev_ev = list(guards)
    for tc in range(tchunks):
        mms = []
        for kc in range(kc_n):
            t = p.op("tensor", (lambda _kc, _tc: lambda e: e.matmul(
                psum[:], wtile[:, _kc, :], xnT[:, _kc, _tc * 512:(_tc + 1) * 512],
                start=(_kc == 0), stop=(_kc == kc_n - 1)))(kc, tc),
                deps=([wtok] + xn_toks + prev_ev) if kc == 0 else [])
            mms.append(t)
        e1 = p.op("vector", (lambda _tc: lambda e: e.tensor_copy(
            outT[:, 0, _tc * 512:(_tc + 1) * 512], psum[0:64, :]))(tc), deps=[mms[-1]])
        e2 = p.op("vector", (lambda _tc: lambda e: e.tensor_copy(
            outT[:, 1, _tc * 512:(_tc + 1) * 512], psum[64:128, :]))(tc), deps=[mms[-1]])
        prev_ev = [e1, e2]
        toks += [e1, e2]
    return toks


def emit_wo_partial(lb, E, att_dram, att_done, nheads=8):
    p = lb.p
    wo_t = [lb.sb(f"wo_t{i}", [64, nheads, 128], BF16) for i in range(2)]
    attb = [lb.sb(f"woatt{i}", [64, nheads, 512], BF16) for i in range(2)]
    wev = [lb.sb(f"woev{i}", [128, 512], BF16) for i in range(2)]
    psW = lb.ps("psWo", [128, 512], F32)
    rs_in = E["rs_in"]
    toks = []
    prev_ev = {}
    prev_lda = {}
    prev_ldw = {}
    prev_mm = []
    for tc in range(8):
        ai = tc % 2
        dla = [prev_lda[ai]] if ai in prev_lda else []
        lda = p.dma("sync", [(attb[ai][:], att_dram[:, :, tc * 512:(tc + 1) * 512])],
                    deps=dla + att_done)
        last_u = None
        for oc in range(8):
            i = (oc + tc * 8) % 2
            dlw = [prev_ldw[i]] if i in prev_ldw else []
            ldw = p.dma("gpsimd", [(wo_t[i][:], E["wo"][:, :, oc * 128:(oc + 1) * 128])],
                        deps=dlw)
            mms = []
            for h in range(nheads):
                t = p.op("tensor", (lambda _h, _i, _ai: lambda e: e.matmul(
                    psW[:], wo_t[_i][:, _h, :],
                    attb[_ai][:, _h, :],
                    start=(_h == 0), stop=(_h == nheads - 1)))(h, i, ai),
                    deps=([ldw, lda] + prev_mm) if h == 0 else [])
                mms.append(t)
            prev_ldw[i] = mms[-1]
            dl = [prev_ev[i]] if i in prev_ev else []
            ev = p.op("vector", (lambda _i: lambda e: e.tensor_copy(
                wev[_i][:], psW[:]))(i), deps=[mms[-1]] + dl)
            prev_mm = [ev]
            last_u = ev
            hh = tc // 4
            st = p.dma("sync", [(rs_in[hh * 128:(hh + 1) * 128,
                                       oc * 2048 + (tc % 4) * 512:
                                       oc * 2048 + (tc % 4) * 512 + 512], wev[i][:])],
                       deps=[ev])
            prev_ev[i] = st
            toks.append(st)
        prev_lda[ai] = last_u
    return toks


def emit_l0a(lb, E):
    nc, p = lb.nc, lb.p
    wq, wk, wv = E["wq"], E["wk"], E["wv"]
    xnT, xn_toks = E["xnT"], E["xn_toks"]

    hc = host_consts()
    identbf, tok_id = lb.const("identity_bf", np.eye(128, dtype=np.float32).astype(BF))
    m512 = np.concatenate([hc["mask_triu"], np.zeros((128, 384), np.float32)], 1)
    mk512, tok_m512 = lb.const("m512", m512)
    mtril, tok_tril = lb.const("mtril", hc["mask_tril"])

    att2 = lb.sb("att2", [64, 2, 4096], BF16)
    qT = lb.sb("qT", [64, 2, 4096], BF16)
    kT = lb.sb("kT", [64, 2, 4096], BF16)
    vtok = lb.sb("vtok", [128, 32, 128], BF16)
    wq_t = lb.sb("wq_t", [128, 8, 128], BF16)
    wk_t = lb.sb("wk_t", [128, 8, 128], BF16)
    wv_t = lb.sb("wv_t", [128, 8, 128], BF16)
    Ssb = lb.sb("Ssb", [128, 640], F32)
    Pbf = lb.sb("Pbf", [128, 640], BF16)
    PT2 = [lb.sb(f"PT{i}", [128, 128], BF16) for i in range(2)]
    red = lb.sb("red", [128, 4], F32)
    psA = lb.ps("psA", [128, 512], F32)
    psB = lb.ps("psB", [128, 128], F32)
    psV = lb.ps("psV", [128, 128], F32)
    psT2 = [lb.ps(f"psT{i}", [128, 128], BF16) for i in range(2)]
    psO = lb.ps("psO", [64, 128], F32)

    pair_guard = []
    att_done_all = []
    for hp in range(4):
        csl = slice(hp * 128, (hp + 1) * 128)
        ldw = p.dma("gpsimd", [
            (wq_t[:], wq[:, csl].rearrange("(c p) n -> p c n", p=128)),
            (wk_t[:], wk[:, csl].rearrange("(c p) n -> p c n", p=128)),
            (wv_t[:], wv[:, csl].rearrange("(c p) n -> p c n", p=128)),
        ], deps=pair_guard)
        qtoks = emit_proj_fm(lb, wq_t, ldw, xnT, xn_toks, qT, 2, 8, psA, guards=pair_guard)
        ktoks = emit_proj_fm(lb, wk_t, ldw, xnT, xn_toks, kT, 2, 8, psA, guards=qtoks[-1:])
        vtoks = []
        prev_v = ktoks[-1:]
        for tb in range(32):
            mms = []
            for kc in range(8):
                t = p.op("tensor", (lambda _kc, _tb: lambda e: e.matmul(
                    psV[:], xnT[:, _kc, _tb * 128:(_tb + 1) * 128], wv_t[:, _kc, :],
                    start=(_kc == 0), stop=(_kc == 7)))(kc, tb),
                    deps=([ldw] + prev_v) if kc == 0 else [])
                mms.append(t)
            ev = p.op("vector", (lambda _tb: lambda e: e.tensor_copy(
                vtok[:, _tb, :], psV[:]))(tb), deps=[mms[-1]])
            prev_v = [ev]
            vtoks.append(ev)
        rq = emit_rope(lb, qT, 2, 4096, qtoks)
        rk = emit_rope(lb, kT, 2, 4096, ktoks + rq)
        prev_unit = []
        for h2 in range(2):
            for qb in range(32):
                I = qb
                Jlo = max(0, I - 4)
                nfull = I - Jlo
                nk = (nfull + 1) * 128
                deps0 = rq + rk + [vtoks[-1]] + prev_unit
                t_mm1 = None
                if nfull > 0:
                    t_mm1 = p.op("tensor", (lambda _h, _qb, _J, _n: lambda e: e.matmul(
                        psA[:, 0:_n * 128], qT[:, _h, _qb * 128:(_qb + 1) * 128],
                        kT[:, _h, _J * 128:(_J + _n) * 128], start=True, stop=True))(h2, qb, Jlo, nfull),
                        deps=deps0)
                t_mm2 = p.op("tensor", (lambda _h, _qb: lambda e: e.matmul(
                    psB[:], qT[:, _h, _qb * 128:(_qb + 1) * 128],
                    kT[:, _h, _qb * 128:(_qb + 1) * 128], start=True, stop=True))(h2, qb),
                    deps=deps0)
                evs = []
                if nfull > 0:
                    if I >= 4:
                        ev1 = p.op("vector", (lambda _n: lambda e: e.tensor_add(
                            Ssb[:, 0:_n * 128], psA[:, 0:_n * 128], mk512[:, 0:_n * 128]))(nfull),
                            deps=[t_mm1, tok_m512])
                    else:
                        ev1 = p.op("vector", (lambda _n: lambda e: e.tensor_copy(
                            Ssb[:, 0:_n * 128], psA[:, 0:_n * 128]))(nfull), deps=[t_mm1])
                    evs.append(ev1)
                ev2 = p.op("vector", (lambda _n: lambda e: e.tensor_add(
                    Ssb[:, _n * 128:_n * 128 + 128], psB[:], mtril[:]))(nfull),
                    deps=[t_mm2, tok_tril])
                evs.append(ev2)
                r1 = p.op("vector", (lambda _nk: lambda e: e.reduce_max(
                    red[:, 0:1], Ssb[:, 0:_nk], AX.X, negate=True))(nk), deps=evs)
                r2 = p.op("vector", lambda e: e.tensor_scalar_mul(red[:, 0:1], red[:, 0:1], 0.125),
                          deps=[r1])
                ex = p.op("scalar", (lambda _nk: lambda e: e.activation(
                    Pbf[:, 0:_nk], Ssb[:, 0:_nk], AF.Exp, bias=red[:, 0:1], scale=0.125,
                    accum_out=red[:, 1:2]))(nk), deps=[r2])
                rc = p.op("vector", lambda e: e.reciprocal(red[:, 2:3], red[:, 1:2]), deps=[ex])
                sc = p.op("scalar", (lambda _nk: lambda e: e.activation(
                    Pbf[:, 0:_nk], Pbf[:, 0:_nk], AF.Copy, scale=red[:, 2:3]))(nk),
                    deps=[rc])
                pv_mms = []
                prev_pe = {}
                for jj in range(nfull + 1):
                    J = Jlo + jj
                    bi = jj % 2
                    war = [prev_pe[bi]] if bi in prev_pe else []
                    tr = p.op("tensor", (lambda _jj, _bi: lambda e: e.transpose(
                        psT2[_bi][:], Pbf[:, _jj * 128:(_jj + 1) * 128], identbf[:]))(jj, bi),
                        deps=[sc, tok_id] + war)
                    pe = p.op("vector", (lambda _bi: lambda e: e.tensor_copy(
                        PT2[_bi][:], psT2[_bi][:]))(bi),
                        deps=[tr] + pv_mms[-2:-1])
                    prev_pe[bi] = pe
                    mv = p.op("tensor", (lambda _J, _jj, _h, _last, _bi: lambda e: e.matmul(
                        psO[:], vtok[:, _J, _h * 64:(_h + 1) * 64], PT2[_bi][:],
                        start=(_jj == 0), stop=_last))(J, jj, h2, jj == nfull, bi),
                        deps=[pe] + (prev_unit if jj == 0 else []))
                    pv_mms.append(mv)
                ao = p.op("vector", (lambda _h, _qb: lambda e: e.tensor_copy(
                    att2[:, _h, _qb * 128:(_qb + 1) * 128], psO[:]))(h2, qb),
                    deps=[pv_mms[-1]])
                prev_unit = [ao]
        sta = p.dma("sync", [(E["att_dram"][:, hp * 2 + h2, :], att2[:, h2, :])
                             for h2 in range(2)], deps=prev_unit)
        pair_guard = [sta]
        att_done_all.append(sta)
    return emit_wo_partial(lb, E, E["att_dram"], att_done_all, nheads=8)

def emit_dequant(lb, q_dram, s_dram, out_dram, K, N, tag, psb=None, guard=()):
    p = lb.p
    ones_row, tok_or = lb.const("ones_row_f32", np.ones((1, 128), np.float32))
    G = K // 128
    CC = N // 512
    qtile = [lb.sb(f"dq_q{tag}{i}", [128, 512], F32) for i in range(2)]
    scb = [lb.sb(f"dq_s{tag}{i}", [128, 512], F32) for i in range(2)]
    wdq = [lb.sb(f"dq_w{tag}{i}", [128, 512], BF16) for i in range(2)]
    srow = [lb.sb(f"dq_r{tag}{i}", [1, N], F32) for i in range(2)]
    if psb is None:
        psb = lb.ps(f"dq_ps{tag}", [128, 512], F32)
    toks = []
    prev = {}
    prev_row = {}
    pv_ev = list(guard)[-1] if guard else None
    for g in range(G):
        gi = g % 2
        rl = [prev_row[gi]] if gi in prev_row else []
        ldr = p.dma("sync", [(srow[gi][:], s_dram[g:g + 1, :])], deps=rl)
        last_mm = None
        for cc in range(CC):
            i = (g * CC + cc) % 2
            dl = [prev[i]] if i in prev else []
            ld = p.dma("gpsimd", [(qtile[i][:], q_dram[g * 128:(g + 1) * 128,
                                                        cc * 512:(cc + 1) * 512])], deps=dl)
            mmdeps = [ldr, tok_or] + ([pv_ev] if pv_ev is not None else [])
            mm = p.op("tensor", (lambda _gi, _cc: lambda e: e.matmul(
                psb[:], ones_row[:], srow[_gi][0:1, _cc * 512:(_cc + 1) * 512],
                start=True, stop=True))(gi, cc), deps=mmdeps)
            last_mm = mm
            ev = p.op("vector", (lambda _i: lambda e: e.tensor_copy(scb[_i][:], psb[:]))(i),
                      deps=[mm] + dl)
            pv_ev = ev
            t1 = p.op("vector", (lambda _i: lambda e: e.tensor_scalar_add(
                qtile[_i][:], qtile[_i][:], -128.0))(i), deps=[ld] + dl)
            t2 = p.op("vector", (lambda _i: lambda e: e.tensor_mul(
                wdq[_i][:], qtile[_i][:], scb[_i][:]))(i), deps=[t1, ev])
            st = p.dma("sync", [(out_dram[g * 128:(g + 1) * 128, cc * 512:(cc + 1) * 512],
                                 wdq[i][:])], deps=[t2])
            prev[i] = st
            toks.append(st)
        prev_row[gi] = last_mm
    return toks


def emit_l0b(lb, E):
    nc, p = lb.nc, lb.p
    x = E["x"]
    g_d, bt_d = E["g_r"], E["b_r"]
    q1_d, s1_d, b1_d = E["fc1_q"], E["fc1_s"], E["fc1_b"]
    q2_d, s2_d, b2_d = E["fc2_q"], E["fc2_s"], E["fc2_b"]
    o_x1 = E["x_out"]
    xnT, ld_xn_l = E["xnT"], E["xn_toks"]
    w1dq = lb.dram_scratch("w1dq", [1024, 4096], BF16)
    w2dq = lb.dram_scratch("w2dq", [4096, 1024], BF16)

    identbf, tok_id = lb.const("identity_bf", np.eye(128, dtype=np.float32).astype(BF))
    epsc, tok_eps = lb.const("eps_col", np.full((128, 1), 1e-5, np.float32))
    onesb, tok_ob = lb.const("ones_col_bf", np.ones((128, 1), np.float32).astype(BF))
    onesf, tok_of = lb.const("ones_col_f32", np.ones((128, 1), np.float32))

    aoutT = lb.sb("aoutT", [128, 8, 2048], BF16)
    gsb = lb.sb("gsb", [128, 8], F32)
    btsb = lb.sb("btsb", [128, 8], F32)
    b1sb = lb.sb("b1sb", [128, 32], F32)
    b2sb = lb.sb("b2sb", [128, 8], F32)
    b2h = lb.sb("b2h", [128, 8], F32)

    ld_small = p.dma("sync", [(gsb[:], g_d), (btsb[:], bt_d), (b1sb[:], b1_d),
                              (b2sb[:], b2_d)])
    t_b2h = p.op("vector", lambda e: e.tensor_scalar_mul(b2h[:], b2sb[:], 0.5), deps=[ld_small])
    ld_ao = p.dma("sync", [(aoutT[:], E["rs_out"].rearrange("p (h t) -> p h t", h=8))],
                  deps=E["rs_tok"])
    ao_toks = [ld_ao]
    ld_xn = p.op("vector", lambda e: e.engine_nop(), deps=ld_xn_l)

    psW = lb.ps("psW", [128, 512], F32)
    psR = lb.ps("psR", [1, 512], F32)
    psR2 = lb.ps("psR2", [1, 512], F32)
    psB = lb.ps("psB", [128, 512], F32)
    psY = lb.ps("psY", [128, 512], F32)
    psZ = lb.ps("psZ", [128, 512], F32)
    psbf = [lb.ps(f"psbf{i}", [128, 128], BF16) for i in range(2)]

    with lb.scoped():
        dq1 = emit_dequant(lb, q1_d, s1_d, w1dq, 1024, 4096, "1", psb=psW,
                           guard=ao_toks[-1:])
        dq2 = emit_dequant(lb, q2_d, s2_d, w2dq, 4096, 1024, "2", psb=psW,
                           guard=dq1[-1:])
    dq_done = p.op("vector", lambda e: e.engine_nop(), deps=dq1 + dq2)
    dq1 = [dq_done]
    dq2 = [dq_done]

    with lb.scoped():
        xn2 = lb.sb("xn2", [128, 8, 512], BF16)
        sqc = lb.sb("sqc", [128, 512], F32)
        rowst = lb.sb("rowst", [1, 8, 512], F32)
        mub = lb.sb("mub", [128, 512], BF16)
        rstdb = lb.sb("rstdb", [128, 512], BF16)
        w1t = [lb.sb(f"w1t{i}", [128, 8, 128], BF16) for i in range(2)]
        w2t = [lb.sb(f"w2t{i}", [128, 32, 128], BF16) for i in range(2)]
        ysb = lb.sb("ysb", [128, 32, 512], BF16)
        ztmp = lb.sb("ztmp", [128, 512], BF16)
        ones_row, tok_or = lb.const("ones_row_f32", np.ones((1, 128), np.float32))

        ffn_toks = []
        prev_w1 = {}
        prev_w2 = {}
        prev_tc = list(dq1[-1:] + dq2[-1:])
        for tc in range(4):
            tsl = slice(tc * 512, (tc + 1) * 512)
            mmsM = []
            for dc in range(8):
                t = p.op("tensor", (lambda _dc, _tc: lambda e: e.matmul(
                    psR[:], onesb[:], xnT[:, _dc, _tc * 512:(_tc + 1) * 512],
                    start=(_dc == 0), stop=(_dc == 7)))(dc, tc),
                    deps=([ld_xn, tok_ob] + prev_tc + dq1[-1:] + dq2[-1:]) if dc == 0 else [])
                mmsM.append(t)
            evM = p.op("vector", lambda e: e.tensor_scalar_mul(rowst[0:1, 0, :], psR[:], 1.0 / 1024.0),
                       deps=[mmsM[-1]] + dq1[-1:] + dq2[-1:])
            sq_toks = []
            for dc in range(8):
                s1 = p.op("scalar", (lambda _dc, _tc: lambda e: e.activation(
                    sqc[:], xnT[:, _dc, _tc * 512:(_tc + 1) * 512], AF.Square))(dc, tc),
                    deps=[ld_xn] + prev_tc + sq_toks[-1:])
                s2 = p.op("tensor", (lambda _dc: lambda e: e.matmul(
                    psR2[:], onesf[:], sqc[:], start=(_dc == 0), stop=(_dc == 7)))(dc),
                    deps=[s1, tok_of])
                sq_toks.append(s2)
            evS = p.op("vector", lambda e: e.tensor_scalar_mul(rowst[0:1, 1, :], psR2[:], 1.0 / 1024.0),
                       deps=[sq_toks[-1]])
            v1 = p.op("vector", lambda e: e.tensor_mul(rowst[0:1, 2, :], rowst[0:1, 0, :],
                                                       rowst[0:1, 0, :]), deps=[evM])
            v2 = p.op("vector", lambda e: e.tensor_sub(rowst[0:1, 2, :], rowst[0:1, 1, :],
                                                       rowst[0:1, 2, :]), deps=[v1, evS])
            v3 = p.op("scalar", lambda e: e.activation(rowst[0:1, 3, :], rowst[0:1, 2, :], AF.Sqrt,
                                                       bias=epsc[0:1, 0:1]), deps=[v2, tok_eps])
            v4 = p.op("vector", lambda e: e.reciprocal(rowst[0:1, 2, :], rowst[0:1, 3, :]), deps=[v3])
            bm = p.op("tensor", lambda e: e.matmul(psB[:], ones_row[:], rowst[0:1, 0, :],
                                                   start=True, stop=True), deps=[evM, tok_or, v1])
            bme = p.op("vector", lambda e: e.tensor_copy(mub[:], psB[:]), deps=[bm])
            br = p.op("tensor", lambda e: e.matmul(psB[:], ones_row[:], rowst[0:1, 2, :],
                                                   start=True, stop=True), deps=[v4, bme])
            bre = p.op("vector", lambda e: e.tensor_copy(rstdb[:], psB[:]), deps=[br])
            xn2_toks = []
            for dc in range(8):
                n1 = p.op("vector", (lambda _dc, _tc: lambda e: e.tensor_sub(
                    xn2[:, _dc, :], xnT[:, _dc, _tc * 512:(_tc + 1) * 512], mub[:]))(dc, tc),
                    deps=[bme] + prev_tc)
                n2 = p.op("vector", (lambda _dc: lambda e: e.tensor_mul(
                    xn2[:, _dc, :], xn2[:, _dc, :], rstdb[:]))(dc), deps=[n1, bre])
                n3 = p.op("vector", (lambda _dc: lambda e: e.tensor_scalar(
                    xn2[:, _dc, :], xn2[:, _dc, :], gsb[:, _dc:_dc + 1], btsb[:, _dc:_dc + 1],
                    ALU.mult, ALU.add))(dc), deps=[n2, ld_small])
                xn2_toks.append(n3)
            y_toks = []
            for oc in range(32):
                i = oc % 2
                wl = [prev_w1[i]] if i in prev_w1 else []
                ldw = p.dma("gpsimd", [(w1t[i][:], w1dq[:, oc * 128:(oc + 1) * 128]
                                        .rearrange("(c p) n -> p c n", p=128))],
                            deps=wl + (dq1 if (oc < 2 and tc == 0) else []))
                mms = []
                for kc in range(8):
                    t = p.op("tensor", (lambda _kc, _i: lambda e: e.matmul(
                        psY[:], w1t[_i][:, _kc, :], xn2[:, _kc, :],
                        start=(_kc == 0), stop=(_kc == 7)))(kc, i),
                        deps=([ldw] + xn2_toks + y_toks[-1:]) if kc == 0 else [])
                    mms.append(t)
                ya = p.op("scalar", (lambda _oc: lambda e: e.activation(
                    ysb[:, _oc, :], psY[:], AF.Silu, bias=b1sb[:, _oc:_oc + 1]))(oc),
                    deps=[mms[-1], ld_small])
                prev_w1[i] = ya
                y_toks.append(ya)
            for zc in range(8):
                i = zc % 2
                wl = [prev_w2[i]] if i in prev_w2 else []
                ldw = p.dma("gpsimd", [(w2t[i][:], w2dq[:, zc * 128:(zc + 1) * 128]
                                        .rearrange("(c p) n -> p c n", p=128))],
                            deps=wl + (dq2 if (zc < 2 and tc == 0) else []))
                mms = []
                for kc in range(32):
                    t = p.op("tensor", (lambda _kc, _i: lambda e: e.matmul(
                        psZ[:], w2t[_i][:, _kc, :], ysb[:, _kc, :],
                        start=(_kc == 0), stop=(_kc == 31)))(kc, i),
                        deps=([ldw] + y_toks + ffn_toks[-1:]) if kc == 0 else [])
                    mms.append(t)
                z1 = p.op("vector", (lambda _zc: lambda e: e.tensor_scalar(
                    ztmp[:], psZ[:], 0.5, b2h[:, _zc:_zc + 1], ALU.mult, ALU.add))(zc),
                    deps=[mms[-1], t_b2h])
                z2 = p.op("vector", (lambda _zc, _tc: lambda e: e.tensor_add(
                    ztmp[:], ztmp[:],
                    xnT[:, _zc, _tc * 512:(_tc + 1) * 512]))(zc, tc), deps=[z1, ld_xn])
                z3 = p.op("vector", (lambda _zc, _tc: lambda e: e.tensor_add(
                    aoutT[:, _zc, _tc * 512:(_tc + 1) * 512],
                    aoutT[:, _zc, _tc * 512:(_tc + 1) * 512], ztmp[:]))(zc, tc),
                    deps=[z2] + ao_toks)
                prev_w2[i] = z3
                ffn_toks.append(z3)
            prev_tc = [ffn_toks[-1]]

        ffn_final = ffn_toks[-1:]

    sum_toks = ffn_final
    xtile = [lb.sb(f"rx{i}", [128, 1024], F32) for i in range(2)]
    otile = [lb.sb(f"ro{i}", [128, 1024], F32) for i in range(2)]
    ttmp = lb.sb("ttmp", [128, 128], F32)
    st_toks = []
    prev_st = {}
    prev_trr = {}
    prev_tt = None
    for tb in range(16):
        i = tb % 2
        dl = [prev_st[i]] if i in prev_st else []
        ld = p.dma("sync", [(xtile[i][:], x[tb * 128:(tb + 1) * 128, :])],
                   deps=dl + (ffn_final if tb < 2 else []))
        adds = []
        for dc in range(8):
            pb = psbf[dc % 2]
            war = [prev_trr[dc % 2]] if (dc % 2) in prev_trr else []
            tr = p.op("tensor", (lambda _dc, _tb, _pb: lambda e: e.transpose(
                _pb[:], aoutT[:, _dc, _tb * 128:(_tb + 1) * 128], identbf[:]))(dc, tb, pb),
                deps=sum_toks + [tok_id] + war)
            cp = p.op("vector", (lambda _pb: lambda e: e.tensor_copy(ttmp[:], _pb[:]))(pb),
                      deps=[tr] + ([prev_tt] if prev_tt is not None else []))
            ad = p.op("vector", (lambda _dc, _i: lambda e: e.tensor_add(
                otile[_i][:, _dc * 128:(_dc + 1) * 128],
                xtile[_i][:, _dc * 128:(_dc + 1) * 128], ttmp[:]))(dc, i),
                deps=[cp, ld] + dl)
            prev_trr[dc % 2] = cp
            prev_tt = ad
            adds.append(ad)
        st = p.dma("sync", [(o_x1[tb * 128:(tb + 1) * 128, :], otile[i][:])], deps=adds)
        prev_st[i] = st
        st_toks.append(st)
    return st_toks

def build_l1a():
    raise NotImplementedError


def emit_l1a(lb, E):
    nc, p = lb.nc, lb.p
    wq, wk, wv = E["wq"], E["wk"], E["wv"]
    xnT, xn_toks = E["xnT"], E["xn_toks"]

    hc = host_consts()
    identbf, tok_id = lb.const("identity_bf", np.eye(128, dtype=np.float32).astype(BF))
    mtril, tok_tril = lb.const("mtril", hc["mask_tril"])

    att2 = lb.sb("att2", [64, 2, 4096], BF16)
    qT = lb.sb("qT", [64, 2, 4096], BF16)
    kT = lb.sb("kT", [64, 2, 4096], BF16)
    vtok = lb.sb("vtok", [128, 32, 128], BF16)
    wq_t = lb.sb("wq_t", [128, 8, 128], BF16)
    wk_t = lb.sb("wk_t", [128, 8, 128], BF16)
    wv_t = lb.sb("wv_t", [128, 8, 128], BF16)
    Ssb = lb.sb("Ssb", [128, 4096], F32)
    Pbf = lb.sb("Pbf", [128, 4096], BF16)
    PT2 = [lb.sb(f"PT{i}", [128, 128], BF16) for i in range(2)]
    red = lb.sb("red", [128, 4], F32)
    psA = lb.ps("psA", [128, 512], F32)
    psA2 = lb.ps("psA2", [128, 512], F32)
    psB = lb.ps("psB", [128, 128], F32)
    psV = lb.ps("psV", [128, 128], F32)
    psT2 = [lb.ps(f"psT{i}", [128, 128], BF16) for i in range(2)]
    psO = lb.ps("psO", [64, 128], F32)

    pair_guard = []
    att_done_all = []
    for hp in range(4):
        csl = slice(hp * 128, (hp + 1) * 128)
        ldw = p.dma("gpsimd", [
            (wq_t[:], wq[:, csl].rearrange("(c p) n -> p c n", p=128)),
            (wk_t[:], wk[:, csl].rearrange("(c p) n -> p c n", p=128)),
            (wv_t[:], wv[:, csl].rearrange("(c p) n -> p c n", p=128)),
        ], deps=pair_guard)
        qtoks = emit_proj_fm(lb, wq_t, ldw, xnT, xn_toks, qT, 2, 8, psA, guards=pair_guard)
        ktoks = emit_proj_fm(lb, wk_t, ldw, xnT, xn_toks, kT, 2, 8, psA, guards=qtoks[-1:])
        vtoks = []
        prev_v = ktoks[-1:]
        for tb in range(32):
            mms = []
            for kc in range(8):
                t = p.op("tensor", (lambda _kc, _tb: lambda e: e.matmul(
                    psV[:], xnT[:, _kc, _tb * 128:(_tb + 1) * 128], wv_t[:, _kc, :],
                    start=(_kc == 0), stop=(_kc == 7)))(kc, tb),
                    deps=([ldw] + prev_v) if kc == 0 else [])
                mms.append(t)
            ev = p.op("vector", (lambda _tb: lambda e: e.tensor_copy(
                vtok[:, _tb, :], psV[:]))(tb), deps=[mms[-1]])
            prev_v = [ev]
            vtoks.append(ev)
        rq = emit_rope(lb, qT, 2, 4096, qtoks)
        rk = emit_rope(lb, kT, 2, 4096, ktoks + rq)
        prev_unit = []
        for h2 in range(2):
            for qb in range(32):
                I = qb
                nk = (I + 1) * 128
                deps0 = rq + rk + [vtoks[-1]] + prev_unit
                evs = []
                pos = 0
                prev_evA = {}
                nch = 0
                while pos < I:
                    w = min(4, I - pos)
                    pa = [psA, psA2][nch % 2]
                    dmm = (deps0 if pos == 0 else []) + (
                        [prev_evA[nch % 2]] if (nch % 2) in prev_evA else [])
                    mm = p.op("tensor", (lambda _h, _qb, _pos, _w, _pa: lambda e: e.matmul(
                        _pa[:, 0:_w * 128], qT[:, _h, _qb * 128:(_qb + 1) * 128],
                        kT[:, _h, _pos * 128:(_pos + _w) * 128], start=True, stop=True))(
                        h2, qb, pos, w, pa), deps=dmm)
                    ev = p.op("vector", (lambda _pos, _w, _pa: lambda e: e.tensor_copy(
                        Ssb[:, _pos * 128:(_pos + _w) * 128], _pa[:, 0:_w * 128]))(pos, w, pa),
                        deps=[mm])
                    prev_evA[nch % 2] = ev
                    nch += 1
                    evs.append(ev)
                    pos += w
                mm2 = p.op("tensor", (lambda _h, _qb: lambda e: e.matmul(
                    psB[:], qT[:, _h, _qb * 128:(_qb + 1) * 128],
                    kT[:, _h, _qb * 128:(_qb + 1) * 128], start=True, stop=True))(h2, qb),
                    deps=deps0)
                ev2 = p.op("vector", (lambda _I: lambda e: e.tensor_add(
                    Ssb[:, _I * 128:(_I + 1) * 128], psB[:], mtril[:]))(I),
                    deps=[mm2, tok_tril])
                evs.append(ev2)
                r1 = p.op("vector", (lambda _nk: lambda e: e.reduce_max(
                    red[:, 0:1], Ssb[:, 0:_nk], AX.X, negate=True))(nk), deps=evs)
                r2 = p.op("vector", lambda e: e.tensor_scalar_mul(red[:, 0:1], red[:, 0:1], 0.125),
                          deps=[r1])
                ex = p.op("scalar", (lambda _nk: lambda e: e.activation(
                    Pbf[:, 0:_nk], Ssb[:, 0:_nk], AF.Exp, bias=red[:, 0:1], scale=0.125,
                    accum_out=red[:, 1:2]))(nk), deps=[r2])
                rc = p.op("vector", lambda e: e.reciprocal(red[:, 2:3], red[:, 1:2]), deps=[ex])
                sc = p.op("scalar", (lambda _nk: lambda e: e.activation(
                    Pbf[:, 0:_nk], Pbf[:, 0:_nk], AF.Copy, scale=red[:, 2:3]))(nk),
                    deps=[rc])
                pv_mms = []
                prev_pe = {}
                for J in range(I + 1):
                    bi = J % 2
                    war = [prev_pe[bi]] if bi in prev_pe else []
                    tr = p.op("tensor", (lambda _J, _bi: lambda e: e.transpose(
                        psT2[_bi][:], Pbf[:, _J * 128:(_J + 1) * 128], identbf[:]))(J, bi),
                        deps=[sc, tok_id] + war)
                    pe = p.op("vector", (lambda _bi: lambda e: e.tensor_copy(
                        PT2[_bi][:], psT2[_bi][:]))(bi),
                        deps=[tr] + pv_mms[-2:-1])
                    prev_pe[bi] = pe
                    mv = p.op("tensor", (lambda _J, _h, _last, _bi: lambda e: e.matmul(
                        psO[:], vtok[:, _J, _h * 64:(_h + 1) * 64], PT2[_bi][:],
                        start=(_J == 0), stop=_last))(J, h2, J == I, bi),
                        deps=[pe] + (prev_unit if J == 0 else []))
                    pv_mms.append(mv)
                ao = p.op("vector", (lambda _h, _qb: lambda e: e.tensor_copy(
                    att2[:, _h, _qb * 128:(_qb + 1) * 128], psO[:]))(h2, qb),
                    deps=[pv_mms[-1]])
                prev_unit = [ao]
        sta = p.dma("sync", [(E["att_dram"][:, hp * 2 + h2, :], att2[:, h2, :])
                             for h2 in range(2)], deps=prev_unit)
        pair_guard = [sta]
        att_done_all.append(sta)
    return emit_wo_partial(lb, E, E["att_dram"], att_done_all, nheads=8)

from concourse.bass_core_rust import IndirectOffsetOnAxis

CAP = 5120
NSEG = 16
WCH = 37
NLOC = 128 * WCH


def emit_l1route(lb, E):
    nc, p = lb.nc, lb.p
    xnT, xn_toks = E["xnT"], E["xn_toks"]
    wg_d = E["w_g"]
    o_xt = E["xn_tok"]

    identbf, tok_id = lb.const("identity_bf", np.eye(128, dtype=np.float32).astype(BF))
    identf, tok_idf = lb.const("identity_f32", np.eye(128, dtype=np.float32))
    ident8, tok_id8 = lb.const("ident8", np.eye(8, dtype=np.float32))
    iota8, tok_io = lb.const(
        "iota8_row", np.broadcast_to(np.arange(8, dtype=np.float32)[None, :], (128, 8)).copy())
    ones_row, tok_or = lb.const("ones_row_f32", np.ones((1, 128), np.float32))
    ones8f, tok_o8 = lb.const("ones8_col", np.ones((8, 1), np.float32))
    e5120c, tok_e5 = lb.const("e5120_col", (np.arange(8, dtype=np.float32) * CAP)[:, None])
    tlt8, tok_tlt = lb.const("tril8_lt", np.tril(np.ones((8, 8), np.float32), -1).T.copy())

    wg_sb = lb.sb("wg_sb", [128, 8, 8], BF16)
    ld_wg = p.dma("gpsimd", [(wg_sb[:], wg_d.rearrange("(c p) n -> p c n", p=128))])
    ld_xn = p.op("vector", lambda e: e.engine_nop(), deps=xn_toks)

    psL = lb.ps("psL", [128, 8], F32)
    psT = lb.ps("psT", [128, 128], BF16)
    psTf = lb.ps("psTf", [128, 128], F32)
    psF = lb.ps("psF", [1, 512], F32)
    psC = lb.ps("psC", [128, 512], F32)

    xtile = lb.sb("xtile", [128, 1024], BF16)
    logit = lb.sb("logit", [128, 8], F32)
    top8 = lb.sb("top8", [128, 8], F32)
    topi = lb.sb("topi", [128, 8], U32)
    topif = lb.sb("topif", [128, 8], F32)
    pcols = lb.sb("pcols", [128, 2, 16], F32)
    oh_tok = lb.sb("oh_tok", [128, 2, 8], F32)
    ohT = lb.sb("ohT", [8, 2, 2048], F32)
    rTs = lb.sb("rTs", [8, 2, 2048], F32)
    red2 = lb.sb("red2", [128, 2], F32)

    ztile = lb.sb("ztile", [128, 8192], BF16)
    zm = p.op("vector", lambda e: e.memset(ztile[:], 0.0))
    zd_toks = []
    mdisp = E["mdisp"]
    for zz in range(40):
        zd = p.dma("sync", [(mdisp[zz * 1024:(zz + 1) * 1024, :].rearrange(
            "(a p) n -> p a n", p=128),
            ztile[:].rearrange("p (a n) -> p a n", a=8))], deps=[zm])
        zd_toks.append(zd)

    prev_blk = []
    prev_tr = None
    prev_trh = None
    st_toks = []
    for tb in range(16):
        tsl = slice(tb * 128, (tb + 1) * 128)
        trs = []
        for dc in range(8):
            tr = p.op("tensor", (lambda _dc, _tb: lambda e: e.transpose(
                psT[:], xnT[:, _dc, _tb * 128:(_tb + 1) * 128], identbf[:]))(dc, tb),
                deps=[ld_xn, tok_id] + ([prev_tr] if prev_tr is not None else []))
            ev = p.op("vector", (lambda _dc: lambda e: e.tensor_copy(
                xtile[:, _dc * 128:(_dc + 1) * 128], psT[:]))(dc),
                deps=[tr] + prev_blk)
            prev_tr = ev
            trs.append(ev)
        stx = p.dma("sync", [(o_xt[tsl, :], xtile[:])], deps=trs)
        st_toks.append(stx)
        mms = []
        for kc in range(8):
            t = p.op("tensor", (lambda _kc, _tb: lambda e: e.matmul(
                psL[:], xnT[:, _kc, _tb * 128:(_tb + 1) * 128], wg_sb[:, _kc, :],
                start=(_kc == 0), stop=(_kc == 7)))(kc, tb),
                deps=([ld_xn, ld_wg] + prev_blk) if kc == 0 else [])
            mms.append(t)
        lv = p.op("vector", lambda e: e.tensor_copy(logit[:], psL[:]), deps=[mms[-1]] + prev_blk)
        m8 = p.op("vector", lambda e: e.max(top8[:], logit[:]), deps=[lv])
        mi = p.op("vector", lambda e: e.max_index(topi[:], top8[:], logit[:]), deps=[m8])
        mif = p.op("vector", lambda e: e.tensor_copy(topif[:], topi[:]), deps=[mi])
        d1 = p.op("vector", lambda e: e.tensor_sub(red2[:, 0:1], top8[:, 1:2], top8[:, 0:1]),
                  deps=[m8] + prev_blk)
        sg = p.op("scalar", lambda e: e.activation(red2[:, 1:2], red2[:, 0:1], AF.Exp),
                  deps=[d1] + prev_blk)
        sg2 = p.op("vector", lambda e: e.tensor_scalar_add(red2[:, 1:2], red2[:, 1:2], 1.0),
                   deps=[sg])
        sg3 = p.op("vector", (lambda _tb: lambda e: e.reciprocal(
            pcols[:, 0, _tb:_tb + 1], red2[:, 1:2]))(tb), deps=[sg2])
        sg4 = p.op("vector", (lambda _tb: lambda e: e.tensor_scalar(
            pcols[:, 1, _tb:_tb + 1], pcols[:, 0, _tb:_tb + 1], -1.0, 1.0,
            ALU.mult, ALU.add))(tb), deps=[sg3])
        khs = []
        for k in range(2):
            oh = p.op("vector", (lambda _k: lambda e: e.tensor_single_scalar(
                oh_tok[:, _k, :], iota8[:], topif[:, _k:_k + 1], ALU.is_equal))(k),
                deps=[mif, tok_io] + prev_blk)
            trh = p.op("tensor", (lambda _k: lambda e: e.transpose(
                psTf[0:8, :], oh_tok[:, _k, :], identf[:]))(k),
                deps=[oh, tok_idf] + ([prev_trh] if prev_trh is not None else []))
            evh = p.op("vector", (lambda _k, _tb: lambda e: e.tensor_copy(
                ohT[:, _k, _tb * 128:(_tb + 1) * 128], psTf[0:8, :]))(k, tb), deps=[trh])
            prev_trh = evh
            khs.append(evh)
        prev_blk = khs + [sg4, stx]

    ones8s = lb.sb("ones8s", [8, 1], F32)
    o8 = p.op("vector", lambda e: e.memset(ones8s[:], 1.0))
    rt_toks = []
    cn = lb.sb("cn", [8, 2], F32)
    for k in range(2):
        d0 = bass.AP(ones8s[:].tensor, 0, [[1, 8], [0, 2048]])
        sc = p.op("vector", (lambda _k, _d0: lambda e: e.tensor_tensor_scan(
            rTs[:, _k, :], _d0, ohT[:, _k, :], 0.0, ALU.mult, ALU.add))(k, d0),
            deps=prev_blk + [o8])
        ex = p.op("vector", (lambda _k: lambda e: e.tensor_sub(
            rTs[:, _k, :], rTs[:, _k, :], ohT[:, _k, :]))(k), deps=[sc])
        tt = p.op("vector", (lambda _k: lambda e: e.tensor_add(
            cn[:, _k:_k + 1], rTs[:, _k, 2047:2048], ohT[:, _k, 2047:2048]))(k), deps=[ex])
        rt_toks += [ex, tt]

    st_cn = p.dma("sync", [(E["cn_ag_in"], cn[:])], deps=rt_toks)
    cc_cn = p.cc(lambda e: e.collective_compute(
        "AllGather", ALU.bypass, replica_groups=[[0, 1, 2, 3, 4, 5, 6, 7]],
        ins=[E["cn_ag_in"].opt()], outs=[E["cn_ag_out"].opt()]), deps=[st_cn])
    cnall = lb.sb("cnall", [64, 2], F32)
    mlt = lb.sb("mlt", [64, 8], F32)
    mall = lb.sb("mall", [64, 8], F32)
    ld_cn = p.dma("sync", [(cnall[:], E["cn_ag_out"]), (mlt[:], E["mlt"]),
                           (mall[:], E["mall"])], deps=[cc_cn])
    psP = lb.ps("psP", [8, 8], F32)
    part = lb.sb("part", [8, 2], F32)
    full = lb.sb("full", [8, 2], F32)
    praw = lb.sb("praw", [8, 2], F32)
    kept = lb.sb("kept", [8, 2], F32)
    p37t = lb.sb("p37t", [8, 2], F32)
    p37i = lb.sb("p37i", [8, 2], I32)
    loff = lb.sb("loff", [8, 2], F32)
    goff = lb.sb("goff", [8, 2], F32)
    tmp82 = lb.sb("tmp82", [8, 2], F32)
    mm1 = p.op("tensor", lambda e: e.matmul(psP[:, 0:2], mlt[:], cnall[:], start=True, stop=True),
               deps=[ld_cn])
    ev1 = p.op("vector", lambda e: e.tensor_copy(part[:], psP[:, 0:2]), deps=[mm1])
    mm2 = p.op("tensor", lambda e: e.matmul(psP[:, 0:2], mall[:], cnall[:], start=True, stop=True),
               deps=[ev1])
    ev2 = p.op("vector", lambda e: e.tensor_copy(full[:], psP[:, 0:2]), deps=[mm2])
    q1 = p.op("vector", lambda e: e.tensor_copy(praw[:, 0:1], part[:, 0:1]), deps=[ev1, ev2])
    q2 = p.op("vector", lambda e: e.tensor_add(praw[:, 1:2], full[:, 0:1], part[:, 1:2]),
              deps=[ev1, ev2])
    k1 = p.op("vector", lambda e: e.tensor_scalar(kept[:], praw[:], -1.0, float(CAP),
                                                  ALU.mult, ALU.add), deps=[q1, q2])
    k2 = p.op("vector", lambda e: e.tensor_single_scalar(kept[:], kept[:], 0.0, ALU.max),
              deps=[k1])
    k3 = p.op("vector", lambda e: e.tensor_tensor(kept[:], kept[:], cn[:], ALU.min),
              deps=[k2] + rt_toks)
    f1 = p.op("vector", lambda e: e.tensor_scalar(p37t[:], kept[:], 1.0 / WCH,
                                                  (WCH - 1.0) / WCH, ALU.mult, ALU.add),
              deps=[k3])
    f2 = p.op("vector", lambda e: e.tensor_copy(p37i[:], p37t[:]), deps=[f1])
    f3 = p.op("vector", lambda e: e.tensor_copy(tmp82[:], p37i[:]), deps=[f2])
    f4 = p.op("vector", lambda e: e.tensor_tensor(goff[:], tmp82[:], p37t[:], ALU.is_gt),
              deps=[f3])
    f5 = p.op("vector", lambda e: e.tensor_sub(tmp82[:], tmp82[:], goff[:]), deps=[f4])
    f6 = p.op("vector", lambda e: e.tensor_scalar_mul(p37t[:], tmp82[:], float(WCH)),
              deps=[f5])
    mm3 = p.op("tensor", lambda e: e.matmul(psP[:, 0:2], tlt8[:, 0:8], p37t[:],
                                            start=True, stop=True), deps=[f6, tok_tlt])
    ev3 = p.op("vector", lambda e: e.tensor_copy(tmp82[:], psP[:, 0:2]), deps=[mm3, f5])
    l1_ = p.op("vector", lambda e: e.tensor_add(loff[:, 0:1], tmp82[:, 0:1], tmp82[:, 1:2]),
               deps=[ev3])
    l2_ = p.op("vector", lambda e: e.tensor_add(loff[:, 1:2], loff[:, 0:1], p37t[:, 0:1]),
               deps=[l1_, f6])
    g1 = p.op("vector", lambda e: e.tensor_scalar_add(goff[:, 0:1], praw[:, 0:1],
                                                      e5120c[:, 0:1]), deps=[q1, f4, tok_e5])
    g2 = p.op("vector", lambda e: e.tensor_scalar_add(goff[:, 1:2], praw[:, 1:2],
                                                      e5120c[:, 0:1]), deps=[q2, f4, tok_e5])
    off_done = [l2_, g1, g2, k3]

    scr = lb.sb("scr", [8, 2, 2048], F32)
    rowS = lb.sb("rowS", [1, 2, 2048], F32)
    rowK = lb.sb("rowK", [1, 2, 2048], F32)
    rowL = rowK
    drop = lb.sb("drop", [1, 2, 2048], F32)
    p0row = lb.sb("p0row", [1, 2048], F32)
    slotp = E["slotpos"]
    wrow = E["wrow"]

    def rowsum(dst, colv, colv_deps, k, extra=0.0):
        t1 = p.op("vector", (lambda _k: lambda e: e.tensor_scalar_add(
            scr[:, _k, :], rTs[:, _k, :], colv[:, k:k + 1]))(k),
            deps=rt_toks + colv_deps + prev_blk)
        t2 = p.op("vector", (lambda _k: lambda e: e.tensor_mul(
            scr[:, _k, :], scr[:, _k, :], ohT[:, _k, :]))(k), deps=[t1])
        out = []
        for ff in range(4):
            m1 = p.op("tensor", (lambda _k, _f: lambda e: e.matmul(
                psF[:], ones8f[:], scr[:, _k, _f * 512:(_f + 1) * 512],
                start=True, stop=True))(k, ff), deps=[t2, tok_o8] + out[-1:])
            f0 = p.op("vector", (lambda _k, _f, _d: lambda e: e.tensor_copy(
                _d[0:1, _k, _f * 512:(_f + 1) * 512], psF[0:1, :]))(k, ff, dst), deps=[m1])
            out.append(f0)
        return out[-1:]

    row_toks = []
    drop_toks = {}
    for k in range(2):
        row_toks += rowsum(rowS, goff, off_done, k)
        kt = rowsum(rowK, praw, off_done, k)
        dd0 = p.op("vector", (lambda _k: lambda e: e.tensor_single_scalar(
            drop[0:1, _k, :], rowK[0:1, _k, :], CAP - 0.5, ALU.is_gt))(k), deps=kt)
        drop_toks[k] = dd0
        row_toks += rowsum(rowL, loff, off_done + [dd0], k)
    sidx_toks = []
    for k in range(2):
        dd = drop_toks[k]
        s1 = p.op("vector", (lambda _k: lambda e: e.scalar_tensor_tensor(
            rowS[0:1, _k, :], drop[0:1, _k, :], 1e8, rowS[0:1, _k, :],
            ALU.mult, ALU.add))(k), deps=[dd] + row_toks)
        s2 = p.op("vector", (lambda _k: lambda e: e.tensor_scalar(
            slotp[0:1, _k, :], rowL[0:1, _k, :], -1.0, float(NLOC - 1),
            ALU.mult, ALU.add))(k), deps=row_toks + E.get("slot_war", []))
        s3 = p.op("vector", (lambda _k: lambda e: e.tensor_mul(
            slotp[0:1, _k, :], slotp[0:1, _k, :], drop[0:1, _k, :]))(k), deps=[s2, dd])
        s4 = p.op("vector", (lambda _k: lambda e: e.tensor_add(
            slotp[0:1, _k, :], slotp[0:1, _k, :], rowL[0:1, _k, :]))(k), deps=[s3])
        sidx_toks += [s1, s4]
        row_toks = row_toks + [dd, s1]
    st_pc = p.dma("sync", [(E["pc_bounce"].rearrange("k j p -> p k j"), pcols[:])],
                  deps=prev_blk, slow=True)
    ld_p0 = p.dma("sync", [(p0row[:], E["pc_bounce"].rearrange(
        "k j p -> k (j p)")[0:1, :])], deps=[st_pc])
    w_toks = []
    for k in range(2):
        w1 = p.op("vector", (lambda _k: lambda e: e.tensor_scalar(
            wrow[0:1, _k, :], drop[0:1, _k, :], -1.0, 1.0, ALU.mult, ALU.add))(k),
            deps=sidx_toks + [ld_p0] + E.get("w_war", []))
        if k == 0:
            w2 = p.op("vector", lambda e: e.tensor_mul(
                wrow[0:1, 0, :], wrow[0:1, 0, :], p0row[0:1, :]), deps=[w1])
        else:
            w2a = p.op("vector", lambda e: e.tensor_scalar(
                p0row[0:1, :], p0row[0:1, :], -1.0, 1.0, ALU.mult, ALU.add), deps=[w1] + w_toks)
            w2 = p.op("vector", lambda e: e.tensor_mul(
                wrow[0:1, 1, :], wrow[0:1, 1, :], p0row[0:1, :]), deps=[w2a])
        w_toks.append(w2)

    rl16 = lb.sb("rl16", [1, 16], F32)
    rd16 = lb.sb("rd16", [1, 16], F32)
    rdd = lb.sb("rdd", [8, 2], F32)
    gcol = lb.sb("gcol", [128, 1], F32)
    gtmp = lb.sb("gtmp", [128, 2], F32)
    gidx = E["gidx"]
    gidxf = lb.sb("gidxf", [128, 1], F32)
    c37, tok_c37 = lb.const("c37_col", (WCH * np.arange(128, dtype=np.float32))[:, None])
    dd1 = p.op("vector", lambda e: e.tensor_sub(rdd[:], goff[:], loff[:]), deps=off_done)
    r16_toks = []
    for k in range(2):
        mmr = p.op("tensor", (lambda _k: lambda e: e.matmul(
            psP[0:1, 0:8], loff[:, _k:_k + 1], ident8[:], start=True, stop=True))(k),
            deps=[l2_, tok_id8] + r16_toks)
        evr = p.op("vector", (lambda _k: lambda e: e.tensor_copy(
            rl16[0:1, _k:16:2], psP[0:1, 0:8]))(k), deps=[mmr])
        mmd = p.op("tensor", (lambda _k: lambda e: e.matmul(
            psP[0:1, 0:8], rdd[:, _k:_k + 1], ident8[:], start=True, stop=True))(k),
            deps=[dd1, tok_id8, evr])
        evd = p.op("vector", (lambda _k: lambda e: e.tensor_copy(
            rd16[0:1, _k:16:2], psP[0:1, 0:8]))(k), deps=[mmd])
        r16_toks += [evr, evd]
    diff_toks = []
    prevd = r16_toks
    for sj in range(15, 0, -1):
        dj = p.op("vector", (lambda _s: lambda e: e.tensor_sub(
            rd16[0:1, _s:_s + 1], rd16[0:1, _s:_s + 1], rd16[0:1, _s - 1:_s]))(sj),
            deps=prevd)
        prevd = [dj]
        diff_toks.append(dj)
    rlh = p.op("vector", lambda e: e.tensor_scalar_add(rl16[:], rl16[:], -0.5),
               deps=r16_toks)
    rld2 = lb.sb("rld2", [1, 32], F32)
    il1 = p.op("vector", lambda e: e.tensor_copy(rld2[0:1, 0:32:2], rl16[0:1, :]),
               deps=[rlh])
    il2 = p.op("vector", lambda e: e.tensor_copy(rld2[0:1, 1:32:2], rd16[0:1, :]),
               deps=[il1] + diff_toks)
    gi0 = p.op("vector", lambda e: e.tensor_copy(gidxf[:], c37[:]),
               deps=[tok_c37] + E.get("g_war", []))
    gprev = [gi0, il2]
    for sj in range(16):
        bm = p.op("tensor", (lambda _s: lambda e: e.matmul(
            psC[:, 0:2], ones_row[:], rld2[0:1, 2 * _s:2 * _s + 2],
            start=True, stop=True))(sj), deps=gprev + [tok_or])
        evb = p.op("vector", lambda e: e.tensor_copy(gtmp[:], psC[:, 0:2]), deps=[bm])
        ge2 = p.op("vector", lambda e: e.tensor_tensor(
            gcol[:], c37[:], gtmp[:, 0:1], ALU.is_gt), deps=[evb] + gprev)
        ac = p.op("vector", lambda e: e.tensor_mul(gcol[:], gcol[:], gtmp[:, 1:2]),
                  deps=[ge2])
        ad = p.op("vector", lambda e: e.tensor_add(gidxf[:], gidxf[:], gcol[:]),
                  deps=[ac])
        gprev = [ad]
    gcast = p.op("vector", lambda e: e.tensor_copy(gidx[:], gidxf[:]), deps=gprev)

    st_row = p.dma("sync", [(E["row_b0"], rowS[0:1, 0, :]), (E["row_b1"], rowS[0:1, 1, :])],
                   deps=sidx_toks)
    sidxc = lb.sb("sidxc", [128, 2, 16], I32)
    ld_sc = p.dma("gpsimd", [(sidxc[:, 0, :], E["row_b0"].rearrange("(j p) -> p j", p=128)),
                             (sidxc[:, 1, :], E["row_b1"].rearrange("(j p) -> p j", p=128))],
                  deps=[st_row], slow=True)
    xt2 = [lb.sb(f"xt2_{i}", [128, 1024], BF16) for i in range(2)]
    sc_toks = []
    prev_g = {}
    for tb in range(16):
        i = tb % 2
        dl = [prev_g[i]] if i in prev_g else []
        ldx = p.dma("sync", [(xt2[i][:], E["xn_tok"][tb * 128:(tb + 1) * 128, :])],
                    deps=dl + [st_toks[tb]])
        for k in range(2):
            g = p.idma((lambda _i, _tb, _k: lambda e: e.indirect_dma_start(
                out=mdisp, out_offset=IndirectOffsetOnAxis(
                    sidxc[:, _k, _tb:_tb + 1], 0),
                in_=xt2[_i][:], in_offset=None,
                bounds_check=8 * CAP, oob_is_err=False))(i, tb, k),
                deps=[ldx, ld_sc] + zd_toks)
            prev_g[i] = g
            sc_toks.append(g)
    cmb_ready = [gcast] + w_toks + sidx_toks
    return sc_toks, cmb_ready


def emit_l1exp(lb, E):
    nc, p = lb.nc, lb.p
    identbf, tok_id = lb.const("identity_bf", np.eye(128, dtype=np.float32).astype(BF))
    cfc_d, cpr_d = E["c_fc"], E["c_proj"]
    mbatch, o_oe = E["mbatch"], E["oe_out"]

    cfc = lb.sb("cfc", [128, 8, 4096], BF16)
    cprb = [lb.sb(f"cprb{i}", [128, 32, 128], BF16) for i in range(2)]
    ldw1 = p.dma("gpsimd", [(cfc[:], cfc_d.rearrange("(c p) n -> p c n", p=128))])
    psT = lb.ps("psT", [128, 128], BF16)
    psY = lb.ps("psY", [128, 512], F32)
    psZ = lb.ps("psZ", [128, 512], F32)

    xe_tok = lb.sb("xe_tok", [128, 4, 1024], BF16)
    xeT = lb.sb("xeT", [128, 8, 512], BF16)
    hT = lb.sb("hT", [128, 32, 512], BF16)
    oeT = lb.sb("oeT", [128, 8, 512], BF16)
    oe_tok = lb.sb("oe_tok", [128, 4, 1024], BF16)
    prev_chunk = []
    prev_tr2 = None
    prev_cpr = {}
    all_st = []
    for ct in range(10):
        gts = []
        for j in range(4):
            g = p.dma("sync", [(xe_tok[:, j, :],
                               mbatch[ct * 512 + j * 128: ct * 512 + (j + 1) * 128, :])],
                      deps=E["rs_tok"] + prev_chunk)
            gts.append(g)
        tr_toks = []
        for j in range(4):
            for dc in range(8):
                tr = p.op("tensor", (lambda _j, _dc: lambda e: e.transpose(
                    psT[:], xe_tok[:, _j, _dc * 128:(_dc + 1) * 128], identbf[:]))(j, dc),
                    deps=[gts[j], tok_id] + ([prev_tr2] if prev_tr2 else [])),
                tr = tr[0] if isinstance(tr, tuple) else tr
                ev = p.op("vector", (lambda _j, _dc: lambda e: e.tensor_copy(
                    xeT[:, _dc, _j * 128:(_j + 1) * 128], psT[:]))(j, dc), deps=[tr] + prev_chunk)
                prev_tr2 = ev
                tr_toks.append(ev)
        y_toks = []
        for oc in range(32):
            mms = []
            for kc in range(8):
                t = p.op("tensor", (lambda _kc, _oc: lambda e: e.matmul(
                    psY[:], cfc[:, _kc, _oc * 128:(_oc + 1) * 128], xeT[:, _kc, :],
                    start=(_kc == 0), stop=(_kc == 7)))(kc, oc),
                    deps=([ldw1] + tr_toks + y_toks[-1:]) if kc == 0 else [])
                mms.append(t)
            ya = p.op("scalar", (lambda _oc: lambda e: e.activation(
                hT[:, _oc, :], psY[:], AF.Gelu))(oc), deps=[mms[-1]] + prev_chunk)
            y_toks.append(ya)
        z_toks = []
        for zc in range(8):
            zi = zc % 2
            wl = [prev_cpr[zi]] if zi in prev_cpr else []
            ldw2 = p.dma("gpsimd", [(cprb[zi][:], cpr_d[:, zc * 128:(zc + 1) * 128]
                                     .rearrange("(c p) n -> p c n", p=128))], deps=wl)
            mms = []
            for kc in range(32):
                t = p.op("tensor", (lambda _kc, _zc, _zi: lambda e: e.matmul(
                    psZ[:], cprb[_zi][:, _kc, :], hT[:, _kc, :],
                    start=(_kc == 0), stop=(_kc == 31)))(kc, zc, zi),
                    deps=([ldw2] + y_toks + z_toks[-1:]) if kc == 0 else [])
                mms.append(t)
            ze = p.op("vector", (lambda _zc: lambda e: e.tensor_copy(
                oeT[:, _zc, :], psZ[:]))(zc), deps=[mms[-1]] + prev_chunk)
            prev_cpr[zi] = ze
            z_toks.append(ze)
        st_toks2 = []
        for j in range(4):
            evs = []
            for dc in range(8):
                tr = p.op("tensor", (lambda _j, _dc: lambda e: e.transpose(
                    psT[:], oeT[:, _dc, _j * 128:(_j + 1) * 128], identbf[:]))(j, dc),
                    deps=z_toks + [tok_id] + ([prev_tr2] if prev_tr2 else []))
                ev = p.op("vector", (lambda _j, _dc: lambda e: e.tensor_copy(
                    oe_tok[:, _j, _dc * 128:(_dc + 1) * 128], psT[:]))(j, dc),
                    deps=[tr] + prev_chunk)
                prev_tr2 = ev
                evs.append(ev)
            st = p.dma("sync", [(o_oe[ct * 512 + j * 128: ct * 512 + (j + 1) * 128, :],
                                 oe_tok[:, j, :])], deps=evs)
            st_toks2.append(st)
        prev_chunk = st_toks2
        all_st += st_toks2
    return all_st


def emit_l1cmb(lb, E):
    nc, p = lb.nc, lb.p
    identbf, tok_id = lb.const("identity_bf", np.eye(128, dtype=np.float32).astype(BF))
    ones_row, tok_or = lb.const("ones_row_f32", np.ones((1, 128), np.float32))
    cW, tok_cW = lb.const("c37_col", (WCH * np.arange(128, dtype=np.float32))[:, None])
    x, o_x2 = E["x"], E["x_out"]

    soe = lb.sb("soe", [128, WCH * 1024], BF16)
    aoutT = lb.sb("aoutT", [128, 8, 2048], BF16)
    Eb = lb.sb("Eb", [128, 2, 2048], F32)
    wb = lb.sb("wb", [128, 2, 2048], F32)
    psC = lb.ps("psCb", [128, 512], F32)

    zs = p.op("vector", lambda e: e.memset(soe[:], 0.0))
    gidx2 = lb.sb("gidx2", [128, 1], I32)
    g2c = p.op("vector", lambda e: e.tensor_single_scalar(
        gidx2[:], E["gidx"][:], 19, ALU.add), deps=E["cmb_ready"])
    gat1 = p.idma(lambda e: e.indirect_dma_start(
        out=soe[:, 0:19 * 1024], out_offset=None, in_=E["oe_all"],
        in_offset=IndirectOffsetOnAxis(E["gidx"][:], 0),
        bounds_check=8 * CAP, oob_is_err=False),
        deps=[zs] + E["ag_tok"] + E["cmb_ready"])
    gat = p.idma(lambda e: e.indirect_dma_start(
        out=soe[:, 19 * 1024:37 * 1024], out_offset=None, in_=E["oe_all"],
        in_offset=IndirectOffsetOnAxis(gidx2[:], 0),
        bounds_check=8 * CAP, oob_is_err=False),
        deps=[gat1, g2c])
    ld_ao = p.dma("sync", [(aoutT[:], E["rs_out"].rearrange("p (h t) -> p h t", h=8))],
                  deps=E["rs_tok"])

    bc_toks = []
    prev_bc = []
    for k in range(2):
        for ff in range(4):
            m1 = p.op("tensor", (lambda _k, _f: lambda e: e.matmul(
                psC[:], ones_row[:], E["slotpos"][0:1, _k, _f * 512:(_f + 1) * 512],
                start=True, stop=True))(k, ff), deps=E["cmb_ready"] + [tok_or] + prev_bc)
            e1 = p.op("vector", (lambda _k, _f: lambda e: e.tensor_copy(
                Eb[:, _k, _f * 512:(_f + 1) * 512], psC[:]))(k, ff), deps=[m1])
            m2 = p.op("tensor", (lambda _k, _f: lambda e: e.matmul(
                psC[:], ones_row[:], E["wrow"][0:1, _k, _f * 512:(_f + 1) * 512],
                start=True, stop=True))(k, ff), deps=[e1])
            e2 = p.op("vector", (lambda _k, _f: lambda e: e.tensor_copy(
                wb[:, _k, _f * 512:(_f + 1) * 512], psC[:]))(k, ff), deps=[m2])
            prev_bc = [e2]
            bc_toks += [e1, e2]
    cWb = [bass.AP(cW[:].tensor, 0, [cW[:].ap[0], [0, 2048]])]
    sub_toks = []
    for k in range(2):
        t = p.op("vector", (lambda _k: lambda e: e.tensor_sub(
            Eb[:, _k, :], Eb[:, _k, :], cWb[0]))(k), deps=bc_toks + [tok_cW])
        sub_toks.append(t)

    mk = lb.sb("mk", [128, 256], F32)
    t1m = lb.sb("t1m", [128, 256], F32)
    Pj = lb.sb("Pj", [128, 256], BF16)
    psMo = [lb.ps(f"psMo{i}", [128, 1024], F32) for i in range(2)]
    psbf = [lb.ps(f"psbf{i}", [128, 128], BF16) for i in range(2)]
    xtile = [lb.sb(f"rx{i}", [128, 1024], F32) for i in range(2)]
    otile = [lb.sb(f"ro{i}", [128, 1024], F32) for i in range(2)]
    ttmp = lb.sb("ttmp", [128, 128], F32)
    st_toks = []
    prev_st = {}
    prev_trr = {}
    prev_tt = None
    prev_grp_mm = []
    prev_pev = None
    for grp in range(8):
        gsl = slice(grp * 256, (grp + 1) * 256)
        mm_last = []
        for j in range(WCH):
            e0 = p.op("vector", (lambda _j, _g: lambda e: e.tensor_single_scalar(
                mk[:], Eb[:, 0, _g * 256:(_g + 1) * 256], float(_j), ALU.is_equal))(j, grp),
                deps=sub_toks + ([prev_pev] if prev_pev else []))
            p0_ = p.op("vector", (lambda _g: lambda e: e.tensor_mul(
                Pj[:], mk[:], wb[:, 0, _g * 256:(_g + 1) * 256]))(grp), deps=[e0])
            e1_ = p.op("vector", (lambda _j, _g: lambda e: e.tensor_single_scalar(
                mk[:], Eb[:, 1, _g * 256:(_g + 1) * 256], float(_j), ALU.is_equal))(j, grp),
                deps=[p0_])
            t1_ = p.op("vector", (lambda _g: lambda e: e.tensor_mul(
                t1m[:], mk[:], wb[:, 1, _g * 256:(_g + 1) * 256]))(grp), deps=[e1_])
            pv = p.op("vector", lambda e: e.tensor_add(Pj[:], Pj[:], t1m[:]), deps=[t1_])
            prev_pev = pv
            for t2 in range(2):
                for ob in range(2):
                    mmd = ([gat, pv] + prev_grp_mm) if j == 0 else [pv]
                    mm = p.op("tensor", (lambda _j, _t2, _ob: lambda e: e.matmul(
                        psMo[_t2][:, _ob * 512:(_ob + 1) * 512],
                        Pj[:, _t2 * 128:(_t2 + 1) * 128],
                        soe[:, _j * 1024 + _ob * 512:_j * 1024 + (_ob + 1) * 512],
                        start=(_j == 0), stop=(_j == WCH - 1)))(j, t2, ob),
                        deps=mmd)
                    if j == WCH - 1:
                        mm_last.append(mm)
        grp_evs = []
        for t2 in range(2):
            tb = grp * 2 + t2
            i = tb % 2
            dl = [prev_st[i]] if i in prev_st else []
            ld = p.dma("sync", [(xtile[i][:], x[tb * 128:(tb + 1) * 128, :])], deps=dl)
            adds = []
            for dc in range(8):
                pb = psbf[dc % 2]
                war = [prev_trr[dc % 2]] if (dc % 2) in prev_trr else []
                tr = p.op("tensor", (lambda _dc, _tb, _pb: lambda e: e.transpose(
                    _pb[:], aoutT[:, _dc, _tb * 128:(_tb + 1) * 128], identbf[:]))(dc, tb, pb),
                    deps=[ld_ao, tok_id] + war + mm_last)
                cp = p.op("vector", (lambda _pb: lambda e: e.tensor_copy(ttmp[:], _pb[:]))(pb),
                          deps=[tr] + ([prev_tt] if prev_tt is not None else []))
                ad = p.op("vector", (lambda _dc, _i: lambda e: e.tensor_add(
                    otile[_i][:, _dc * 128:(_dc + 1) * 128],
                    xtile[_i][:, _dc * 128:(_dc + 1) * 128], ttmp[:]))(dc, i),
                    deps=[cp, ld] + dl)
                prev_trr[dc % 2] = cp
                prev_tt = ad
                adds.append(ad)
            am = p.op("vector", (lambda _i, _t2: lambda e: e.tensor_add(
                otile[_i][:], otile[_i][:], psMo[_t2][:]))(i, t2),
                deps=adds + mm_last)
            st = p.dma("sync", [(o_x2[tb * 128:(tb + 1) * 128, :], otile[i][:])],
                       deps=[am])
            prev_st[i] = st
            st_toks.append(st)
            grp_evs.append(am)
        prev_grp_mm = grp_evs
    return st_toks

def fft_dev_consts():
    c = {}
    b_, q_ = np.meshgrid(np.arange(64), np.arange(64), indexing="ij")
    ang = 2 * np.pi * b_ * q_ / 64.0
    w64dup = np.concatenate([np.cos(ang), -np.sin(ang)], 1)
    c["w64dup"] = np.concatenate([w64dup, w64dup], 0).astype(BF)
    al = np.arange(128)[:, None]
    r_ = np.arange(128)[None, :]
    F = np.zeros((64, 3, 128, 128), np.float32)
    for q in range(64):
        fq = np.exp(-2j * np.pi * al * q / 8192.0) * np.exp(-2j * np.pi * al * r_ / 128.0)
        F[q, 0] = fq.real
        F[q, 1] = fq.imag
        F[q, 2] = -fq.imag
    c["Ftab"] = np.ascontiguousarray(F.transpose(2, 0, 1, 3)).astype(BF)
    G = np.zeros((64, 3, 128, 128), np.float32)
    W128i = np.exp(2j * np.pi * al * r_ / 128.0)
    for q in range(64):
        T = np.exp(-2j * np.pi * 4095.0 * (q + 64 * np.arange(128)) / 8192.0)
        gq = (W128i * np.exp(2j * np.pi * al * q / 8192.0)).T * T[:, None]
        G[q, 0] = gq.real
        G[q, 1] = gq.imag
        G[q, 2] = -gq.imag
    c["Gtab"] = np.ascontiguousarray(G.transpose(2, 0, 1, 3)).astype(BF)
    q_b = np.exp(2j * np.pi * np.outer(np.arange(64), np.arange(32)) / 64.0) / 8192.0
    wi = np.concatenate([q_b.real.astype(np.float32), -q_b.imag.astype(np.float32)], 1)
    c["w64idup"] = np.concatenate([wi, wi], 0).astype(BF)
    L = 4096
    t = (np.arange(L) / (L - 1)).astype(np.float32)
    c["trows"] = np.broadcast_to(t[None, :], (64, L)).copy().astype(BF)
    dmax, dmin = (math.log(0.01) / 0.3, math.log(0.01) / 1.5)
    deltas = np.abs(np.linspace(dmax, dmin, 1024)).astype(np.float32)
    c["ndel_full"] = -deltas
    ang2 = 2.0 * math.pi * np.arange(L, dtype=np.float64) / L
    ph = 0.0001 * ang2
    c["peT"] = np.stack([t.astype(np.float64), np.cos(ph), -np.sin(ph)], 0).astype(np.float32)
    return c


def emit_l2a(lb, E):
    nc, p = lb.nc, lb.p
    wina_d, winb_d = E["win_a"], E["win_b"]
    bia_d, bib_d = E["bia"], E["bib"]
    shw0_d, shb0_d = E["shw_0"], E["shb_0"]
    shw1_d, shb1_d = E["shw_1"], E["shb_1"]
    shwb_d, shbb_d = E["shw_b"], E["shb_b"]
    W1_d, b1f_d, W2_d = E["W1"], E["b1f"], E["W2"]
    W3r_d, b3c_d, ndel_d = E["W3r"], E["b3c"], E["ndel"]
    fb_d = E["fbrow"]

    with lb.scoped():
        fc = fft_dev_consts()
        identbf, tok_id = lb.const("identity_bf", np.eye(128, dtype=np.float32).astype(BF))
        w64dup, tok_w64 = lb.const("w64dup", fc["w64dup"])
        w64idup, tok_w64i = lb.const("w64idup", fc["w64idup"])
        trows, tok_tr = lb.const("trows", fc["trows"])
        peT_d = lb.dram_in("c_peT", [3, 4096], F32)
        lb.const_arrays["c_peT"] = fc["peT"]
        negpi, tok_np = lb.const("negpi_col", np.full((128, 1), -math.pi, np.float32))
        ones_row, tok_or = lb.const("ones_row_f32", np.ones((1, 128), np.float32))
        Ftab_d = lb.dram_in("c_Ftab", [128, 64, 3, 128], BF16)
        lb.const_arrays["c_Ftab"] = fc["Ftab"]
        Gtab_d = lb.dram_in("c_Gtab", [128, 64, 3, 128], BF16)
        lb.const_arrays["c_Gtab"] = fc["Gtab"]

        xnT, xn_toks = E["xnT"], E["xn_toks"]
        ystg = [lb.sb("ystg%d" % i, [64, 128], BF16) for i in range(2)]
        smalls = {}
        tr_list = [("bia", bia_d, [128, 16]), ("bib", bib_d, [64, 16]),
                   ("shw0", shw0_d, [64, 16, 3]), ("shb0", shb0_d, [64, 16]),
                   ("shw1", shw1_d, [64, 16, 3]), ("shb1", shb1_d, [64, 16]),
                   ("shwb", shwb_d, [64, 16, 3]), ("shbb", shbb_d, [64, 16]),
                   ("W1", W1_d, [3, 64]), ("b1f", b1f_d, [64, 4]),
                   ("W2", W2_d, [64, 64]), ("W3r", W3r_d, [64, 8, 64]),
                   ("b3c", b3c_d, [64, 8]), ("ndel", ndel_d, [64, 8]),
                   ("fbrow", fb_d, [1, 512])]
        for nm, d, shp in tr_list:
            smalls[nm] = lb.sb("sm_" + nm, shp, F32)
        ld_small = p.dma("sync", [(smalls[nm][:], d) for nm, d, shp in tr_list])
        wina = lb.sb("wina", [128, 8, 128], BF16)
        winb = lb.sb("winb", [128, 8, 64], BF16)

        psA = lb.ps("psA", [128, 512], F32)
        psB = lb.ps("psB", [64, 512], F32)
        psT = lb.ps("psT", [128, 128], BF16)
        psS = lb.ps("psS", [128, 128], F32)
        psXr = lb.ps("psXr", [128, 64], F32)
        psXi = lb.ps("psXi", [128, 64], F32)
        psI2 = lb.ps("psI2", [128, 32], F32)

        s2_dram = lb.dram_scratch("s2_dram", [64, 4096], F32)
        _mlp_scope = lb.scoped()
        _mlp_scope.__enter__()
        s1 = lb.sb("s1", [64, 4096], F32)
        srf = lb.sb("srf", [64, 512], F32)
        srfI = lb.sb("srfI", [64, 512], I32)
        srfF = lb.sb("srfF", [64, 512], F32)
        srfG = lb.sb("srfG", [64, 512], F32)
        peTb = [lb.sb(f"peTb{i}", [3, 512], F32) for i in range(2)]
        b1f = smalls["b1f"]
        mlp_toks = []
        prev_ev = []
        prev_pe = {}
        for tc in range(8):
            pi = tc % 2
            pl = [prev_pe[pi]] if pi in prev_pe else []
            ld_pe = p.dma("sync", [(peTb[pi][:], peT_d[:, tc * 512:(tc + 1) * 512])], deps=pl)
            m = p.op("tensor", (lambda _tc, _pi: lambda e: e.matmul(
                psB[:], smalls["W1"][:], peTb[_pi][:],
                start=True, stop=True))(tc, pi), deps=[ld_small, ld_pe] + prev_ev + xn_toks[-1:])
            prev_pe[pi] = m
            t1 = p.op("vector", lambda e: e.tensor_scalar(
                srf[:], psB[:], b1f[:, 0:1], b1f[:, 1:2], ALU.mult, ALU.add), deps=[m] + prev_ev)
            c1 = p.op("vector", lambda e: e.tensor_copy(srfI[:], srf[:]), deps=[t1])
            c2 = p.op("vector", lambda e: e.tensor_copy(srfF[:], srfI[:]), deps=[c1])
            c3 = p.op("vector", lambda e: e.tensor_tensor(srfG[:], srfF[:], srf[:], ALU.is_gt),
                      deps=[c2])
            c4 = p.op("vector", lambda e: e.tensor_sub(srf[:], srf[:], srfF[:]), deps=[c3])
            t2 = p.op("vector", lambda e: e.tensor_add(srf[:], srf[:], srfG[:]), deps=[c4])
            t3 = p.op("scalar", (lambda _tc: lambda e: e.activation(
                s1[:, _tc * 512:(_tc + 1) * 512], srf[:], AF.Sin, bias=negpi[0:64, 0:1],
                scale=2.0 * math.pi))(tc), deps=[t2, tok_np])
            prev_ev = [t3]
            mlp_toks.append(t3)
        prev_ev2 = [mlp_toks[-1]]
        mlp2_toks = []
        for tc in range(8):
            m = p.op("tensor", (lambda _tc: lambda e: e.matmul(
                psB[:], smalls["W2"][:], s1[:, _tc * 512:(_tc + 1) * 512],
                start=True, stop=True))(tc), deps=[ld_small] + mlp_toks + prev_ev2)
            t1 = p.op("vector", lambda e: e.tensor_scalar(
                srf[:], psB[:], b1f[:, 2:3], b1f[:, 3:4], ALU.mult, ALU.add), deps=[m] + prev_ev2)
            c1 = p.op("vector", lambda e: e.tensor_copy(srfI[:], srf[:]), deps=[t1])
            c2 = p.op("vector", lambda e: e.tensor_copy(srfF[:], srfI[:]), deps=[c1])
            c3 = p.op("vector", lambda e: e.tensor_tensor(srfG[:], srfF[:], srf[:], ALU.is_gt),
                      deps=[c2])
            c4 = p.op("vector", lambda e: e.tensor_sub(srf[:], srf[:], srfF[:]), deps=[c3])
            t2 = p.op("vector", lambda e: e.tensor_add(srf[:], srf[:], srfG[:]), deps=[c4])
            t3 = p.op("scalar", (lambda _tc: lambda e: e.activation(
                srf[:], srf[:], AF.Sin, bias=negpi[0:64, 0:1],
                scale=2.0 * math.pi))(tc), deps=[t2, tok_np])
            t4 = p.dma("sync", [(s2_dram[:, tc * 512:(tc + 1) * 512], srf[:])], deps=[t3])
            prev_ev2 = [t4]
            mlp2_toks.append(t4)
        _mlp_scope.__exit__(None, None, None)
        s2b = [lb.sb(f"s2b{i}", [64, 512], F32) for i in range(2)]

        fbb = lb.sb("fbb", [128, 512], F32)
        b1_ = p.op("tensor", lambda e: e.matmul(psA[:], ones_row[:], smalls["fbrow"][0:1, :],
                                                start=True, stop=True),
                   deps=[ld_small, tok_or] + xn_toks[-1:] + mlp2_toks[-1:])
        b2_ = p.op("vector", lambda e: e.tensor_copy(fbb[:], psA[:]), deps=[b1_])
        fb_toks = [b2_]

        x0u = lb.sb("x0u", [64, 4096], BF16)
        x1u = lb.sb("x1u", [64, 4096], BF16)
        vu = lb.sb("vu", [64, 4096], BF16)
        upad = lb.sb("upad", [64, 4098], BF16)
        vtmf = lb.sb("vtmf", [128, 8192], BF16)
        vtm3 = vtmf[:, 0:4096].rearrange("p (a b c) -> p a b c", a=32, b=2)
        B2f = lb.sb("B2f", [128, 4096], BF16)
        B2 = B2f[:].rearrange("p (a b) -> p a b", a=32)
        Pw = B2f[:].rearrange("p (a b) -> p a b", a=64)
        HAf = lb.sb("HAf", [128, 8192], BF16)
        Hview = HAf[:].rearrange("p (c q) -> p c q", c=64)
        Ablkv = HAf[:].rearrange("p (pr ri hf q) -> p pr ri hf q", pr=32, ri=2, hf=2)
        XVr = lb.sb("XVr", [128, 64, 64], BF16)
        XVi = lb.sb("XVi", [128, 64, 64], BF16)
        XKr = lb.sb("XKr", [128, 64, 64], BF16)
        XKi = lb.sb("XKi", [128, 64, 64], BF16)
        Ft = [lb.sb(f"Ft{i}", [128, 3, 128], BF16) for i in range(2)]
        Gt = [lb.sb(f"Gt{i}", [128, 3, 128], BF16) for i in range(2)]
        ycv = lb.sb("ycv", [128, 32, 64], BF16)
        x0tm = lb.sb("x0tm", [128, 32, 64], BF16)

        zt = p.op("vector", lambda e: e.memset(vtmf[:], 0.0), deps=mlp2_toks[-1:])


        def fwd_fft(src, src_toks, XR, XI, guards):
            toks = list(guards)
            t1s = []
            prev_t1 = None
            for J in range(32):
                war = [prev_t1] if prev_t1 else []
                tr = p.op("tensor", (lambda _J: lambda e: e.transpose(
                    psT[0:128, 0:64], src[:, _J * 128:(_J + 1) * 128], identbf[0:64, 0:64]))(J),
                    deps=src_toks + [tok_id] + war + toks)
                ev = p.op("vector", (lambda _J: lambda e: e.tensor_copy(
                    vtm3[:, :, :, _J], psT[0:128, 0:64]))(J), deps=[tr] + guards)
                prev_t1 = ev
                t1s.append(ev)
            t2s = []
            for pr in range(32):
                inap = vtm3[:, pr, :, :].rearrange("p a b -> p (a b)")
                tr = p.op("tensor", (lambda _in: lambda e: e.transpose(
                    psT[:], _in, identbf[:]))(inap), deps=t1s + [tok_id] + t2s[-1:])
                ev = p.op("vector", (lambda _pr: lambda e: e.tensor_copy(
                    B2[:, _pr, :], psT[:]))(pr), deps=[tr])
                t2s.append(ev)
            s1s = []
            for ch in range(64):
                pr, hf = ch // 2, ch % 2
                mm = p.op("tensor", (lambda _pr, _hf: lambda e: e.matmul(
                    psS[:], B2[_hf * 64:(_hf + 1) * 64, _pr, :],
                    w64dup[_hf * 64:(_hf + 1) * 64, :], start=True, stop=True))(pr, hf),
                    deps=t2s + [tok_w64] + s1s[-1:])
                ev = p.op("vector", (lambda _ch: lambda e: e.tensor_copy(
                    Hview[:, _ch, :], psS[:]))(ch), deps=[mm])
                s1s.append(ev)
            prev_ld = {}
            prev_x = []
            for q in range(64):
                i = q % 2
                wl = [prev_ld[i]] if i in prev_ld else []
                ldf = p.dma("sync", [(Ft[i][:], Ftab_d[:, q, :, :])],
                            deps=wl + (list(guards) if q < 2 else []))
                Hr = Hview[:, :, q]
                Hi = Hview[:, :, 64 + q]
                mm1 = p.op("tensor", (lambda _i, _Hr: lambda e: e.matmul(
                    psXr[:], Ft[_i][:, 0, :], _Hr, start=True, stop=False))(i, Hr),
                    deps=[ldf] + s1s + prev_x)
                mm2 = p.op("tensor", (lambda _i, _Hi: lambda e: e.matmul(
                    psXr[:], Ft[_i][:, 2, :], _Hi, start=False, stop=True))(i, Hi), deps=[])
                mm3 = p.op("tensor", (lambda _i, _Hi: lambda e: e.matmul(
                    psXi[:], Ft[_i][:, 0, :], _Hi, start=True, stop=False))(i, Hi), deps=[])
                mm4 = p.op("tensor", (lambda _i, _Hr: lambda e: e.matmul(
                    psXi[:], Ft[_i][:, 1, :], _Hr, start=False, stop=True))(i, Hr), deps=[])
                e1 = p.op("vector", (lambda _q: lambda e: e.tensor_copy(XR[:, _q, :], psXr[:]))(q),
                          deps=[mm2])
                e2 = p.op("vector", (lambda _q: lambda e: e.tensor_copy(XI[:, _q, :], psXi[:]))(q),
                          deps=[mm4])
                prev_ld[i] = e2
                prev_x = [e1, e2]
            return prev_x

        st_toks = []
        prev_chg = [zt] + fb_toks
        for chg in range(8):
            ld_win = p.dma("gpsimd", [
                (wina[:], wina_d[:, chg, :].rearrange("(c p) n -> p c n", p=128)),
                (winb[:], winb_d[:, chg, :].rearrange("(c p) n -> p c n", p=128))],
                deps=prev_chg)
            u_toks = []
            for tcc in range(8):
                mmsA = []
                for kc in range(8):
                    tA = p.op("tensor", (lambda _kc, _tcc, _chg: lambda e: e.matmul(
                        psA[:], wina[:, _kc, :], xnT[:, _kc, _tcc * 512:(_tcc + 1) * 512],
                        start=(_kc == 0), stop=(_kc == 7)))(kc, tcc, chg),
                        deps=([ld_win] + xn_toks + prev_chg + u_toks[-1:]) if kc == 0 else [])
                    mmsA.append(tA)
                mmsB = []
                for kc in range(8):
                    tB = p.op("tensor", (lambda _kc, _tcc, _chg: lambda e: e.matmul(
                        psB[:], winb[:, _kc, :], xnT[:, _kc, _tcc * 512:(_tcc + 1) * 512],
                        start=(_kc == 0), stop=(_kc == 7)))(kc, tcc, chg),
                        deps=[mmsA[-1]] + u_toks[-1:])
                    mmsB.append(tB)
                eA0 = p.op("vector", (lambda _tcc, _chg: lambda e: e.tensor_scalar_add(
                    x0u[:, _tcc * 512:(_tcc + 1) * 512], psA[0:64, :],
                    smalls["bia"][0:64, _chg:_chg + 1]))(tcc, chg),
                    deps=[mmsA[-1], ld_small] + prev_chg)
                eA1 = p.op("vector", (lambda _tcc, _chg: lambda e: e.tensor_scalar_add(
                    x1u[:, _tcc * 512:(_tcc + 1) * 512], psA[64:128, :],
                    smalls["bia"][64:128, _chg:_chg + 1]))(tcc, chg), deps=[mmsA[-1]] + prev_chg)
                eB = p.op("vector", (lambda _tcc, _chg: lambda e: e.tensor_scalar_add(
                    vu[:, _tcc * 512:(_tcc + 1) * 512], psB[:, :],
                    smalls["bib"][:, _chg:_chg + 1]))(tcc, chg), deps=[mmsB[-1]] + prev_chg)
                u_toks += [eA0, eA1, eB]
            sc_toks = []
            for (buf, wnm, bnm, r0) in ((x0u, "shw0", "shb0", 0), (x1u, "shw1", "shb1", 0),
                                        (vu, "shwb", "shbb", 0)):
                rs = slice(0, 64)
                z1 = p.op("vector", lambda e: e.memset(upad[:, 0:2], 0.0),
                          deps=u_toks + sc_toks[-1:])
                z2 = p.op("vector", (lambda _b: lambda e: e.tensor_copy(upad[:, 2:4098], _b[:]))(buf),
                          deps=[z1] + u_toks)
                wsl = smalls[wnm]
                bsl = smalls[bnm]
                c1 = p.op("vector", (lambda _b, _w, _chg, _r: lambda e: e.tensor_scalar_mul(
                    _b[:], upad[:, 0:4096], _w[_r, _chg, 0:1]))(buf, wsl, chg, rs), deps=[z2])
                c2 = p.op("vector", (lambda _b, _w, _chg, _r: lambda e: e.scalar_tensor_tensor(
                    _b[:], upad[:, 1:4097], _w[_r, _chg, 1:2], _b[:], ALU.mult, ALU.add))(
                    buf, wsl, chg, rs), deps=[c1])
                c3 = p.op("vector", (lambda _b, _w, _chg, _r: lambda e: e.scalar_tensor_tensor(
                    _b[:], upad[:, 2:4098], _w[_r, _chg, 2:3], _b[:], ALU.mult, ALU.add))(
                    buf, wsl, chg, rs), deps=[c2])
                c4 = p.op("vector", (lambda _b, _bs, _chg, _r: lambda e: e.tensor_scalar_add(
                    _b[:], _b[:], _bs[_r, _chg:_chg + 1]))(buf, bsl, chg, rs), deps=[c3])
                sc_toks.append(c4)
            g1 = p.op("vector", lambda e: e.tensor_mul(vu[:], vu[:], x1u[:]), deps=sc_toks)
            d1 = p.op("scalar", (lambda _chg: lambda e: e.activation(
                upad[:, 0:4096], trows[:], AF.Exp, scale=smalls["ndel"][:, _chg:_chg + 1]))(chg),
                deps=[tok_tr, ld_small] + prev_chg + sc_toks + [g1])
            k_toks = []
            prev_s2 = {}
            for tcc in range(8):
                s2i = tcc % 2
                sl = [prev_s2[s2i]] if s2i in prev_s2 else []
                ld_s2 = p.dma("sync", [(s2b[s2i][:], s2_dram[:, tcc * 512:(tcc + 1) * 512])],
                              deps=sl + mlp2_toks)
                m = p.op("tensor", (lambda _tcc, _chg, _s2i: lambda e: e.matmul(
                    psB[:], smalls["W3r"][:, _chg, :], s2b[_s2i][:],
                    start=True, stop=True))(tcc, chg, s2i),
                    deps=[ld_s2, ld_small] + k_toks[-1:] + u_toks + [g1])
                prev_s2[s2i] = m
                ev = p.op("vector", (lambda _tcc, _chg: lambda e: e.tensor_scalar(
                    x1u[:, _tcc * 512:(_tcc + 1) * 512], psB[:], -1.0,
                    smalls["b3c"][:, _chg:_chg + 1], ALU.mult, ALU.add))(tcc, chg), deps=[m])
                k_toks.append(ev)
            k2 = p.op("vector", lambda e: e.tensor_mul(x1u[:], x1u[:], upad[:, 0:4096]),
                      deps=k_toks + [d1])
            z0 = p.op("vector", lambda e: e.memset(vtmf[:], 0.0), deps=prev_chg + [g1] + k_toks + [k2])
            xv_done = fwd_fft(vu, [g1, z0], XVr, XVi, prev_chg)
            xk_done = fwd_fft(x1u, [k2], XKr, XKi, xv_done)
            pw1 = p.op("vector", lambda e: e.tensor_mul(Pw[:], XVr[:], XKr[:]),
                       deps=xv_done + xk_done)
            pw2 = p.op("vector", lambda e: e.tensor_mul(XVr[:], XVr[:], XKi[:]), deps=[pw1])
            pw3 = p.op("vector", lambda e: e.tensor_mul(XKi[:], XVi[:], XKi[:]), deps=[pw2])
            pw4 = p.op("vector", lambda e: e.tensor_mul(XVi[:], XVi[:], XKr[:]), deps=[pw3])
            pw5 = p.op("vector", lambda e: e.tensor_add(Pw[:], Pw[:], XKi[:]), deps=[pw4])
            pw6 = p.op("vector", lambda e: e.tensor_sub(XVi[:], XVi[:], XVr[:]), deps=[pw5])
            pw_done = [pw5, pw6]
            prev_gld = {}
            prev_a = []
            for q in range(64):
                i = q % 2
                wl = [prev_gld[i]] if i in prev_gld else []
                ldg = p.dma("sync", [(Gt[i][:], Gtab_d[:, q, :, :])],
                            deps=wl + (pw_done if q < 2 else []))
                Pr = Pw[:, q, :]
                Pi = XVi[:, q, :]
                mm1 = p.op("tensor", (lambda _i, _P: lambda e: e.matmul(
                    psXr[:], Gt[_i][:, 0, :], _P, start=True, stop=False))(i, Pr),
                    deps=[ldg] + pw_done + prev_a)
                mm2 = p.op("tensor", (lambda _i, _P: lambda e: e.matmul(
                    psXr[:], Gt[_i][:, 2, :], _P, start=False, stop=True))(i, Pi), deps=[])
                mm3 = p.op("tensor", (lambda _i, _P: lambda e: e.matmul(
                    psXi[:], Gt[_i][:, 0, :], _P, start=True, stop=False))(i, Pi), deps=[])
                mm4 = p.op("tensor", (lambda _i, _P: lambda e: e.matmul(
                    psXi[:], Gt[_i][:, 1, :], _P, start=False, stop=True))(i, Pr), deps=[])
                e1 = p.op("vector", (lambda _q: lambda e: e.tensor_copy(
                    Ablkv[:, :, 0, :, _q], psXr[:].rearrange("p (a b) -> p a b", a=32)))(q),
                    deps=[mm2])
                e2 = p.op("vector", (lambda _q: lambda e: e.tensor_copy(
                    Ablkv[:, :, 1, :, _q], psXi[:].rearrange("p (a b) -> p a b", a=32)))(q),
                    deps=[mm4])
                prev_gld[i] = e2
                prev_a = [e1, e2]
            atr_toks = []
            prev_at = None
            for pr in range(32):
                for ri in range(2):
                    inap = Ablkv[:, pr, ri, :, :].rearrange("p a b -> p (a b)")
                    war = [prev_at] if prev_at else []
                    tr = p.op("tensor", (lambda _in: lambda e: e.transpose(
                        psT[:], _in, identbf[:]))(inap), deps=prev_a + [tok_id] + war)
                    ev = p.op("vector", (lambda _pr, _ri: lambda e: e.tensor_copy(
                        vtm3[:, _pr, _ri, :].rearrange("p (a b) -> p a b", a=2) if False else
                        vtmf[:, (2 * _pr + _ri) * 128:(2 * _pr + _ri + 1) * 128], psT[:]))(pr, ri),
                        deps=[tr])
                    prev_at = ev
                    atr_toks.append(ev)
            i2_toks = []
            for ch in range(64):
                pr, hf = ch // 2, ch % 2
                mm1 = p.op("tensor", (lambda _pr, _hf: lambda e: e.matmul(
                    psI2[:], vtmf[_hf * 64:(_hf + 1) * 64, (2 * _pr) * 128:(2 * _pr + 1) * 128],
                    w64idup[_hf * 64:(_hf + 1) * 64, 0:32], start=True, stop=False))(pr, hf),
                    deps=atr_toks + [tok_w64i] + i2_toks[-1:])
                mm2 = p.op("tensor", (lambda _pr, _hf: lambda e: e.matmul(
                    psI2[:], vtmf[_hf * 64:(_hf + 1) * 64, (2 * _pr + 1) * 128:(2 * _pr + 2) * 128],
                    w64idup[_hf * 64:(_hf + 1) * 64, 32:64], start=False, stop=True))(pr, hf),
                    deps=[])
                ev = p.op("vector", (lambda _ch, _chg: lambda e: e.tensor_scalar_add(
                    ycv[:, :, _ch], psI2[:], fbb[:, _chg * 64 + _ch:_chg * 64 + _ch + 1]))(ch, chg),
                    deps=[mm2] + fb_toks)
                i2_toks.append(ev)
            x0_toks = []
            prev_x0 = None
            for J in range(32):
                war = [prev_x0] if prev_x0 else []
                tr = p.op("tensor", (lambda _J: lambda e: e.transpose(
                    psT[0:128, 0:64], x0u[:, _J * 128:(_J + 1) * 128], identbf[0:64, 0:64]))(J),
                    deps=sc_toks + [tok_id] + war + i2_toks[-1:])
                ev = p.op("vector", (lambda _J: lambda e: e.tensor_copy(
                    x0tm[:, _J, :], psT[0:128, 0:64]))(J), deps=[tr] + st_toks[-1:])
                prev_x0 = ev
                x0_toks.append(ev)
            if globals().get("L2_TAP7", False) and chg == 7:
                tapy = lb.dram_out("L2_ycv7", [128, 32, 64], BF16)
                tapx = lb.dram_out("L2_x0tm7", [128, 32, 64], BF16)
                tp = p.dma("sync", [(tapy, ycv[:]), (tapx, x0tm[:])],
                           deps=x0_toks + i2_toks)
                x0_toks = x0_toks + [tp]
            gt = p.op("vector", lambda e: e.tensor_mul(ycv[:], ycv[:], x0tm[:]),
                      deps=x0_toks + i2_toks)
            yt_last = None
            prev_sty = {}
            for beta in range(32):
                bi = beta % 2
                war = [yt_last] if yt_last else []
                trY = p.op("tensor", (lambda _b: lambda e: e.transpose(
                    psT[0:64, 0:128], ycv[:, _b, :], identbf[:]))(beta),
                    deps=[gt, tok_id] + war + st_toks[-1:])
                evY = p.op("vector", (lambda _bi: lambda e: e.tensor_copy(
                    ystg[_bi][:], psT[0:64, 0:128]))(bi),
                    deps=[trY] + ([prev_sty[bi]] if bi in prev_sty else []))
                yt_last = evY
                sty = p.dma("sync", [(E["yt_dram"][chg * 64:(chg + 1) * 64,
                                                   beta * 128:(beta + 1) * 128],
                                      ystg[bi][:])], deps=[evY])
                prev_sty[bi] = sty
                st_toks.append(sty)
            prev_chg = [st_toks[-1]]

    wout_sb = lb.sb("wout_sb", [128, 4, 1024], BF16)
    ld_wo2 = p.dma("gpsimd", [(wout_sb[:], E["wout_own"].rearrange(
        "(g p) n -> p g n", p=128))], deps=st_toks)
    psW2 = lb.ps("psW2", [128, 512], F32)
    aev = [lb.sb(f"l2aev{i}", [128, 1024], BF16) for i in range(2)]
    ytb = [lb.sb(f"l2ytb{i}", [128, 4, 128], BF16) for i in range(2)]
    ytv = E["yt_dram"].rearrange("(g p) t -> p g t", p=128)
    rs_in = E["rs_in"]
    wo_toks = []
    prev_ev2 = {}
    prev_ldy = {}
    prev_mm2 = list(st_toks)
    for tb in range(32):
        i = tb % 2
        dly = [prev_ldy[i]] if i in prev_ldy else []
        ldy = p.dma("sync", [(ytb[i][:], ytv[:, :, tb * 128:(tb + 1) * 128])],
                    deps=dly + st_toks)
        evd = [prev_ev2[i]] if i in prev_ev2 else []
        last_mm = None
        for ob in range(2):
            mms = []
            for kc in range(4):
                t = p.op("tensor", (lambda _kc, _i, _ob: lambda e: e.matmul(
                    psW2[:], ytb[_i][:, _kc, :],
                    wout_sb[:, _kc, _ob * 512:(_ob + 1) * 512],
                    start=(_kc == 0), stop=(_kc == 3)))(kc, i, ob),
                    deps=([ld_wo2, ldy] + prev_mm2) if kc == 0 else [])
                mms.append(t)
            ev = p.op("vector", (lambda _i, _ob: lambda e: e.tensor_copy(
                aev[_i][:, _ob * 512:(_ob + 1) * 512], psW2[:]))(i, ob),
                deps=[mms[-1]] + evd)
            prev_mm2 = [ev]
            evd = [ev]
            last_mm = ev
        prev_ldy[i] = last_mm
        st = p.dma("sync", [(rs_in[tb * 128:(tb + 1) * 128, :], aev[i][:])],
                   deps=evd)
        prev_ev2[i] = st
        wo_toks.append(st)
    return wo_toks

def emit_l2b(lb, E):
    nc, p = lb.nc, lb.p
    x = E["x"]
    wf_d, a1_d, a2_d = E["wf"], E["a1"], E["a2"]
    bf2_d, boutr_d = E["bf2"], E["bout_row"]
    o_x3 = E["x_out"]
    xnT = E["xnT"]

    identbf, tok_id = lb.const("identity_bf", np.eye(128, dtype=np.float32).astype(BF))
    ones_row, tok_or = lb.const("ones_row_f32", np.ones((1, 128), np.float32))
    hA = lb.sb("hA", [128, 8, 2048], BF16)
    hB = lb.sb("hB", [128, 8, 2048], BF16)
    bsb = lb.sb("bsb", [128, 2, 8], F32)
    brow = lb.sb("brow", [1, 1024], F32)
    boutb = lb.sb("boutb", [128, 1024], F32)
    wt = [lb.sb(f"wt{i}", [128, 8, 128], BF16) for i in range(2)]
    psW = lb.ps("psW", [128, 512], F32)
    psT = lb.ps("psT", [128, 128], BF16)
    psbf = [lb.ps(f"psbf{i}", [128, 128], BF16) for i in range(2)]

    ld_xn = p.op("vector", lambda e: e.engine_nop(), deps=E["xn_toks"])
    ld_b = p.dma("sync", [(bsb[:, 1, :], bf2_d), (brow[:], boutr_d)])
    bb1 = p.op("tensor", lambda e: e.matmul(psW[:], ones_row[:], brow[0:1, 0:512],
                                            start=True, stop=True), deps=[ld_b, tok_or])
    bb2 = p.op("vector", lambda e: e.tensor_copy(boutb[:, 0:512], psW[:]), deps=[bb1])
    bb3 = p.op("tensor", lambda e: e.matmul(psW[:], ones_row[:], brow[0:1, 512:1024],
                                            start=True, stop=True), deps=[bb2])
    bb4 = p.op("vector", lambda e: e.tensor_copy(boutb[:, 512:1024], psW[:]), deps=[bb3])
    bias_toks = [bb4]

    def matmul_fm(w_dram, src, src_toks, dst, act=None, bias_idx=None, guards=()):
        toks = []
        prev_w = {}
        prev_ev = list(guards)
        for oc in range(8):
            i = oc % 2
            wl = [prev_w[i]] if i in prev_w else []
            ldw = p.dma("gpsimd", [(wt[i][:], w_dram[:, oc * 128:(oc + 1) * 128]
                                    .rearrange("(c p) n -> p c n", p=128))], deps=wl + list(guards))
            for tc in range(4):
                mms = []
                for kc in range(8):
                    t = p.op("tensor", (lambda _kc, _tc, _i, _s: lambda e: e.matmul(
                        psW[:], wt[_i][:, _kc, :], _s[:, _kc, _tc * 512:(_tc + 1) * 512],
                        start=(_kc == 0), stop=(_kc == 7)))(kc, tc, i, src),
                        deps=([ldw] + src_toks + prev_ev) if kc == 0 else [])
                    mms.append(t)
                if act == "tanh":
                    ev = p.op("scalar", (lambda _oc, _tc, _d: lambda e: e.activation(
                        _d[:, _oc, _tc * 512:(_tc + 1) * 512], psW[:], AF.Tanh,
                        bias=bsb[:, 1, _oc:_oc + 1]))(oc, tc, dst), deps=[mms[-1], ld_b])
                elif bias_idx is not None:
                    ev = p.op("vector", (lambda _oc, _tc, _d, _bi: lambda e: e.tensor_scalar_add(
                        _d[:, _oc, _tc * 512:(_tc + 1) * 512], psW[:],
                        bsb[:, _bi, _oc:_oc + 1]))(oc, tc, dst, bias_idx), deps=[mms[-1], ld_b])
                else:
                    ev = p.op("vector", (lambda _oc, _tc, _d: lambda e: e.tensor_copy(
                        _d[:, _oc, _tc * 512:(_tc + 1) * 512], psW[:]))(oc, tc, dst),
                        deps=[mms[-1]])
                prev_ev = [ev]
                toks.append(ev)
            prev_w[i] = prev_ev[0]
        return toks

    L2B_PART = globals().get("L2B_PART", 4)
    if L2B_PART >= 1:
        h1_toks = matmul_fm(wf_d, xnT, [ld_xn], hA, guards=bias_toks)
    else:
        h1_toks = [p.op("vector", lambda e: e.memset(hA[:], 0.0), deps=bias_toks)]
    if L2B_PART >= 2:
        h2_toks = matmul_fm(a1_d, hA, h1_toks, hB, guards=h1_toks[-1:])
    else:
        h2_toks = h1_toks
    if L2B_PART >= 4:
        h3_toks = matmul_fm(a2_d, hB, h2_toks, hA, act="tanh", guards=h2_toks)
    elif L2B_PART == 3:
        h3_toks = matmul_fm(a2_d, hB, h2_toks, hA, bias_idx=1, guards=h2_toks)
    else:
        h3_toks = h2_toks

    xtile = [lb.sb(f"rx{i}", [128, 1024], F32) for i in range(2)]
    atile = [lb.sb(f"ra{i}", [128, 1024], F32) for i in range(2)]
    otile = [lb.sb(f"ro{i}", [128, 1024], F32) for i in range(2)]
    ttmp = lb.sb("ttmp", [128, 128], F32)
    st_toks = []
    prev_st = {}
    prev_trr = {}
    prev_tt = None
    for tb in range(16):
        i = tb % 2
        dl = [prev_st[i]] if i in prev_st else []
        ld = p.dma("sync", [(xtile[i][:], x[tb * 128:(tb + 1) * 128, :])],
                   deps=dl + E["rs_tok"])
        lda2 = p.dma("gpsimd", [(atile[i][:], E["rs_out"][tb * 128:(tb + 1) * 128, :])],
                     deps=dl + E["rs_tok"])
        adds = []
        for dc in range(8):
            pb = psbf[dc % 2]
            war = [prev_trr[dc % 2]] if (dc % 2) in prev_trr else []
            tr = p.op("tensor", (lambda _dc, _tb, _pb: lambda e: e.transpose(
                _pb[:], hA[:, _dc, _tb * 128:(_tb + 1) * 128], identbf[:]))(dc, tb, pb),
                deps=h3_toks + [tok_id] + war)
            cp = p.op("vector", (lambda _pb: lambda e: e.tensor_copy(ttmp[:], _pb[:]))(pb),
                      deps=[tr] + ([prev_tt] if prev_tt is not None else []))
            ad = p.op("vector", (lambda _dc, _i: lambda e: e.tensor_add(
                otile[_i][:, _dc * 128:(_dc + 1) * 128],
                xtile[_i][:, _dc * 128:(_dc + 1) * 128], ttmp[:]))(dc, i),
                deps=[cp, ld] + dl)
            prev_trr[dc % 2] = cp
            prev_tt = ad
            adds.append(ad)
        a1t = p.op("vector", (lambda _i: lambda e: e.tensor_add(
            otile[_i][:], otile[_i][:], atile[_i][:]))(i), deps=adds + [ld, lda2])
        a2t = p.op("vector", (lambda _i: lambda e: e.tensor_add(
            otile[_i][:], otile[_i][:], boutb[:]))(i), deps=[a1t] + bias_toks)
        st = p.dma("sync", [(o_x3[tb * 128:(tb + 1) * 128, :], otile[i][:])],
                   deps=[a2t])
        prev_st[i] = st
        st_toks.append(st)
    return st_toks

def emit_l3a(lb, E):
    nc, p = lb.nc, lb.p
    wk_d, wv_d, wr_d = E["wk"], E["wv"], E["wr"]
    td_d, tf_d = E["td"], E["tf"]

    identbf, tok_id = lb.const("identity_bf", np.eye(128, dtype=np.float32).astype(BF))
    jr = np.arange(16, dtype=np.float32)
    c_j7, tok_j7 = lb.const("jrow7", np.broadcast_to((jr - 7.5)[None, :], (64, 16)).copy())
    c_j7n, tok_j7n = lb.const("jrow7n", np.broadcast_to((7.5 - jr)[None, :], (64, 16)).copy())
    c_d16, tok_d16 = lb.const("d16row", np.broadcast_to(
        (16.0 * np.arange(8, dtype=np.float32))[None, :], (64, 8)).copy())
    c_tau, tok_tau = lb.const("taurow", np.broadcast_to(
        (np.arange(128, dtype=np.float32) + 1.0)[None, :], (64, 128)).copy())
    c_k127, tok_k127 = lb.const("krow127", np.broadcast_to(
        (127.0 - np.arange(128, dtype=np.float32))[None, :], (64, 128)).copy())
    c_128, tok_c128 = lb.const("c128col", np.full((64, 1), 128.0, np.float32))
    ii = np.arange(128)[:, None]
    jj = np.arange(128)[None, :]
    masks = []
    for d in range(8):
        m = ((ii // 16 - jj // 16) == d) & (ii >= jj)
        mk, tokm = lb.const(f"smask{d}", m.astype(np.uint8), dt=U8)
        masks.append((mk, tokm))
    zeroc, tok_z = lb.const("zero128", np.zeros((128, 128), np.float32))

    xnT, xn_toks = E["xnT"], E["xn_toks"]
    ypair = lb.sb("ypair", [64, 2, 4096], BF16)
    att_done_all = []
    tdsb = lb.sb("tdsb", [64, 8], F32)
    tfsb = lb.sb("tfsb", [64, 8], F32)
    ld_tdf = p.dma("sync", [(tdsb[:], td_d), (tfsb[:], tf_d)])

    kTB = lb.sb("kTB", [64, 2, 4096], BF16)
    rrA = lb.sb("rrA", [64, 2, 4096], BF16)
    Rtil = lb.sb("Rtil", [64, 2, 4096], BF16)
    vtok = lb.sb("vtok", [128, 32, 128], BF16)
    ktok = lb.sb("ktok", [128, 32, 128], BF16)
    wq_t = lb.sb("wq_t", [128, 8, 128], BF16)
    wk_t = lb.sb("wk_t", [128, 8, 128], BF16)
    wv_t = lb.sb("wv_t", [128, 8, 128], BF16)

    lnD = lb.sb("lnD", [64, 2], F32)
    P7 = lb.sb("P7", [64, 2, 16], BF16)
    Pb7 = lb.sb("Pb7", [64, 2, 16], BF16)
    D16 = lb.sb("D16", [64, 2, 8], F32)
    Ptau = lb.sb("Ptau", [64, 2, 128], BF16)
    Pkfm = lb.sb("Pkfm", [64, 128], F32)
    Pktok = lb.sb("Pktok", [128, 2, 64], BF16)
    D128 = lb.sb("D128", [64, 2], F32)
    St = lb.sb("St", [64, 64], F32)
    Stb = lb.sb("Stb", [64, 64], BF16)
    Bd2 = [lb.sb(f"Bd{i}", [64, 128], BF16) for i in range(2)]
    Sacc = lb.sb("Sacc", [128, 128], F32)
    Sbf = lb.sb("Sbf", [128, 128], BF16)
    PT = lb.sb("PT", [128, 128], BF16)

    psA = lb.ps("psA", [128, 512], F32)
    psV = lb.ps("psV", [128, 128], F32)
    psY = lb.ps("psY", [64, 128], F32)
    psS = lb.ps("psS", [128, 128], F32)
    psC = lb.ps("psC", [64, 64], F32)
    psT = lb.ps("psT", [128, 128], BF16)
    psTf = lb.ps("psTf", [128, 64], F32)

    def proj_tok(wtile, wtok, dst, scale_col=None, guards=()):
        toks = []
        prev = list(guards)
        for tb in range(32):
            mms = []
            for kc in range(8):
                t = p.op("tensor", (lambda _kc, _tb: lambda e: e.matmul(
                    psV[:], xnT[:, _kc, _tb * 128:(_tb + 1) * 128], wtile[:, _kc, :],
                    start=(_kc == 0), stop=(_kc == 7)))(kc, tb),
                    deps=([wtok] + xn_toks + prev) if kc == 0 else [])
                mms.append(t)
            ev = p.op("vector", (lambda _tb: lambda e: e.tensor_copy(
                dst[:, _tb, :], psV[:]))(tb), deps=[mms[-1]])
            prev = [ev]
            toks.append(ev)
        return toks

    carry_state = []
    pair_guard = []
    for hp in range(4):
        csl = slice(hp * 128, (hp + 1) * 128)
        ldw = p.dma("gpsimd", [
            (wk_t[:], wk_d[:, csl].rearrange("(c p) n -> p c n", p=128)),
            (wv_t[:], wv_d[:, csl].rearrange("(c p) n -> p c n", p=128)),
            (wq_t[:], wr_d[:, csl].rearrange("(c p) n -> p c n", p=128)),
        ], deps=pair_guard)
        k_toks = []
        r_toks = []
        prev = list(pair_guard)
        for h2 in range(2):
            hh = hp * 2 + h2
            for tc in range(8):
                mms = []
                for kc in range(8):
                    t = p.op("tensor", (lambda _kc, _tc, _h2: lambda e: e.matmul(
                        psA[:, 0:512], wk_t[:, _kc, _h2 * 64:(_h2 + 1) * 64].rearrange("p n -> p n")
                        if False else wk_t[:, _kc, :],
                        xnT[:, _kc, _tc * 512:(_tc + 1) * 512],
                        start=(_kc == 0), stop=(_kc == 7)))(kc, tc, h2),
                        deps=([ldw, ld_tdf] + xn_toks + prev) if kc == 0 else [])
                    mms.append(t)
                if h2 == 0 and tc < 8:
                    pass
                e1 = p.op("vector", (lambda _tc, _hp: lambda e: e.tensor_scalar_mul(
                    kTB[:, 0, _tc * 512:(_tc + 1) * 512], psA[0:64, :],
                    tfsb[:, 2 * _hp:2 * _hp + 1]))(tc, hp), deps=[mms[-1], ld_tdf])
                e2 = p.op("vector", (lambda _tc, _hp: lambda e: e.tensor_scalar_mul(
                    kTB[:, 1, _tc * 512:(_tc + 1) * 512], psA[64:128, :],
                    tfsb[:, 2 * _hp + 1:2 * _hp + 2]))(tc, hp), deps=[mms[-1]])
                prev = [e1, e2]
                k_toks += [e1, e2]
            break
        for tc in range(8):
            mms = []
            for kc in range(8):
                t = p.op("tensor", (lambda _kc, _tc: lambda e: e.matmul(
                    psA[:, 0:512], wq_t[:, _kc, :],
                    xnT[:, _kc, _tc * 512:(_tc + 1) * 512],
                    start=(_kc == 0), stop=(_kc == 7)))(kc, tc),
                    deps=([ldw] + xn_toks + prev) if kc == 0 else [])
                mms.append(t)
            e1 = p.op("scalar", (lambda _tc: lambda e: e.activation(
                rrA[:, 0, _tc * 512:(_tc + 1) * 512], psA[0:64, :], AF.Sigmoid))(tc),
                deps=[mms[-1]] + prev)
            e2 = p.op("scalar", (lambda _tc: lambda e: e.activation(
                rrA[:, 1, _tc * 512:(_tc + 1) * 512], psA[64:128, :], AF.Sigmoid))(tc),
                deps=[mms[-1]] + prev)
            prev = [e1, e2]
            r_toks += [e1, e2]
        vt_toks = proj_tok(wv_t, ldw, vtok, guards=prev)
        kt_toks = proj_tok(wk_t, ldw, ktok, guards=vt_toks[-1:])
        tbl_toks = []
        for h2 in range(2):
            hh = hp * 2 + h2
            l1 = p.op("scalar", (lambda _h2, _hh: lambda e: e.activation(
                lnD[:, _h2:_h2 + 1], tdsb[:, _hh:_hh + 1], AF.Ln))(h2, hh),
                deps=[ld_tdf] + pair_guard + tbl_toks)
            a1 = p.op("scalar", (lambda _h2: lambda e: e.activation(
                P7[:, _h2, :], c_j7[:], AF.Exp, scale=lnD[:, _h2:_h2 + 1]))(h2),
                deps=[l1, tok_j7])
            a2 = p.op("scalar", (lambda _h2: lambda e: e.activation(
                Pb7[:, _h2, :], c_j7n[:], AF.Exp, scale=lnD[:, _h2:_h2 + 1]))(h2),
                deps=[l1, tok_j7n])
            a3 = p.op("scalar", (lambda _h2: lambda e: e.activation(
                D16[:, _h2, :], c_d16[:], AF.Exp, scale=lnD[:, _h2:_h2 + 1]))(h2),
                deps=[l1, tok_d16])
            a4 = p.op("scalar", (lambda _h2: lambda e: e.activation(
                Ptau[:, _h2, :], c_tau[:], AF.Exp, scale=lnD[:, _h2:_h2 + 1]))(h2),
                deps=[l1, tok_tau])
            a5 = p.op("scalar", (lambda _h2: lambda e: e.activation(
                Pkfm[:], c_k127[:], AF.Exp, scale=lnD[:, _h2:_h2 + 1]))(h2),
                deps=[l1, tok_k127] + tbl_toks)
            a6 = p.op("scalar", (lambda _h2: lambda e: e.activation(
                D128[:, _h2:_h2 + 1], c_128[:], AF.Exp, scale=lnD[:, _h2:_h2 + 1]))(h2),
                deps=[l1, tok_c128])
            pkb = p.op("vector", (lambda: lambda e: e.tensor_copy(Bd2[0][:], Pkfm[:]))(),
                       deps=[a5] + carry_state)
            trp = p.op("tensor", (lambda: lambda e: e.transpose(
                psT[:, 0:64], Bd2[0][:], identbf[0:64, 0:64]))(), deps=[pkb, tok_id] + tbl_toks)
            evp = p.op("vector", (lambda _h2: lambda e: e.tensor_copy(
                Pktok[:, _h2, :], psT[:, 0:64]))(h2), deps=[trp])
            tbl_toks += [a1, a2, a3, a4, a6, evp]
        sc_toks = []
        ap_P7 = [None, None]
        for h2 in range(2):
            pa = Ptau[0:64, h2, :]
            rt = p.op("vector", (lambda _h2, _a: lambda e: e.tensor_mul(
                Rtil[:, _h2, :], rrA[:, _h2, :],
                bass.AP(_a.tensor, _a.offset, [_a.ap[0], [0, 32], [1, 128]])))(h2, pa),
                deps=r_toks + tbl_toks + pair_guard)
            p7a = P7[0:64, h2, :]
            ra = p.op("vector", (lambda _h2, _a: lambda e: e.tensor_mul(
                rrA[:, _h2, :], rrA[:, _h2, :],
                bass.AP(_a.tensor, _a.offset, [_a.ap[0], [0, 256], [1, 16]])))(h2, p7a),
                deps=[rt] + r_toks + tbl_toks)
            pb = Pb7[0:64, h2, :]
            kb = p.op("vector", (lambda _h2, _a: lambda e: e.tensor_mul(
                kTB[:, _h2, :], kTB[:, _h2, :],
                bass.AP(_a.tensor, _a.offset, [_a.ap[0], [0, 256], [1, 16]])))(h2, pb),
                deps=k_toks + tbl_toks + [ra])
            pk = Pktok[0:128, h2, :]
            kt = p.op("vector", (lambda _h2, _a: lambda e: e.tensor_mul(
                ktok[:, :, _h2 * 64:(_h2 + 1) * 64], ktok[:, :, _h2 * 64:(_h2 + 1) * 64],
                bass.AP(_a.tensor, _a.offset, [_a.ap[0], [0, 32], [1, 64]])))(h2, pk),
                deps=kt_toks + tbl_toks + [kb])
            sc_toks += [rt, ra, kb, kt]
        for h2 in range(2):
            z1 = p.op("vector", lambda e: e.memset(St[:], 0.0),
                      deps=sc_toks + pair_guard + carry_state)
            prev_y = None
            prev_state = [z1]
            for ch in range(32):
                csl2 = slice(ch * 128, (ch + 1) * 128)
                sb1 = p.op("scalar", lambda e: e.activation(Stb[:], St[:], AF.Copy),
                           deps=prev_state)
                mmI = p.op("tensor", (lambda _h2, _c: lambda e: e.matmul(
                    psY[:], Stb[:], Rtil[:, _h2, _c * 128:(_c + 1) * 128],
                    start=True, stop=False))(h2, ch),
                    deps=[sb1] + ([prev_y] if prev_y else []) + sc_toks)
                sel_prev = None
                prev_mmS = {}
                prev_sel2 = {}
                for d in range(8):
                    di = d % 2
                    _ps = psS if di == 0 else psV
                    bd = p.op("vector", (lambda _h2, _c, _d, _di: lambda e: e.tensor_scalar_mul(
                        Bd2[_di][:], kTB[:, _h2, _c * 128:(_c + 1) * 128],
                        D16[:, _h2, _d:_d + 1]))(h2, ch, d, di),
                        deps=[mmI] + ([prev_mmS[di]] if di in prev_mmS else [])
                        + sc_toks + tbl_toks)
                    mmS = p.op("tensor", (lambda _h2, _c, _di, _p: lambda e: e.matmul(
                        _p[:], rrA[:, _h2, _c * 128:(_c + 1) * 128], Bd2[_di][:],
                        start=True, stop=True))(h2, ch, di, _ps),
                        deps=[bd] + ([prev_sel2[di]] if di in prev_sel2 else []))
                    prev_mmS[di] = mmS
                    sel = p.op("vector", (lambda _d, _p: lambda e: e.select(
                        Sacc[:], masks[_d][0][:], _p[:],
                        zeroc[:] if _d == 0 else Sacc[:], add_drain=True))(d, _ps),
                        deps=[mmS, masks[d][1], tok_z] + ([sel_prev] if sel_prev else []))
                    prev_sel2[di] = sel
                    sel_prev = sel
                cb = p.op("vector", lambda e: e.tensor_copy(Sbf[:], Sacc[:]), deps=[sel_prev])
                trS = p.op("tensor", (lambda: lambda e: e.transpose(
                    psT[:], Sbf[:], identbf[:]))(), deps=[cb, tok_id])
                evS = p.op("vector", lambda e: e.tensor_copy(PT[:], psT[:]), deps=[trS])
                mmP = p.op("tensor", (lambda _h2, _c: lambda e: e.matmul(
                    psY[:], vtok[:, _c, _h2 * 64:(_h2 + 1) * 64], PT[:],
                    start=False, stop=True))(h2, ch), deps=[evS])
                evY = p.op("vector", (lambda _h2, _c: lambda e: e.tensor_copy(
                    ypair[:, _h2, _c * 128:(_c + 1) * 128], psY[:]))(h2, ch),
                    deps=[mmP])
                prev_y = evY
                mmC = p.op("tensor", (lambda _h2, _c: lambda e: e.matmul(
                    psC[:], ktok[:, _c, _h2 * 64:(_h2 + 1) * 64],
                    vtok[:, _c, _h2 * 64:(_h2 + 1) * 64],
                    start=True, stop=True))(h2, ch), deps=[sb1] + sc_toks)
                u1 = p.op("vector", (lambda _h2: lambda e: e.tensor_scalar_mul(
                    St[:], St[:], D128[:, _h2:_h2 + 1]))(h2), deps=[mmI, sb1] + tbl_toks)
                u2 = p.op("vector", lambda e: e.tensor_add(St[:], St[:], psC[:]), deps=[u1, mmC])
                prev_state = [u2]
            carry_state = list(prev_state) + [prev_y]
        sta = p.dma("sync", [(E["att_dram"][:, hp * 2 + h2, :], ypair[:, h2, :])
                             for h2 in range(2)], deps=[prev_y, prev_state[0]])
        pair_guard = [sta]
        att_done_all.append(sta)
    return emit_wo_partial(lb, E, E["att_dram"], att_done_all, nheads=8)

def emit_l3b(lb, E):
    nc, p = lb.nc, lb.p
    x = E["x"]
    qg_d, qu_d, qd_d = E["qgT"], E["quT"], E["qdT"]
    o_x4 = E["x_out"]
    xnT = E["xnT"]

    identbf, tok_id = lb.const("identity_bf", np.eye(128, dtype=np.float32).astype(BF))
    aoutT = lb.sb("aoutT", [128, 8, 2048], BF16)
    ld_xn = p.op("vector", lambda e: e.engine_nop(), deps=E["xn_toks"])
    ld_ao = p.dma("sync", [(aoutT[:], E["rs_out"].rearrange("p (h t) -> p h t", h=8))],
                  deps=E["rs_tok"])
    ao_toks = [ld_ao]

    psU = lb.ps("psU", [128, 512], F32)
    psZ = lb.ps("psZ", [128, 512], F32)
    psbf = [lb.ps(f"psbf{i}", [128, 128], BF16) for i in range(2)]

    wgt = [lb.sb(f"wgt{i}", [128, 8, 128], BF16) for i in range(2)]
    wut = [lb.sb(f"wut{i}", [128, 8, 128], BF16) for i in range(2)]
    wdt = [lb.sb(f"wdt{i}", [128, 24, 128], BF16) for i in range(2)]
    hT = lb.sb("hT", [128, 24, 512], BF16)
    ztmp = lb.sb("ztmp", [128, 512], BF16)
    ffn_toks = []
    prev_wg = {}
    prev_wd = {}
    prev_tc2 = []
    for tc in range(4):
        tsl = slice(tc * 512, (tc + 1) * 512)
        h_toks = []
        for oc in range(24):
            i = oc % 2
            wl = [prev_wg[i]] if i in prev_wg else []
            ldg = p.dma("gpsimd", [
                (wgt[i][:], qg_d[:, oc * 128:(oc + 1) * 128].rearrange("(c p) n -> p c n", p=128)),
                (wut[i][:], qu_d[:, oc * 128:(oc + 1) * 128].rearrange("(c p) n -> p c n", p=128)),
            ], deps=wl)
            mmsG = []
            for kc in range(8):
                t = p.op("tensor", (lambda _kc, _i, _tc: lambda e: e.matmul(
                    psU[:], wgt[_i][:, _kc, :], xnT[:, _kc, _tc * 512:(_tc + 1) * 512],
                    start=(_kc == 0), stop=(_kc == 7)))(kc, i, tc),
                    deps=([ldg, ld_xn] + prev_tc2 + h_toks[-1:]) if kc == 0 else [])
                mmsG.append(t)
            sg = p.op("scalar", (lambda _oc: lambda e: e.activation(
                ztmp[:], psU[:], AF.Silu, scale=1.0 / 127.0))(oc), deps=[mmsG[-1]] + prev_tc2)
            mmsU = []
            for kc in range(8):
                t = p.op("tensor", (lambda _kc, _i, _tc: lambda e: e.matmul(
                    psU[:], wut[_i][:, _kc, :], xnT[:, _kc, _tc * 512:(_tc + 1) * 512],
                    start=(_kc == 0), stop=(_kc == 7)))(kc, i, tc),
                    deps=[sg])
                mmsU.append(t)
            hm = p.op("vector", (lambda _oc: lambda e: e.tensor_mul(
                hT[:, _oc, :], ztmp[:], psU[:]))(oc), deps=[mmsU[-1], sg] + prev_tc2)
            prev_wg[i] = hm
            h_toks.append(hm)
        for zc in range(8):
            i = zc % 2
            wl = [prev_wd[i]] if i in prev_wd else []
            ldd = p.dma("gpsimd", [(wdt[i][:], qd_d[:, zc * 128:(zc + 1) * 128]
                                    .rearrange("(c p) n -> p c n", p=128))], deps=wl)
            mms = []
            for kc in range(24):
                t = p.op("tensor", (lambda _kc, _i: lambda e: e.matmul(
                    psZ[:], wdt[_i][:, _kc, :], hT[:, _kc, :],
                    start=(_kc == 0), stop=(_kc == 23)))(kc, i),
                    deps=([ldd] + h_toks + ffn_toks[-1:]) if kc == 0 else [])
                mms.append(t)
            z1 = p.op("vector", (lambda _zc, _tc: lambda e: e.tensor_scalar_mul(
                ztmp[:], psZ[:], 1.0 / 16129.0))(zc, tc),
                deps=[mms[-1]] + h_toks)
            z2 = p.op("vector", (lambda _zc, _tc: lambda e: e.tensor_add(
                aoutT[:, _zc, _tc * 512:(_tc + 1) * 512],
                aoutT[:, _zc, _tc * 512:(_tc + 1) * 512], ztmp[:]))(zc, tc),
                deps=[z1] + ao_toks)
            prev_wd[i] = z2
            ffn_toks.append(z2)
        prev_tc2 = [ffn_toks[-1]]

    xtile = [lb.sb(f"rx{i}", [128, 1024], F32) for i in range(2)]
    otile = [lb.sb(f"ro{i}", [128, 1024], F32) for i in range(2)]
    ttmp = lb.sb("ttmp", [128, 128], F32)
    st_toks = []
    prev_st = {}
    prev_trr = {}
    prev_tt = None
    for tb in range(16):
        i = tb % 2
        dl = [prev_st[i]] if i in prev_st else []
        ld = p.dma("sync", [(xtile[i][:], x[tb * 128:(tb + 1) * 128, :])], deps=dl)
        adds = []
        for dc in range(8):
            pb = psbf[dc % 2]
            war = [prev_trr[dc % 2]] if (dc % 2) in prev_trr else []
            tr = p.op("tensor", (lambda _dc, _tb, _pb: lambda e: e.transpose(
                _pb[:], aoutT[:, _dc, _tb * 128:(_tb + 1) * 128], identbf[:]))(dc, tb, pb),
                deps=ffn_toks + [tok_id] + war)
            cp = p.op("vector", (lambda _pb: lambda e: e.tensor_copy(ttmp[:], _pb[:]))(pb),
                      deps=[tr] + ([prev_tt] if prev_tt is not None else []))
            ad = p.op("vector", (lambda _dc, _i: lambda e: e.tensor_add(
                otile[_i][:, _dc * 128:(_dc + 1) * 128],
                xtile[_i][:, _dc * 128:(_dc + 1) * 128], ttmp[:]))(dc, i),
                deps=[cp, ld] + dl)
            prev_trr[dc % 2] = cp
            prev_tt = ad
            adds.append(ad)
        st = p.dma("sync", [(o_x4[tb * 128:(tb + 1) * 128, :], otile[i][:])], deps=adds)
        prev_st[i] = st
        st_toks.append(st)
    return st_toks



PAIRS = [[0, 1], [2, 3], [4, 5], [6, 7]]
ALL8 = [[0, 1, 2, 3, 4, 5, 6, 7]]


def build_fused(debug_taps=False, upto=4, lfrom=0):
    lb = LB("full")
    nc, p = lb.nc, lb.p

    def din(name, shape, dt=F32):
        return lb.dram_in(name, shape, dt)

    x0 = din("x0_own", [2048, 1024])
    o_x4 = lb.dram_out("x4_own", [2048, 1024], F32)
    W = {}
    for i in (0, 1):
        W[f"l{i}_wq"] = din(f"l{i}_wq", [1024, 512])
        W[f"l{i}_wk"] = din(f"l{i}_wk", [1024, 512])
        W[f"l{i}_wv"] = din(f"l{i}_wv", [1024, 512])
        W[f"l{i}_wo"] = din(f"l{i}_wo", [64, 8, 1024])
    W["l0_g_r"] = din("l0_g_r", [128, 8])
    W["l0_b_r"] = din("l0_b_r", [128, 8])
    W["l0_fc1_q"] = din("l0_fc1_q", [1024, 4096], I32)
    W["l0_fc1_s"] = din("l0_fc1_s", [8, 4096])
    W["l0_fc1_b"] = din("l0_fc1_b", [128, 32])
    W["l0_fc2_q"] = din("l0_fc2_q", [4096, 1024], I32)
    W["l0_fc2_s"] = din("l0_fc2_s", [32, 1024])
    W["l0_fc2_b"] = din("l0_fc2_b", [128, 8])
    W["l1_wg"] = din("l1_wg", [1024, 8])
    W["l1_mlt"] = din("l1_mlt", [64, 8])
    W["l1_mall"] = din("l1_mall", [64, 8])
    W["l1_cfc"] = din("l1_cfc", [1024, 4096])
    W["l1_cpr"] = din("l1_cpr", [4096, 1024])
    W["l2_win_a"] = din("l2_win_a", [1024, 16, 128])
    W["l2_win_b"] = din("l2_win_b", [1024, 16, 64])
    W["l2_bia"] = din("l2_bia", [128, 16])
    W["l2_bib"] = din("l2_bib", [64, 16])
    for nm in ("shw_0", "shb_0", "shw_1", "shb_1", "shw_b", "shb_b"):
        shp = [64, 16, 3] if nm.startswith("shw") else [64, 16]
        W["l2_" + nm] = din("l2_" + nm, shp)
    W["l2_W1"] = din("l2_W1", [3, 64])
    W["l2_b1f"] = din("l2_b1f", [64, 4])
    W["l2_W2"] = din("l2_W2", [64, 64])
    W["l2_W3r"] = din("l2_W3r", [64, 8, 64])
    W["l2_b3c"] = din("l2_b3c", [64, 8])
    W["l2_ndel"] = din("l2_ndel", [64, 8])
    W["l2_fbrow"] = din("l2_fbrow", [1, 512])
    W["l2_wout_own"] = din("l2_wout_own", [512, 1024])
    W["l2_bout_row"] = din("l2_bout_row", [1, 1024])
    W["l2_wf"] = din("l2_wf", [1024, 1024])
    W["l2_a1"] = din("l2_a1", [1024, 1024])
    W["l2_a2"] = din("l2_a2", [1024, 1024])
    W["l2_bf2"] = din("l2_bf2", [128, 8])
    W["l3_wk"] = din("l3_wk", [1024, 512])
    W["l3_wv"] = din("l3_wv", [1024, 512])
    W["l3_wr"] = din("l3_wr", [1024, 512])
    W["l3_td"] = din("l3_td", [64, 8])
    W["l3_tf"] = din("l3_tf", [64, 8])
    W["l3_wo"] = din("l3_wo", [64, 8, 1024])
    W["l3_qgT"] = din("l3_qgT", [1024, 3072], I32)
    W["l3_quT"] = din("l3_quT", [1024, 3072], I32)
    W["l3_qdT"] = din("l3_qdT", [3072, 1024], I32)

    def mkout(name, shape, dt):
        if debug_taps:
            return lb.dram_out(name, shape, dt)
        return lb.dram_scratch(name, shape, dt)
    mids = [mkout("x1_own", [2048, 1024], F32), mkout("x2_own", [2048, 1024], F32),
            mkout("x3_own", [2048, 1024], F32)]
    xs = [None, mids[0], mids[1], mids[2], o_x4]
    xs[lfrom] = x0


    final_toks = []
    for li in range(lfrom, upto):
        lb.tag = f"L{li}_"
        ln_in = lb.dram_scratch("ln_in", [128, 16384], BF16)
        ln_out = lb.dram_scratch("ln_out", [256, 16384], BF16)
        if li == 2:
            rs_in = lb.dram_scratch("rs_in", [4096, 1024], BF16)
            rs_out = lb.dram_scratch("rs_out", [2048, 1024], BF16)
            att_dram = lb.dram_scratch("yt_dram", [512, 4096], BF16)
        else:
            rs_in = lb.dram_scratch("rs_in", [256, 16384], BF16)
            rs_out = lb.dram_scratch("rs_out", [128, 16384], BF16)
            att_dram = lb.dram_scratch("att_dram", [64, 8, 4096], BF16)
        with lb.scoped():
            with lb.scoped():
                xn_ownT = lb.sb("xnOT", [128, 8, 2048], BF16)
                ln_toks = emit_ln_transpose(lb, xs[li], xn_ownT, nt=16)
                st_ln = p.dma("sync", [(ln_in, xn_ownT[:].rearrange("p h t -> p (h t)"))],
                              deps=ln_toks)
            cc1 = p.cc((lambda _i, _o: lambda e: e.collective_compute(
                "AllGather", ALU.bypass, replica_groups=PAIRS,
                ins=[_i.opt()], outs=[_o.opt()]))(ln_in, ln_out), deps=[st_ln])
            p.barrier([cc1])

            def own_xn():
                t = lb.sb("xnOT2", [128, 8, 2048], BF16)
                ld = p.dma("sync", [(t[:], ln_in.rearrange("p (h t) -> p h t", h=8))],
                           deps=[cc1])
                return t, [ld]
            cc3 = None
            moe_env = {}
            if li == 1:
                xn_tok = lb.dram_scratch("xn_tok", [2048, 1024], BF16)
                mdisp = lb.dram_scratch("mdisp", [8 * CAP, 1024], BF16)
                mbatch = lb.dram_scratch("mbatch", [CAP, 1024], BF16)
                cn_ag_in = lb.dram_scratch("cn_ag_in", [8, 2], F32)
                cn_ag_out = lb.dram_scratch("cn_ag_out", [64, 2], F32)
                pc_b = lb.dram_scratch("pc_b", [2, 16, 128], F32)
                row_b0 = lb.dram_scratch("row_b0", [2048], F32)
                row_b1 = lb.dram_scratch("row_b1", [2048], F32)
                sp_d = lb.dram_scratch("sp_d", [1, 2, 2048], F32)
                wr_d = lb.dram_scratch("wr_d", [1, 2, 2048], F32)
                gi_d = lb.dram_scratch("gi_d", [128, 1], I32)
                with lb.scoped():
                    xnb, xnb_toks = own_xn()
                    slotpos = lb.sb("slotpos", [1, 2, 2048], F32)
                    wrow = lb.sb("wrow", [1, 2, 2048], F32)
                    gidx = lb.sb("gidx", [128, 1], I32)
                    E3 = dict(xnT=xnb, xn_toks=xnb_toks, w_g=W["l1_wg"],
                              mlt=W["l1_mlt"], mall=W["l1_mall"],
                              cn_ag_in=cn_ag_in, cn_ag_out=cn_ag_out,
                              mdisp=mdisp, xn_tok=xn_tok, pc_bounce=pc_b,
                              row_b0=row_b0, row_b1=row_b1,
                              slotpos=slotpos, wrow=wrow, gidx=gidx)
                    sc_toks, cmb_ready = emit_l1route(lb, E3)
                    st_sp = p.dma("sync", [(sp_d, slotpos[0:1, :, :]),
                                           (wr_d, wrow[0:1, :, :]),
                                           (gi_d, gidx[:])], deps=cmb_ready)
                cc3i = p.cc_issue((lambda _i, _o: lambda e: e.collective_compute(
                    "ReduceScatter", ALU.add, replica_groups=ALL8,
                    ins=[_i.opt()], outs=[_o.opt()]))(mdisp, mbatch),
                    deps=sc_toks)
                moe_env = dict(mbatch=mbatch, sp_d=sp_d, wr_d=wr_d, gi_d=gi_d,
                               st_sp=st_sp, cc3i=cc3i)
                p.barrier(sc_toks + [st_sp])
            with lb.scoped():
                xnT = lb.sb("xnTf", [128, 8, 4096], BF16)
                ld1 = p.dma("sync", [
                    (xnT[:, :, 0:2048], ln_out[0:128, :].rearrange("p (h t) -> p h t", h=8)),
                    (xnT[:, :, 2048:4096], ln_out[128:256, :].rearrange("p (h t) -> p h t", h=8)),
                ], deps=[cc1])
                E = dict(xnT=xnT, xn_toks=[ld1], rs_in=rs_in, att_dram=att_dram,
                         yt_dram=att_dram)
                if li == 0:
                    E.update(wq=W["l0_wq"], wk=W["l0_wk"], wv=W["l0_wv"], wo=W["l0_wo"])
                    mix_toks = emit_l0a(lb, E)
                elif li == 1:
                    E.update(wq=W["l1_wq"], wk=W["l1_wk"], wv=W["l1_wv"], wo=W["l1_wo"])
                    mix_toks = emit_l1a(lb, E)
                elif li == 2:
                    for nm in ("win_a", "win_b", "bia", "bib", "shw_0", "shb_0", "shw_1",
                               "shb_1", "shw_b", "shb_b", "W1", "b1f", "W2", "W3r", "b3c",
                               "ndel", "fbrow", "wout_own"):
                        E[nm] = W["l2_" + nm]
                    mix_toks = emit_l2a(lb, E)
                else:
                    E.update(wk=W["l3_wk"], wv=W["l3_wv"], wr=W["l3_wr"],
                             td=W["l3_td"], tf=W["l3_tf"], wo=W["l3_wo"])
                    mix_toks = emit_l3a(lb, E)
            cc2 = p.cc((lambda _i, _o: lambda e: e.collective_compute(
                "ReduceScatter", ALU.add, replica_groups=PAIRS,
                ins=[_i.opt()], outs=[_o.opt()]))(rs_in, rs_out), deps=mix_toks)
            if debug_taps:
                tap = lb.dram_out(f"L{li}_rs_tap",
                                  [2048, 1024] if li == 2 else [128, 16384], BF16)
                cc2 = p.dma("sync", [(tap, rs_out[:, :] if li == 2 else rs_out)],
                            deps=[cc2])
                if li == 2:
                    tap2 = lb.dram_out("L2_yt_tap", [512, 4096], BF16)
                    tap3 = lb.dram_out("L2_ln_tap", [256, 16384], BF16)
                    cc2 = p.dma("sync", [(tap2, att_dram[:, :]), (tap3, ln_out)],
                                deps=[cc2] + mix_toks)
            p.barrier([cc2])
            if li == 2 and globals().get("L2_SKIP_B", False):
                st = [p.dma("sync", [(xs[3], xs[2])], deps=[cc2])]
            elif li != 1:
                with lb.scoped():
                    xnb, xnb_toks = own_xn()
                    E2 = dict(x=xs[li], rs_out=rs_out, rs_tok=[cc2],
                              xnT=xnb, xn_toks=xnb_toks, x_out=xs[li + 1])
                    if li == 0:
                        for nm in ("g_r", "b_r", "fc1_q", "fc1_s", "fc1_b",
                                   "fc2_q", "fc2_s", "fc2_b"):
                            E2[nm] = W["l0_" + nm]
                        st = emit_l0b(lb, E2)
                    elif li == 2:
                        for nm in ("wf", "a1", "a2", "bf2", "bout_row"):
                            E2[nm] = W["l2_" + nm]
                        st = emit_l2b(lb, E2)
                    else:
                        for nm in ("qgT", "quT", "qdT"):
                            E2[nm] = W["l3_" + nm]
                        st = emit_l3b(lb, E2)
            else:
                oe_ag = lb.dram_scratch("oe_ag", [CAP, 1024], BF16)
                oe_all = lb.dram_scratch("oe_all", [8 * CAP + 64, 1024], BF16)
                mbatch = moe_env["mbatch"]
                cc3 = p.cc_wait(moe_env["cc3i"])
                with lb.scoped():
                    E4 = dict(mbatch=mbatch, rs_tok=[cc3], c_fc=W["l1_cfc"],
                              c_proj=W["l1_cpr"], oe_out=oe_ag)
                    exp_st = emit_l1exp(lb, E4)
                cc4 = p.cc((lambda _i, _o: lambda e: e.collective_compute(
                    "AllGather", ALU.bypass, replica_groups=ALL8,
                    ins=[_i.opt()], outs=[_o[0:8 * CAP, :].opt()]))(oe_ag, oe_all),
                    deps=exp_st)
                cc4s = [cc4]
                p.barrier(exp_st)
                with lb.scoped():
                    slotpos = lb.sb("slotposL", [1, 2, 2048], F32)
                    wrow = lb.sb("wrowL", [1, 2, 2048], F32)
                    gidx = lb.sb("gidxL", [128, 1], I32)
                    ld_sp = p.dma("sync", [
                        (slotpos[0:1, :, :], moe_env["sp_d"]),
                        (wrow[0:1, :, :], moe_env["wr_d"]),
                        (gidx[:], moe_env["gi_d"])], deps=[moe_env["st_sp"]])
                    E5 = dict(oe_all=oe_all, ag_tok=cc4s, gidx=gidx,
                              cmb_ready=[ld_sp], slotpos=slotpos, wrow=wrow,
                              rs_out=rs_out, rs_tok=[cc2], x=xs[1], x_out=xs[2])
                    st = emit_l1cmb(lb, E5)
            p.barrier(st)
            final_toks = st
        lb.tag = ""
    lb.finish(final_toks)
    return lb

PERM64 = np.concatenate([np.arange(0, 64, 2), np.arange(1, 64, 2)])

_BUILD_CACHE = {}


def _get_prog(name, build_fn):
    if name not in _BUILD_CACHE:
        lb = build_fn()
        consts = dict(lb.const_arrays)
        nc = lb.materialize()
        _BUILD_CACHE[name] = (nc, consts)
    return _BUILD_CACHE[name]


def _attn_w_slices(wq_rows, wk_rows, wv_rows, half, rope=True):
    h0 = half * 8
    if rope:
        wq = np.concatenate([wq_rows[h * 64:(h + 1) * 64][PERM64] for h in range(h0, h0 + 8)], 0).T
        wk = np.concatenate([wk_rows[h * 64:(h + 1) * 64][PERM64] for h in range(h0, h0 + 8)], 0).T
    else:
        wq = wq_rows[h0 * 64:(h0 + 8) * 64].T
        wk = wk_rows[h0 * 64:(h0 + 8) * 64].T
    wv = wv_rows[h0 * 64:(h0 + 8) * 64].T
    return (np.ascontiguousarray(wq), np.ascontiguousarray(wk), np.ascontiguousarray(wv))


def fused_in_map(inp, c):
    z = inp
    b, half = c // 2, c % 2
    h0 = half * 8
    own = slice(half * 2048, (half + 1) * 2048)
    x0 = np.asarray(z["L0_attn_x"], np.float32)
    im = {}
    im["x0_own"] = np.ascontiguousarray(x0[b][own])

    wqkv = np.asarray(z["L0_attn_w_qkv"])
    im["l0_wq"] = np.ascontiguousarray(np.concatenate(
        [wqkv[:, h * 64:(h + 1) * 64][:, PERM64] for h in range(h0, h0 + 8)], 1))
    im["l0_wk"] = np.ascontiguousarray(np.concatenate(
        [wqkv[:, 1024 + h * 64:1024 + (h + 1) * 64][:, PERM64] for h in range(h0, h0 + 8)], 1))
    im["l0_wv"] = np.ascontiguousarray(np.concatenate(
        [wqkv[:, 2048 + h * 64:2048 + (h + 1) * 64] for h in range(h0, h0 + 8)], 1))
    im["l0_wo"] = np.ascontiguousarray(np.asarray(z["L0_attn_w_o"])
                                       .reshape(16, 64, 1024).transpose(1, 0, 2)[:, h0:h0 + 8, :])
    im["l0_g_r"] = np.ascontiguousarray(np.asarray(z["L0_ffn_ln_gamma"]).reshape(8, 128).T)
    im["l0_b_r"] = np.ascontiguousarray(np.asarray(z["L0_ffn_ln_beta"]).reshape(8, 128).T)
    im["l0_fc1_q"] = np.asarray(z["L0_ffn_fc1_q"])
    im["l0_fc1_s"] = np.asarray(z["L0_ffn_fc1_scales"])
    im["l0_fc1_b"] = np.ascontiguousarray(np.asarray(z["L0_ffn_fc1_bias"]).reshape(32, 128).T)
    im["l0_fc2_q"] = np.asarray(z["L0_ffn_fc2_q"])
    im["l0_fc2_s"] = np.asarray(z["L0_ffn_fc2_scales"])
    im["l0_fc2_b"] = np.ascontiguousarray(np.asarray(z["L0_ffn_fc2_bias"]).reshape(8, 128).T)

    wq, wk, wv = _attn_w_slices(np.asarray(z["L1_attn_Wq"]), np.asarray(z["L1_attn_Wk"]),
                                np.asarray(z["L1_attn_Wv"]), half, rope=True)
    im["l1_wq"], im["l1_wk"], im["l1_wv"] = wq, wk, wv
    im["l1_wo"] = np.ascontiguousarray(np.asarray(z["L1_attn_Wo"]).T
                                       .reshape(16, 64, 1024).transpose(1, 0, 2)[:, h0:h0 + 8, :])
    im["l1_wg"] = np.ascontiguousarray(np.asarray(z["L1_ffn_w_g"]))
    mlt = np.zeros((64, 8), np.float32)
    mall = np.zeros((64, 8), np.float32)
    for cc in range(8):
        for e in range(8):
            mall[cc * 8 + e, e] = 1.0
            if cc < c:
                mlt[cc * 8 + e, e] = 1.0
    im["l1_mlt"] = mlt
    im["l1_mall"] = mall
    im["l1_cfc"] = np.ascontiguousarray(np.asarray(z["L1_ffn_c_fc"])[c])
    im["l1_cpr"] = np.ascontiguousarray(np.asarray(z["L1_ffn_c_proj"])[c])

    ch0 = half * 512
    Wi = np.asarray(z["L2_attn_W_in"])
    bi = np.asarray(z["L2_attn_b_in"])
    sw = np.asarray(z["L2_attn_short_w"])
    sb_ = np.asarray(z["L2_attn_short_b"])
    win_a = np.zeros((1024, 16, 128), np.float32)
    win_b = np.zeros((1024, 16, 64), np.float32)
    bia = np.zeros((128, 16), np.float32)
    bib = np.zeros((64, 16), np.float32)
    shw_0 = np.zeros((64, 16, 3), np.float32)
    shb_0 = np.zeros((64, 16), np.float32)
    shw_1 = np.zeros((64, 16, 3), np.float32)
    shb_1 = np.zeros((64, 16), np.float32)
    shw_b = np.zeros((64, 16, 3), np.float32)
    shb_b = np.zeros((64, 16), np.float32)
    W3r = np.zeros((64, 8, 64), np.float32)
    b3c = np.zeros((64, 8), np.float32)
    ndel = np.zeros((64, 8), np.float32)
    fcd = fft_dev_consts()
    for chg in range(8):
        cs = ch0 + chg * 64
        win_a[:, chg, 0:64] = Wi[:, cs:cs + 64]
        win_a[:, chg, 64:128] = Wi[:, 1024 + cs:1024 + cs + 64]
        win_b[:, chg, :] = Wi[:, 2048 + cs:2048 + cs + 64]
        bia[0:64, chg] = bi[cs:cs + 64]
        bia[64:128, chg] = bi[1024 + cs:1024 + cs + 64]
        bib[:, chg] = bi[2048 + cs:2048 + cs + 64]
        shw_0[:, chg] = sw[cs:cs + 64]
        shw_1[:, chg] = sw[1024 + cs:1024 + cs + 64]
        shb_0[:, chg] = sb_[cs:cs + 64]
        shb_1[:, chg] = sb_[1024 + cs:1024 + cs + 64]
        shw_b[:, chg] = sw[2048 + cs:2048 + cs + 64]
        shb_b[:, chg] = sb_[2048 + cs:2048 + cs + 64]
        W3r[:, chg, :] = np.asarray(z["L2_attn_W3"])[:, cs:cs + 64]
        b3c[:, chg] = np.asarray(z["L2_attn_b3"])[cs:cs + 64]
        ndel[:, chg] = fcd["ndel_full"][cs:cs + 64]
    f1 = np.asarray(z["L2_attn_f1"])
    b1 = np.asarray(z["L2_attn_b1"])
    f2 = np.asarray(z["L2_attn_f2"])
    b2 = np.asarray(z["L2_attn_b2"])
    b1f = np.stack([f1 / (2 * np.pi), f1 * b1 / (2 * np.pi),
                    -f2 / (2 * np.pi), f2 * b2 / (2 * np.pi)], 1).astype(np.float32)
    im["l2_win_a"] = win_a
    im["l2_win_b"] = win_b
    im["l2_bia"] = bia
    im["l2_bib"] = bib
    im["l2_shw_0"] = shw_0
    im["l2_shb_0"] = shb_0
    im["l2_shw_1"] = shw_1
    im["l2_shb_1"] = shb_1
    im["l2_shw_b"] = shw_b
    im["l2_shb_b"] = shb_b
    im["l2_W1"] = np.ascontiguousarray(np.asarray(z["L2_attn_W1"], np.float32))
    im["l2_b1f"] = b1f
    im["l2_W2"] = np.ascontiguousarray(np.asarray(z["L2_attn_W2"], np.float32))
    im["l2_W3r"] = W3r
    im["l2_b3c"] = b3c
    im["l2_ndel"] = ndel
    im["l2_fbrow"] = np.ascontiguousarray(
        np.asarray(z["L2_attn_filter_bias"], np.float32)[ch0:ch0 + 512][None, :])
    im["l2_wout_own"] = np.ascontiguousarray(np.asarray(z["L2_attn_W_out"])[ch0:ch0 + 512, :])
    im["l2_bout_row"] = np.ascontiguousarray(np.asarray(z["L2_attn_b_out"])[None, :])
    im["l2_wf"] = np.ascontiguousarray(np.asarray(z["L2_ffn_weight"]))
    im["l2_a1"] = np.ascontiguousarray(np.asarray(z["L2_ffn_aux_weight1"]))
    im["l2_a2"] = np.ascontiguousarray(np.asarray(z["L2_ffn_aux_weight2"]))
    im["l2_bf2"] = np.ascontiguousarray(np.asarray(z["L2_ffn_bias"]).reshape(8, 128).T)

    rows = slice(h0 * 64, (h0 + 8) * 64)
    im["l3_wk"] = np.ascontiguousarray(np.asarray(z["L3_attn_Wk"])[rows].T)
    im["l3_wv"] = np.ascontiguousarray(np.asarray(z["L3_attn_Wv"])[rows].T)
    im["l3_wr"] = np.ascontiguousarray(np.asarray(z["L3_attn_Wr"])[rows].T)
    im["l3_td"] = np.ascontiguousarray(np.asarray(z["L3_attn_time_decay"])[h0:h0 + 8].T)
    im["l3_tf"] = np.ascontiguousarray(np.asarray(z["L3_attn_time_first"])[h0:h0 + 8].T)
    im["l3_wo"] = np.ascontiguousarray(np.asarray(z["L3_attn_Wo"]).T
                                       .reshape(16, 64, 1024).transpose(1, 0, 2)[:, h0:h0 + 8, :])
    im["l3_qgT"] = np.ascontiguousarray(np.asarray(z["L3_ffn_w_gate"]).T)
    im["l3_quT"] = np.ascontiguousarray(np.asarray(z["L3_ffn_w_up"]).T)
    im["l3_qdT"] = np.ascontiguousarray(np.asarray(z["L3_ffn_w_down"]).T)
    return im


def kernel(**inp):
    nc, consts = _get_prog("fused", build_fused)
    in_maps = []
    for c in range(8):
        im = fused_in_map(inp, c)
        im.update(consts)
        in_maps.append(im)
    res = run_bass_kernel_spmd(nc, in_maps, core_ids=list(range(8)))
    x4 = np.stack([np.concatenate([np.asarray(res.results[2 * b]["x4_own"]),
                                   np.asarray(res.results[2 * b + 1]["x4_own"])], 0)
                   for b in range(4)], 0)
    return x4.astype(np.float32)
